# version 06
# baseline (speedup 1.0000x reference)
import math
from dataclasses import dataclass

import numpy as np
import ml_dtypes

import concourse.bass as bass
import concourse.mybir as mybir
from concourse.bass_utils import run_bass_kernel_spmd

F32 = mybir.dt.float32
BF16 = mybir.dt.bfloat16
U32 = mybir.dt.uint32
AF = mybir.ActivationFunctionType
ALU = mybir.AluOpType
AXX = mybir.AxisListType.X
BF = ml_dtypes.bfloat16

ROPE_BASE = 6000.0
EPS = 1e-5


@dataclass
class Cfg:
    S: int = 8192
    D: int = 1024
    F: int = 1024
    NH: int = 16
    HPC: int = 4
    DK: int = 64
    E: int = 6
    R: int = 16

    @property
    def DCH(self): return self.D // 128
    @property
    def FCH(self): return self.F // 128
    @property
    def NG(self): return self.S // 512
    @property
    def NQT(self): return self.S // 512
    @property
    def NKT(self): return self.S // 128
    @property
    def OWN(self): return self.S // 4
    @property
    def NOT(self): return self.OWN // 128
    @property
    def NFG(self): return self.OWN // 512
    @property
    def NST(self): return -(-self.D // 512)
    @property
    def LR(self): return self.R * (1 + self.E)
    @property
    def VW(self): return self.HPC * (self.DK + 1)
    @property
    def NTS(self): return self.S // 128


CFG = Cfg()
ENGS = ("tensor", "vector", "scalar", "gpsimd", "sync")
SELF_SEM = {"vector": "sv", "scalar": "sa"}


class Plan:
    def __init__(self):
        import collections
        self.ops = {e: [] for e in ENGS}
        self.cnt = collections.defaultdict(int)

    def em(self, eng, fn, waits=(), inc=None, incn=1):
        ws = [w for w in waits if w is not None]
        if eng in SELF_SEM:
            ws.append((SELF_SEM[eng], self.cnt[SELF_SEM[eng]]))
        tok = None
        if inc is not None:
            self.cnt[inc] += incn
            tok = (inc, self.cnt[inc])
        self.ops[eng].append((ws, fn, tok, incn))
        return tok

    def wait(self, eng, waits):
        self.ops[eng].append(([w for w in waits if w is not None], None, None, 0))


def build(cfg: Cfg):
    SEMS = {}
    nc = bass.Bass("TRN2", target_bir_lowering=False)
    S, D, Fd = cfg.S, cfg.D, cfg.F
    DCH, FCH, NST = cfg.DCH, cfg.FCH, cfg.NST
    NG, NQT, NKT, OWN, NOT = cfg.NG, cfg.NQT, cfg.NKT, cfg.OWN, cfg.NOT
    HPC, DK, LR, E, R, VW, NTS = cfg.HPC, cfg.DK, cfg.LR, cfg.E, cfg.R, cfg.VW, cfg.NTS
    NDH = -(-D // 512)

    din = lambda n, sh, dt=F32: nc.dram_tensor(n, list(sh), dt, kind="ExternalInput")
    x_fm = din("x_fm", [D, S]);           x_tm = din("x_tm", [S, D])
    x_res = din("x_res", [OWN, D]);       rkoff = din("rkoff", [1, 1], U32)
    wqkT = din("wqkT", [D, 512]);         wvT = din("wvT", [D, HPC * DK])
    woT = din("woT", [Fd, D]);            baseWT = din("baseWT", [D, D])
    baseb = din("baseb", [1, D]);         lrAT = din("lrAT", [D, LR])
    routerWT = din("routerWT", [D, E]);   routerb = din("routerb", [1, E])
    BstackT = din("BstackT", [LR, D])
    ropeC = din("ropeC", [128, S], BF16); ropeS = din("ropeS", [128, S], BF16)
    masksI = din("masks", [512, 512], BF16)
    iden_bf = din("iden_bf", [128, 128], BF16)
    iden_f = din("iden_f", [128, 128], F32)
    bselI = din("bsel", [16, 8 * 128], BF16)
    g6I = din("g6sel", [E, LR], BF16);    shI = din("shsel", [1, LR], BF16)
    out_t = nc.dram_tensor("out", [OWN, D], F32, kind="ExternalOutput")

    zstats_d = nc.dram_tensor("zstats_d", [2, OWN], F32)
    xch_in = nc.dram_tensor("xch_in", [HPC * (DK + 1), S], BF16)
    xch_out = [nc.dram_tensor(f"xch_out{p}", [4 * 2 * (DK + 1), S], BF16)
               for p in range(2)]

    P = Plan()
    em, wt = P.em, P.wait

    import contextlib
    es = contextlib.ExitStack()
    sb = lambda n, sh, dt: es.enter_context(nc.sbuf_tensor(n, list(sh), dt))

    onesb = sb("onesb", [1, 512], BF16)
    onescol = sb("onescol", [128, 1], BF16)
    epst = sb("epst", [128, 1], F32)
    idenb = sb("idenb", [128, 128], BF16)
    idenf = sb("idenf", [128, 128], F32)
    bneg = sb("bneg", [128, 1], F32)
    r_cols = sb("r_cols", [128, NTS], F32)
    woT_sb = sb("woT_sb", [128, FCH * D], BF16)
    baseWT_sb = sb("baseWT_sb", [128, DCH * D], BF16)
    lrAT_sb = sb("lrAT_sb", [128, DCH * LR], BF16)
    routerWT_sb = sb("routerWT_sb", [128, DCH * E], BF16)
    BstackT_sb = sb("BstackT_sb", [LR, D], BF16)
    bsel_sb = sb("bsel_sb", [16, 8 * 128], BF16)
    g6_sb = sb("g6_sb", [E, LR], BF16)
    shsel_sb = sb("shsel_sb", [1, LR], BF16)
    rb6_sb = sb("rb6_sb", [128, 4 * E], F32)
    bbc_sb = sb("bbc_sb", [128, D], BF16)
    w1base = sb("w1base", [1, D], BF16)
    w1lr = sb("w1lr", [1, LR], BF16)
    w1ro = sb("w1ro", [1, E], BF16)
    bscb = sb("bscb", [1, 1], BF16)

    dmag = lambda fn, ch, waits=(): em("gpsimd", fn, waits=waits, inc=ch, incn=16)
    dmas = lambda fn, ch, waits=(): em("sync", fn, waits=waits, inc=ch, incn=16)

    def chunked_load(dst, src_t, nchunk, width, ch, q="g"):
        f = lambda g: g.dma_start(
            out=dst[:, :].rearrange("p (c n) -> p c n", c=nchunk),
            in_=src_t.ap().rearrange("(c p) n -> p c n", p=128))
        return (dmag if q == "g" else dmas)(f, ch)

    L = {}
    L["woT"] = chunked_load(woT_sb, woT, FCH, D, "dgw")
    L["baseWT"] = chunked_load(baseWT_sb, baseWT, DCH, D, "dgw")
    L["lrAT"] = chunked_load(lrAT_sb, lrAT, DCH, LR, "dgw")
    L["routerWT"] = chunked_load(routerWT_sb, routerWT, DCH, E, "dgw")
    L["Bstack"] = dmag(lambda g: g.dma_start(out=BstackT_sb[:, :], in_=BstackT.ap()), "dgw")
    L["bbrow"] = dmag(lambda g: g.dma_start(out=bbrow[:, :], in_=baseb.ap()), "dgw")
    L["rbrow"] = dmag(lambda g: g.dma_start(out=rbrow[:, :], in_=routerb.ap()), "dgw")
    L["bsel"] = dmas(lambda s: s.dma_start(out=bsel_sb[:, :], in_=bselI.ap()), "dsw")
    L["g6"] = dmas(lambda s: s.dma_start(out=g6_sb[:, :], in_=g6I.ap()), "dsw")
    L["sh"] = dmas(lambda s: s.dma_start(out=shsel_sb[:, :], in_=shI.ap()), "dsw")
    L["idb"] = dmas(lambda s: s.dma_start(out=idenb[:, :], in_=iden_bf.ap()), "dsw")
    L["idf"] = dmas(lambda s: s.dma_start(out=idenf[:, :], in_=iden_f.ap()), "dsw")

    t_ones = em("vector", lambda v: v.memset(onesb[:, :], 1.0), inc="sv")
    t_onesc = em("vector", lambda v: v.memset(onescol[:, :], 1.0), inc="sv")
    t_eps = em("vector", lambda v: v.memset(epst[:, :], EPS), inc="sv")

    AE = {}

    with contextlib.ExitStack() as esA:
        sba = lambda n, sh, dt: esA.enter_context(nc.sbuf_tensor(n, list(sh), dt))
        rbrow = sba("rbrow", [1, E], BF16)
        bbrow = sba("bbrow", [1, D], BF16)
        Qp = [sba(f"Qp{p}", [128, S], BF16) for p in range(2)]
        Kp = [sba(f"Kp{p}", [128, S], BF16) for p in range(2)]
        Vtm = sba("Vtm", [128, NTS * VW], BF16)
        wqk_sb = sba("wqk_sb", [128, DCH * 512], BF16)
        wv_sb = sba("wv_sb", [128, DCH * HPC * DK], BF16)
        xbuf = [sba(f"xbuf{i}", [128, DCH * 512], BF16) for i in range(2)]
        ropeCb = [sba(f"ropeCb{i}", [128, 512], BF16) for i in range(2)]
        ropeSb = [sba(f"ropeSb{i}", [128, 512], BF16) for i in range(2)]
        nmrow = [sba(f"nmrow{i}", [1, 512], BF16) for i in range(2)]
        rrow = [sba(f"rrow{i}", [1, 512], BF16) for i in range(2)]
        rb_sb = sba("rb_sb", [128, 512], F32)
        ropet = [sba(f"ropet{i}", [128, 512], BF16) for i in range(2)]
        w1qk = sba("w1qk", [1, 512], BF16)
        w1v = sba("w1v", [1, HPC * DK], BF16)
        Pt = [sba(f"Pt{i}", [128, 1024], BF16) for i in range(3)]
        ostg = [sba(f"ostg{i}", [65, 512], BF16) for i in range(2)]
        mask_sb = sba("mask_sb", [128, 4 * 512], BF16)
        qsq = sba("qsq", [128, 512], BF16)
        bcol = sba("bcol", [1, 4 * NG], BF16)
        bsc = sba("bsc", [1, 1], F32)

        t_msk = dmas(lambda s: s.dma_start(
            out=mask_sb[:, :].rearrange("p (i n) -> p i n", i=4),
            in_=masksI.ap().rearrange("(i p) n -> p i n", p=128)), "dsw")
        t_wqk = chunked_load(wqk_sb, wqkT, DCH, 512, "dgw")
        t_wv = chunked_load(wv_sb, wvT, DCH, HPC * DK, "dgw")
        _gw = ("dgw", P.cnt["dgw"]); _sw = ("dsw", P.cnt["dsw"])
        for _k in list(L):
            L[_k] = _gw if L[_k][0] == "dgw" else _sw
        t_msk, t_wqk, t_wv = _sw, _gw, _gw

        with contextlib.ExitStack() as esP:
            ps_w1 = esP.enter_context(nc.psum_tensor("ps_w1", [1, 512], F32))
            ps_bb = esP.enter_context(nc.psum_tensor("ps_bb", [128, 512], F32))

            _w1prev = [None]
            def w1row(dst, wsb, width, stride, ldtok, w0=0):
                def fn(t):
                    ii = None
                    for k in range(DCH):
                        ii = t.matmul(ps_w1[0:1, 0:width], onescol[:, 0:1],
                                      wsb[:, k * stride + w0:k * stride + w0 + width],
                                      start=(k == 0), stop=(k == DCH - 1))
                    return ii
                tok = em("tensor", fn, waits=[ldtok, t_onesc, _w1prev[0]], inc="st")
                ev = em("vector", lambda v: v.tensor_copy(dst, ps_w1[0:1, 0:width]),
                        waits=[tok], inc="sv")
                _w1prev[0] = ev
                return ev

            e_w1qk = w1row(w1qk[0:1, :], wqk_sb, 512, 512, t_wqk)
            e_w1v = w1row(w1v[0:1, :], wv_sb, HPC * DK, HPC * DK, t_wv)
            e_w1b = []
            for h in range(NDH):
                w0 = h * 512
                wd = min(512, D - w0)
                e_w1b.append(w1row(w1base[0:1, w0:w0 + wd], baseWT_sb, wd, D,
                                   L["baseWT"], w0=w0))
            e_w1lr = w1row(w1lr[0:1, :], lrAT_sb, LR, LR, L["lrAT"])
            e_w1ro = w1row(w1ro[0:1, :], routerWT_sb, E, E, L["routerWT"])

            t1 = em("tensor", lambda t: t.matmul(ps_bb[:, 0:E], onesb[0:1, 0:128],
                                                 rbrow[0:1, :], start=True, stop=True),
                    waits=[L["rbrow"], t_ones, e_w1ro], inc="st")
            e_rb6 = em("vector", lambda v: v.tensor_copy(
                rb6_sb[:, :].rearrange("p (j e) -> p j e", j=4),
                ps_bb[:, 0:E].unsqueeze(1).broadcast_to([128, 4, E])),
                waits=[t1], inc="sv")
            e_bbc = None
            for h in range(NDH):
                w0 = h * 512
                wd = min(512, D - w0)
                t2 = em("tensor", lambda t, w0=w0, wd=wd: t.matmul(
                    ps_bb[:, 0:wd], onesb[0:1, 0:128], bbrow[0:1, w0:w0 + wd],
                    start=True, stop=True),
                    waits=[L["bbrow"], t_ones, e_rb6 if h == 0 else e_bbc], inc="st")
                e_bbc = em("vector", lambda v, w0=w0, wd=wd: v.tensor_copy(
                    bbc_sb[:, w0:w0 + wd], ps_bb[:, 0:wd]), waits=[t2], inc="sv")

        xtbuf = [sba(f"xtbuf{i}", [128, D], F32) for i in range(2)]
        stat12 = sba("stat12", [128, 6 * NST], F32)
        mvall = sba("mvall", [128, 2 * NTS], F32)
        rowst = [sba(f"rowst{i}", [4, 128], BF16) for i in range(4)]
        assert NTS <= 128

        rope_done = []
        with contextlib.ExitStack() as esP:
            ps_qkv = [esP.enter_context(nc.psum_tensor(f"ps_qkv{m}", [128, 512], F32))
                      for m in range(4)]
            ps_V2 = esP.enter_context(nc.psum_tensor("ps_V2", [128, 2 * HPC * DK], F32))
            ps_V = [ps_V2[:, 0:HPC * DK], ps_V2[:, HPC * DK:2 * HPC * DK]]
            ps_rb = esP.enter_context(nc.psum_tensor("ps_rb", [128, 512], F32))
            ps_misc = esP.enter_context(nc.psum_tensor("ps_misc", [128, 512], F32))
            ps_st = esP.enter_context(nc.psum_tensor("ps_st", [4, 128], F32))

            t_vms = em("vector", lambda v: v.memset(Vtm[:, :], 1.0), inc="sv")
            qk_evac, v_evac = {}, {}
            rb_prev = None
            bnd_mm_prev = None
            bnd_red_last = None
            tstat_toks = {}
            ckr_prev = [None]
            rl1_prev = [None, None]
            rl2_prev = [None, None]
            for g in range(NG):
                sl = g % 2
                wx = []
                if g >= 2:
                    wx = [qk_evac[(g - 2, 3)], v_evac[4 * (g - 2) + 3],
                          rope_done[4 * (g - 1) - 1]]
                xl = dmag(lambda gg, g=g, sl=sl: gg.dma_start(
                    out=xbuf[sl][:, :].rearrange("p (c n) -> p c n", c=DCH),
                    in_=x_fm.ap().rearrange("(c p) s -> p c s", p=128)
                        [:, :, g * 512:(g + 1) * 512]), f"dgx{sl}", waits=wx)
                wct = [rope_done[4 * (g - 1) - 1]] if g >= 2 else []
                ct = dmas(lambda s, g=g, sl=sl: s.dma_start(
                    out=ropeCb[sl][:, :], in_=ropeC.ap()[:, g * 512:(g + 1) * 512]),
                    f"dtc{sl}", waits=wct)
                st_ = dmas(lambda s, g=g, sl=sl: s.dma_start(
                    out=ropeSb[sl][:, :], in_=ropeS.ap()[:, g * 512:(g + 1) * 512]),
                    f"dts{sl}", waits=wct)

                taggr = None
                for tsl in range(4):
                    ts = 4 * g + tsl
                    sl2 = ts % 2
                    ld = dmas(lambda s, ts=ts, sl2=sl2: s.dma_start(
                        out=xtbuf[sl2][:, :], in_=x_tm.ap()[ts * 128:(ts + 1) * 128, :]),
                        f"dsx{sl2}", waits=[tstat_toks.get(ts - 2)])
                    def fstat(v, sl2=sl2):
                        ii = None
                        for j in range(NST):
                            c0 = j * 512
                            cw = min(512, D - c0)
                            ii = v.bn_stats(stat12[:, 6 * j:6 * j + 6],
                                            xtbuf[sl2][:, c0:c0 + cw])
                        return ii
                    tstat = em("vector", fstat, waits=[ld], inc="sv")
                    tstat_toks[ts] = tstat
                    taggr = em("vector", lambda v, ts=ts: v.bn_aggr(
                        mvall[:, 2 * ts:2 * ts + 2], stat12[:, 0:6 * NST]),
                        waits=[tstat], inc="sv")
                t_nm_g = em("vector", lambda v, g=g: v.tensor_scalar(
                    mvall[:, 8 * g:8 * g + 8:2], mvall[:, 8 * g:8 * g + 8:2],
                    -1.0, None, ALU.mult), waits=[taggr], inc="sv")
                t_lv_g = em("scalar", lambda s, g=g: s.activation(
                    mvall[:, 8 * g + 1:8 * g + 8:2], mvall[:, 8 * g + 1:8 * g + 8:2],
                    AF.Ln, bias=epst[:, 0:1]),
                    waits=[taggr, t_eps], inc="sa")
                t_rr = em("scalar", lambda s, g=g: s.activation(
                    r_cols[:, 4 * g:4 * g + 4], mvall[:, 8 * g + 1:8 * g + 8:2],
                    AF.Exp, scale=-0.5), waits=[t_lv_g], inc="sa")
                tkn = em("tensor", lambda t, g=g: t.transpose(
                    ps_st[0:4, 0:128], mvall[:, 8 * g:8 * g + 8:2], idenf[:, :]),
                    waits=[t_nm_g, L["idf"], ckr_prev[0]], inc="st")
                ckn = em("vector", lambda v, g=g: v.tensor_copy(rowst[g % 2][0:4, :],
                                                                ps_st[0:4, 0:128]),
                         waits=[tkn, rl1_prev[g % 2]], inc="sv")
                rl1 = dmag(lambda gg, sl=sl, g=g: gg.dma_start(
                    out=nmrow[sl][:, :], in_=rowst[g % 2][0:4, :]),
                    f"dgr{sl}", waits=[ckn] + wx)
                rl1_prev[g % 2] = rl1
                tkr = em("tensor", lambda t, g=g: t.transpose(
                    ps_st[0:4, 0:128], r_cols[:, 4 * g:4 * g + 4], idenf[:, :]),
                    waits=[t_rr, ckn], inc="st")
                ckr = em("vector", lambda v, g=g: v.tensor_copy(rowst[2 + g % 2][0:4, :],
                                                                ps_st[0:4, 0:128]),
                         waits=[tkr, rl2_prev[g % 2]], inc="sv")
                ckr_prev[0] = ckr
                rl2 = dmag(lambda gg, sl=sl, g=g: gg.dma_start(
                    out=rrow[sl][:, :], in_=rowst[2 + g % 2][0:4, :]),
                    f"dgs{sl}", waits=[ckr, rb_prev])
                rl2_prev[g % 2] = rl2

                trb = em("tensor", lambda t, sl=sl: t.matmul(
                    ps_rb[:, :], onesb[0:1, 0:128], rrow[sl][0:1, :],
                    start=True, stop=True),
                    waits=[rl2, t_ones, rb_prev], inc="st")
                tre = em("vector", lambda v: v.tensor_copy(rb_sb[:, :], ps_rb[:, :]),
                         waits=[trb], inc="sv")
                rb_prev = tre
                tcC = em("vector", lambda v, sl=sl: v.tensor_tensor(
                    ropeCb[sl][:, :], ropeCb[sl][:, :], rb_sb[:, :], ALU.mult),
                    waits=[ct, tre], inc="sv")
                tcS = em("vector", lambda v, sl=sl: v.tensor_tensor(
                    ropeSb[sl][:, :], ropeSb[sl][:, :], rb_sb[:, :], ALU.mult),
                    waits=[st_, tcC], inc="sv")

                for m in range(4):
                    wqkv = [xl, rl1, t_wqk, e_w1qk]
                    if g >= 1:
                        wqkv.append(qk_evac[(g - 1, m)])
                    def fqk(t, m=m, sl=sl):
                        for k in range(DCH):
                            t.matmul(ps_qkv[m][:, :],
                                     wqk_sb[:, k * 512 + m * 128:k * 512 + (m + 1) * 128],
                                     xbuf[sl][:, k * 512:(k + 1) * 512],
                                     start=(k == 0), stop=False)
                        return t.matmul(ps_qkv[m][:, :], w1qk[0:1, m * 128:(m + 1) * 128],
                                        nmrow[sl][0:1, :], start=False, stop=True)
                    tmm = em("tensor", fqk, waits=wqkv, inc="st")
                    pair, isK = m // 2, (m % 2 == 1)
                    dst = (Kp if isK else Qp)[pair]
                    qk_evac[(g, m)] = em("scalar", lambda s, dst=dst, m=m, g=g: s.activation(
                        dst[:, g * 512:(g + 1) * 512], ps_qkv[m][:, :], AF.Copy),
                        waits=[tmm], inc="sa")

                for tsl in range(4):
                    ts = 4 * g + tsl
                    wv_ = [xl, rl1, t_wv, e_w1v, t_vms]
                    if ts >= 1:
                        wv_.append(v_evac[ts - 1])
                    def fv(t, tsl=tsl, sl=sl):
                        for k in range(DCH):
                            t.matmul(ps_V[tsl % 2],
                                     xbuf[sl][:, k * 512 + tsl * 128:k * 512 + (tsl + 1) * 128],
                                     wv_sb[:, k * HPC * DK:(k + 1) * HPC * DK],
                                     start=(k == 0), stop=False)
                        return t.matmul(ps_V[tsl % 2],
                                        nmrow[sl][0:1, tsl * 128:(tsl + 1) * 128],
                                        w1v[0:1, :], start=False, stop=True)
                    tmv = em("tensor", fv, waits=wv_, inc="st")
                    v_evac[ts] = em("scalar", lambda s, ts=ts, tsl=tsl: s.activation(
                        Vtm[:, ts * VW:(ts + 1) * VW]
                            .rearrange("p (h d) -> p h d", h=HPC)[:, :, 0:DK],
                        ps_V[tsl % 2].rearrange("p (h d) -> p h d", h=HPC),
                        AF.Copy, scale=r_cols[:, ts:ts + 1]),
                        waits=[tmv, t_rr, t_vms], inc="sa")

                shmask = list(range(16, 32)) + list(range(16))
                for pair in range(2):
                    for which, base in ((0, Qp[pair]), (1, Kp[pair])):
                        wdep = qk_evac[(g, pair * 2 + which)]
                        t1_ = em("vector", lambda v, base=base, g=g: v.stream_shuffle(
                            ropet[0][:, :], base[:, g * 512:(g + 1) * 512], shmask),
                            waits=[wdep, tcS], inc="sv")
                        t2_ = em("vector", lambda v, sl=sl: v.tensor_tensor(
                            ropet[0][:, :], ropet[0][:, :], ropeSb[sl][:, :], ALU.mult),
                            waits=[t1_], inc="sv")
                        t3_ = em("vector", lambda v, base=base, g=g, sl=sl: v.tensor_tensor(
                            ropet[1][:, :], base[:, g * 512:(g + 1) * 512],
                            ropeCb[sl][:, :], ALU.mult),
                            waits=[t2_], inc="sv")
                        t4_ = em("vector", lambda v, base=base, g=g: v.tensor_tensor(
                            base[:, g * 512:(g + 1) * 512], ropet[0][:, :],
                            ropet[1][:, :], ALU.add),
                            waits=[t3_], inc="sv")
                        rope_done.append(t4_)

                for m in range(4):
                    pair, isK = m // 2, (m % 2 == 1)
                    srcT = (Kp if isK else Qp)[pair]
                    tsq = em("vector", lambda v, srcT=srcT, g=g: v.tensor_tensor(
                        qsq[:, :], srcT[:, g * 512:(g + 1) * 512],
                        srcT[:, g * 512:(g + 1) * 512], ALU.mult),
                        waits=[rope_done[4 * g + m], bnd_mm_prev], inc="sv")
                    bnd_mm = em("tensor", lambda t: t.matmul(
                        ps_misc[0:1, 0:512], onescol[:, 0:1], qsq[:, :],
                        start=True, stop=True),
                        waits=[tsq, bnd_red_last, t_onesc], inc="st")
                    bnd_mm_prev = bnd_mm
                    bnd_red_last = em("vector", lambda v, g=g, m=m: v.tensor_reduce(
                        bcol[0:1, 4 * g + m:4 * g + m + 1], ps_misc[0:1, 0:512],
                        AXX, ALU.max),
                        waits=[bnd_mm], inc="sv")
        qkv_sv = ("sv", P.cnt["sv"])
        qkv_sa = ("sa", P.cnt["sa"])
        qkv_st = ("st", P.cnt["st"])

        with contextlib.ExitStack() as esP:
            ps_bc = esP.enter_context(nc.psum_tensor("ps_bc", [128, 1], F32))
            tq = em("vector", lambda v: v.tensor_reduce(
                bsc[0:1, 0:1], bcol[0:1, 0:4 * NG:2], AXX, ALU.max),
                waits=[bnd_red_last, qkv_sv], inc="sv")
            tk = em("vector", lambda v: v.tensor_reduce(
                bcol[0:1, 0:1], bcol[0:1, 1:4 * NG:2], AXX, ALU.max),
                waits=[tq], inc="sv")
            tm = em("vector", lambda v: v.tensor_mul(bsc[0:1, 0:1], bsc[0:1, 0:1],
                                                     bcol[0:1, 0:1]), waits=[tk], inc="sv")
            tl = em("scalar", lambda s: s.activation(bsc[0:1, 0:1], bsc[0:1, 0:1], AF.Ln),
                    waits=[tm], inc="sa")
            te = em("scalar", lambda s: s.activation(bsc[0:1, 0:1], bsc[0:1, 0:1],
                                                     AF.Exp, scale=0.5),
                    waits=[tl], inc="sa")
            tcst = em("vector", lambda v: v.tensor_copy(bscb[0:1, 0:1], bsc[0:1, 0:1]),
                      waits=[te], inc="sv")
            tbc = em("tensor", lambda t: t.matmul(ps_bc[:, 0:1], onesb[0:1, 0:128],
                                                  bscb[0:1, 0:1], start=True, stop=True),
                     waits=[tcst], inc="st")
            bound_sv = em("vector", lambda v: v.tensor_scalar(
                bneg[:, 0:1], ps_bc[:, 0:1], -0.125, None, ALU.mult),
                waits=[tbc], inc="sv")

        with contextlib.ExitStack() as esP:
            ps_S = [esP.enter_context(nc.psum_tensor(f"ps_S{i}", [128, 1024], F32))
                    for i in range(3)]
            ps_O = [esP.enter_context(nc.psum_tensor(f"ps_O{h}", [65, 512], F32))
                    for h in range(2)]

            i_glob = 0
            seq = 0
            score_tok, exp_tok, mask_tok, pv_tok = {}, {}, {}, {}
            evac_toks, store_toks = [], []
            cc_toks = []
            for pair in range(2):
                for qt in range(NQT):
                    nkt = 4 * (qt + 1)
                    base_i = i_glob

                    def emit_sc(j):
                        kt = j
                        i = base_i + j
                        w = [bound_sv]
                        if i == 0:
                            w += [rope_done[-1], qkv_sv, qkv_sa]
                        if i - 3 >= 0:
                            w.append(exp_tok[i - 3])
                        def fsc(t, pair=pair, qt=qt, kt=kt, i=i):
                            ii = None
                            for h in range(2):
                                ii = t.matmul(
                                    ps_S[i % 3][:, h * 512:(h + 1) * 512],
                                    Kp[pair][h * 64:(h + 1) * 64,
                                             kt * 128:(kt + 1) * 128],
                                    Qp[pair][h * 64:(h + 1) * 64,
                                             qt * 512:(qt + 1) * 512],
                                    start=True, stop=True)
                            return ii
                        score_tok[i] = em("tensor", fsc, waits=w, inc="st")
                        w = [score_tok[i]]
                        if i - 3 >= 0:
                            w.append(pv_tok[i - 3])
                        exp_tok[i] = em("scalar", lambda s, i=i: s.activation(
                            Pt[i % 3][:, :], ps_S[i % 3][:, :], AF.Exp,
                            bias=bneg[:, 0:1], scale=0.125),
                            waits=w, inc="sa")
                        if kt >= 4 * qt:
                            mi = kt - 4 * qt
                            def fmk(v, i=i, mi=mi):
                                v.tensor_tensor(Pt[i % 3][:, 0:512], Pt[i % 3][:, 0:512],
                                                mask_sb[:, mi * 512:(mi + 1) * 512],
                                                ALU.mult)
                                return v.tensor_tensor(
                                    Pt[i % 3][:, 512:1024], Pt[i % 3][:, 512:1024],
                                    mask_sb[:, mi * 512:(mi + 1) * 512], ALU.mult)
                            mask_tok[i] = em("vector", fmk,
                                             waits=[exp_tok[i], t_msk], inc="sv")

                    def emit_pv(j):
                        kt = j
                        i = base_i + j
                        w = [mask_tok[i] if i in mask_tok else exp_tok[i]]
                        if j == 0:
                            w.append(evac_toks[-1] if evac_toks else qkv_sa)
                        def fpv(t, pair=pair, kt=kt, i=i, first=(j == 0),
                                last=(j == nkt - 1)):
                            ii = None
                            for h in range(2):
                                hl = 2 * pair + h
                                ii = t.matmul(
                                    ps_O[h][0:65, :],
                                    Vtm[:, kt * VW + hl * (DK + 1):
                                        kt * VW + hl * (DK + 1) + 65],
                                    Pt[i % 3][:, h * 512:(h + 1) * 512],
                                    start=first, stop=last)
                            return ii
                        pv_tok[i] = em("tensor", fpv, waits=w, inc="st")

                    for j in range(nkt):
                        emit_sc(j)
                        if j >= 2:
                            emit_pv(j - 2)
                    emit_pv(nkt - 2)
                    emit_pv(nkt - 1)
                    i_glob = base_i + nkt

                    for h in range(2):
                        hl = 2 * pair + h
                        w = [pv_tok[i_glob - 1]]
                        if seq >= 1:
                            w.append(store_toks[2 * (seq - 1) + h])
                        tev = em("vector", lambda v, h=h: v.tensor_copy(
                            ostg[h][0:65, :], ps_O[h][0:65, :]), waits=w, inc="sv")
                        evac_toks.append(tev)
                        store_toks.append(dmas(
                            lambda s, h=h, hl=hl, qt=qt: s.dma_start(
                                out=xch_in.ap()[hl * 65:(hl + 1) * 65,
                                                qt * 512:(qt + 1) * 512],
                                in_=ostg[h][0:65, :]),
                            f"dso{h}", waits=[tev]))
                    seq += 1
                cc_toks.append(em(
                    "gpsimd",
                    lambda g, pair=pair: g.collective_compute(
                        "AllGather", ALU.bypass,
                        replica_groups=[[0, 1, 2, 3], [4, 5, 6, 7]],
                        ins=[xch_in.ap()[pair * 130:(pair + 1) * 130, :]],
                        outs=[xch_out[pair].ap()]),
                    waits=[("dso0", P.cnt["dso0"]), ("dso1", P.cnt["dso1"])],
                    inc="scc"))

        for s in ("st", "sv", "sa", "dg", "ds"):
            AE[s] = P.cnt[s]

    with contextlib.ExitStack() as esC:
        sbc = lambda n, sh, dt: esC.enter_context(nc.sbuf_tensor(n, list(sh), dt))
        z_sb = sbc("z_sb", [128, NOT * D], F32)
        xres_b = [sbc(f"xres{i}", [128, D], F32) for i in range(2)]
        yln = sbc("yln", [128, D], F32)
        st12c = sbc("st12c", [128, 6 * NST], F32)
        mvc = sbc("mvc", [128, 2], F32)
        nmc = sbc("nmc", [128, 1], F32)
        rsc = sbc("rsc", [128, 1], F32)
        zbf = sbc("zbf", [128, D], BF16)
        negmu4 = sbc("negmu4", [128, NOT], F32)
        lnv4 = sbc("lnv4", [128, NOT], F32)
        r4 = sbc("r4", [128, NOT], F32)
        rowtr2 = sbc("rowtr2", [128, 128], F32)
        nm4row = [sbc(f"nm4row{i}", [1, 512], BF16) for i in range(2)]
        ro_sb = sbc("ro_sb", [E, 512], F32)
        gtm = sbc("gtm", [128, 4 * E], F32)
        g8 = sbc("g8", [128, 8], F32)
        m8 = sbc("m8", [128, 8], F32)
        gmask = sbc("gmask", [128, 4 * E], F32)
        wtm = sbc("wtm", [128, 4 * E], F32)
        wfm = sbc("wfm", [E, 512], BF16)
        wr_sb = sbc("wr_sb", [LR, 512], F32)
        hs_sb = sbc("hs_sb", [LR, 512], BF16)
        ffn1 = sbc("ffn1", [128, D], F32)
        outb = [sbc(f"outb{i}", [128, D], F32) for i in range(2)]
        d1 = sbc("d1", [128, 1], F32)
        rk_sb = sbc("rk_sb", [1, 1], U32)

        AEw = [("st", AE["st"]), ("sv", AE["sv"]), ("sa", AE["sa"])]

        with contextlib.ExitStack() as esC1:
            sbc1 = lambda n, sh, dt: esC1.enter_context(nc.sbuf_tensor(n, list(sh), dt))
            Och = sbc1("Och", [128, FCH * OWN], BF16)
            sums_sb = [sbc1(f"sums_sb{p}", [8, OWN], BF16) for p in range(2)]
            recf = sbc1("recf", [8, 512], F32)
            esP = esC1.enter_context(contextlib.ExitStack())
            ps_y = [esP.enter_context(nc.psum_tensor(f"ps_y{i}", [128, D], F32))
                    for i in range(2)]
            ps_rc = [esP.enter_context(nc.psum_tensor(f"ps_rc{i}", [128, 512], F32))
                     for i in range(2)]

            roff = nc.alloc_register(mybir.EngineType.SP, "roff")
            rtmp = nc.alloc_register(mybir.EngineType.SP, "rtmp")
            t_rk = dmas(lambda s: s.dma_start(out=rk_sb[:, :], in_=rkoff.ap()), "dsv0")

            def mk_recv(p):
                def f_recv(s):
                    if p == 0:
                        s.reg_load(roff, rk_sb[0:1, 0:1])
                    for k in range(p, FCH, 2):
                        rowbase = 2 * (DK + 1) * (k // 2)
                        s.reg_mov(rtmp, rowbase * S)
                        s.reg_add(rtmp, rtmp, roff)
                        s.dma_start(
                            out=Och[:, k * OWN:(k + 1) * OWN],
                            in_=bass.AP(xch_out[p], rtmp,
                                        [[(DK + 1) * S, 2], [S, DK], [1, OWN]])
                        ).then_inc(SEMS[f"dsv{p}"], 16)
                    s.reg_mov(rtmp, DK * S)
                    s.reg_add(rtmp, rtmp, roff)
                    return s.dma_start(
                        out=sums_sb[p][:, :],
                        in_=bass.AP(xch_out[p], rtmp,
                                    [[2 * (DK + 1) * S, 4], [(DK + 1) * S, 2], [1, OWN]]))
                return f_recv
            t_recv = [None, None]
            for p in range(2):
                P.cnt[f"dsv{p}"] += 16 * (FCH // 2)
                t_recv[p] = em("sync", mk_recv(p),
                               waits=[("scc", p + 1), t_rk], inc=f"dsv{p}", incn=16)

            t_rcb = [None, None]
            for p in range(2):
                for _c in range(OWN // 512):
                    a_ = em("vector", lambda v, p=p, _c=_c: v.tensor_copy(
                        recf[:, :], sums_sb[p][:, _c * 512:(_c + 1) * 512]),
                        waits=[t_recv[p]] + AEw, inc="sv")
                    b_ = em("vector", lambda v: v.reciprocal(recf[:, :], recf[:, :]),
                            waits=[a_], inc="sv")
                    t_rcb[p] = em("vector", lambda v, p=p, _c=_c: v.tensor_copy(
                        sums_sb[p][:, _c * 512:(_c + 1) * 512], recf[:, :]),
                        waits=[b_], inc="sv")

            div_last = None
            div_tts = {}
            di = 0
            for p in range(2):
                for k in range(p, FCH, 2):
                    for nt in range(OWN // 512):
                        w = [t_rcb[p], ("sa", AE["sa"]), L["bsel"], div_tts.get(di - 2)]
                        tmm = em("tensor", lambda t, p=p, k=k, nt=nt, di=di: t.matmul(
                            ps_rc[di % 2][:, :], bsel_sb[0:8, k * 128:(k + 1) * 128],
                            sums_sb[p][:, nt * 512:(nt + 1) * 512],
                            start=True, stop=True),
                            waits=w, inc="st")
                        div_last = em("vector", lambda v, k=k, nt=nt, di=di: v.tensor_tensor(
                            Och[:, k * OWN + nt * 512:k * OWN + (nt + 1) * 512],
                            Och[:, k * OWN + nt * 512:k * OWN + (nt + 1) * 512],
                            ps_rc[di % 2][:, :], ALU.mult),
                            waits=[tmm], inc="sv")
                        div_tts[di] = div_last
                        di += 1

            z_toks = []
            prev_ln = None
            prev_ln2 = [None, None]
            prev_ta3 = None
            xr_war = [None, None]
            for ti in range(NOT):
                sl = ti % 2
                t_xr = dmas(lambda s, ti=ti, sl=sl: s.dma_start(
                    out=xres_b[sl][:, :], in_=x_res.ap()[ti * 128:(ti + 1) * 128, :]),
                    f"dsr{sl}", waits=[xr_war[sl]])
                w = [div_last, L["woT"], prev_ln2[ti % 2]]
                def fwo(t, ti=ti):
                    ii = None
                    for hh in range(NDH):
                        w0 = hh * 512
                        wd = min(512, D - w0)
                        for k in range(FCH):
                            ii = t.matmul(ps_y[ti % 2][:, w0:w0 + wd],
                                          Och[:, k * OWN + ti * 128:k * OWN + (ti + 1) * 128],
                                          woT_sb[:, k * D + w0:k * D + w0 + wd],
                                          start=(k == 0), stop=(k == FCH - 1))
                    return ii
                tmy = em("tensor", fwo, waits=w, inc="st")
                def fst3(v, ti=ti):
                    ii = None
                    for j in range(NST):
                        c0 = j * 512
                        cw = min(512, D - c0)
                        ii = v.bn_stats(st12c[:, 6 * j:6 * j + 6],
                                        ps_y[ti % 2][:, c0:c0 + cw])
                    return ii
                ts1 = em("vector", fst3, waits=[tmy], inc="sv")
                ts2 = em("vector", lambda v: v.bn_aggr(mvc[:, :], st12c[:, 0:6 * NST]),
                         waits=[ts1, prev_ta3], inc="sv")
                ts3 = em("vector", lambda v: v.tensor_scalar(
                    nmc[:, 0:1], mvc[:, 0:1], -1.0, None, ALU.mult), waits=[ts2], inc="sv")
                ta1 = em("scalar", lambda s: s.activation(
                    rsc[:, 0:1], mvc[:, 1:2], AF.Ln, bias=epst[:, 0:1]),
                    waits=[ts2, prev_ln], inc="sa")
                ta2 = em("scalar", lambda s: s.activation(
                    rsc[:, 0:1], rsc[:, 0:1], AF.Exp, scale=-0.5), waits=[ta1], inc="sa")
                ts4 = em("vector", lambda v, ti=ti: v.tensor_scalar(
                    yln[:, :], ps_y[ti % 2][:, :], nmc[:, 0:1], rsc[:, 0:1],
                    ALU.add, ALU.mult),
                    waits=[ts3, ta2], inc="sv")
                prev_ln = ts4
                prev_ln2[ti % 2] = ts4
                prev_ta3 = ta2
                tz = em("vector", lambda v, ti=ti, sl=sl: v.tensor_tensor(
                    z_sb[:, ti * D:(ti + 1) * D], yln[:, :], xres_b[sl][:, :], ALU.add),
                    waits=[ts4, t_xr], inc="sv")
                xr_war[sl] = tz
                z_toks.append(tz)

        zfm = sbc("zfm", [128, DCH * OWN], BF16)
        wo_done_st = ("st", P.cnt["st"])
        with contextlib.ExitStack() as esP:
            ps_zt = [esP.enter_context(nc.psum_tensor(f"ps_zt{i}", [128, 128], BF16))
                     for i in range(2)]
            ps_tr2 = esP.enter_context(nc.psum_tensor("ps_tr2", [128, 128], F32))
            ln4_nm, ln4_r = [], []
            prev_tr = None
            ztev = {}
            nzt = 0
            for ti in range(NOT):
                tc = em("vector", lambda v, ti=ti: v.tensor_copy(
                    zbf[:, :], z_sb[:, ti * D:(ti + 1) * D]),
                    waits=[z_toks[ti], prev_tr, wo_done_st], inc="sv")
                for k in range(DCH):
                    tt = em("tensor", lambda t, k=k, nzt=nzt: t.transpose(
                        ps_zt[nzt % 2][:, :], zbf[:, k * 128:(k + 1) * 128], idenb[:, :]),
                        waits=[tc, L["idb"], ztev.get(nzt - 2)], inc="st")
                    prev_tr = em("vector", lambda v, k=k, ti=ti, nzt=nzt: v.tensor_copy(
                        zfm[:, k * OWN + ti * 128:k * OWN + (ti + 1) * 128],
                        ps_zt[nzt % 2][:, :]),
                        waits=[tt], inc="sv")
                    ztev[nzt] = prev_tr
                    nzt += 1
                def fs4(v, ti=ti):
                    ii = None
                    for j in range(NST):
                        c0 = j * 512
                        cw = min(512, D - c0)
                        ii = v.bn_stats(st12c[:, 6 * j:6 * j + 6],
                                        z_sb[:, ti * D + c0:ti * D + c0 + cw])
                    return ii
                t41 = em("vector", fs4, waits=[z_toks[ti]], inc="sv")
                t42 = em("vector", lambda v: v.bn_aggr(mvc[:, :], st12c[:, 0:6 * NST]),
                         waits=[t41, prev_ta3, ln4_r[-1] if ln4_r else None], inc="sv")
                t43 = em("vector", lambda v, ti=ti: v.tensor_scalar(
                    negmu4[:, ti:ti + 1], mvc[:, 0:1], -1.0, None, ALU.mult),
                    waits=[t42], inc="sv")
                t44 = em("scalar", lambda s, ti=ti: s.activation(
                    lnv4[:, ti:ti + 1], mvc[:, 1:2], AF.Ln, bias=epst[:, 0:1]),
                    waits=[t42], inc="sa")
                t45 = em("scalar", lambda s, ti=ti: s.activation(
                    r4[:, ti:ti + 1], lnv4[:, ti:ti + 1], AF.Exp, scale=-0.5),
                    waits=[t44], inc="sa")
                ln4_nm.append(t43)
                ln4_r.append(t45)
            ztr_sv = ("sv", P.cnt["sv"])

            assert NOT <= 128
            tz1 = em("tensor", lambda t: t.transpose(
                ps_tr2[0:NOT, 0:128], negmu4[:, 0:NOT], idenf[:, :]),
                waits=ln4_nm + [prev_tr], inc="st")
            cz1 = em("vector", lambda v: v.tensor_copy(rowtr2[0:NOT, 0:128],
                                                       ps_tr2[0:NOT, 0:128]),
                     waits=[tz1], inc="sv")
            sz1 = dmas(lambda s: s.dma_start(
                out=zstats_d.ap()[0:1, :].rearrange("o (a b) -> (o a) b", b=128),
                in_=rowtr2[0:NOT, 0:128]), "dst", waits=[cz1])
            tz2 = em("tensor", lambda t: t.transpose(
                ps_tr2[0:NOT, 0:128], r4[:, 0:NOT], idenf[:, :]),
                waits=ln4_r + [cz1], inc="st")
            cz2 = em("vector", lambda v: v.tensor_copy(rowtr2[0:NOT, 0:128],
                                                       ps_tr2[0:NOT, 0:128]),
                     waits=[tz2, sz1], inc="sv")
            sz2 = dmas(lambda s: s.dma_start(
                out=zstats_d.ap()[1:2, :].rearrange("o (a b) -> (o a) b", b=128),
                in_=rowtr2[0:NOT, 0:128]), "dst", waits=[cz2])
            c_mid_st = ("st", P.cnt["st"])

        with contextlib.ExitStack() as esP:
            ps_ffn = esP.enter_context(nc.psum_tensor("ps_ffn", [128, D], F32))
            ps_lr = [esP.enter_context(nc.psum_tensor(f"ps_lr{i}", [LR, 512], F32))
                     for i in range(2)]
            ps_ro = esP.enter_context(nc.psum_tensor("ps_ro", [E, 512], F32))
            ps_g = esP.enter_context(nc.psum_tensor("ps_g", [128, 4 * E], F32))
            ps_wt = esP.enter_context(nc.psum_tensor("ps_wt", [E, 512], F32))
            ps_wb = esP.enter_context(nc.psum_tensor("ps_wb", [LR, 512], F32))

            prev_rows = None
            prev_hs = None
            prev_hs2 = [None, None]
            prev_ffn = None
            out_final = None
            prev_ua2 = [None]
            prev_u4 = [None]
            for g4 in range(cfg.NFG):
                sl = g4 % 2
                l1 = dmag(lambda g, g4=g4, sl=sl: g.dma_start(
                    out=nm4row[sl][:, :],
                    in_=zstats_d.ap()[0:1, g4 * 512:(g4 + 1) * 512]),
                    f"dgz{sl}", waits=[sz1, sz2, prev_rows])

                w = [ztr_sv, l1, L["lrAT"], e_w1lr, c_mid_st, prev_hs2[g4 % 2]]
                def flr(t, g4=g4, sl=sl):
                    for k in range(DCH):
                        t.matmul(ps_lr[g4 % 2][0:LR, :], lrAT_sb[:, k * LR:(k + 1) * LR],
                                 zfm[:, k * OWN + g4 * 512:k * OWN + (g4 + 1) * 512],
                                 start=(k == 0), stop=False)
                    return t.matmul(ps_lr[g4 % 2][0:LR, :], w1lr[0:1, :],
                                    nm4row[sl][0:1, :], start=False, stop=True)
                t_lr = em("tensor", flr, waits=w, inc="st")

                wro = [L["routerWT"], e_w1ro]
                def fro(t, g4=g4, sl=sl):
                    for k in range(DCH):
                        t.matmul(ps_ro[0:E, :], routerWT_sb[:, k * E:(k + 1) * E],
                                 zfm[:, k * OWN + g4 * 512:k * OWN + (g4 + 1) * 512],
                                 start=(k == 0), stop=False)
                    return t.matmul(ps_ro[0:E, :], w1ro[0:1, :], nm4row[sl][0:1, :],
                                    start=False, stop=True)
                t_ro = em("tensor", fro, waits=wro, inc="st")
                t_roe = em("vector", lambda v: v.tensor_copy(ro_sb[0:E, :], ps_ro[0:E, :]),
                           waits=[t_ro], inc="sv")

                tr_toks = []
                for j in range(4):
                    tt = em("tensor", lambda t, j=j: t.transpose(
                        ps_g[:, j * E:(j + 1) * E], ro_sb[0:E, j * 128:(j + 1) * 128],
                        idenf[0:E, 0:E]),
                        waits=[t_roe], inc="st")
                    tr_toks.append(tt)
                tg0 = em("vector", lambda v: v.tensor_copy(gtm[:, :], ps_g[:, :]),
                         waits=tr_toks, inc="sv")
                gk = tg0
                for j in range(4):
                    ti4 = 4 * g4 + j
                    gk = em("vector", lambda v, j=j, ti4=ti4: v.tensor_scalar(
                        gtm[:, j * E:(j + 1) * E], gtm[:, j * E:(j + 1) * E],
                        r4[:, ti4:ti4 + 1], None, ALU.mult),
                        waits=[gk, ln4_r[ti4]], inc="sv")
                gk = em("vector", lambda v: v.tensor_add(gtm[:, :], gtm[:, :], rb6_sb[:, :]),
                        waits=[gk, e_rb6], inc="sv")
                for j in range(4):
                    t1_ = em("vector", lambda v, j=j: v.tensor_reduce(
                        d1[:, 0:1], gtm[:, j * E:(j + 1) * E], AXX, ALU.max, negate=True),
                        waits=[gk], inc="sv")
                    t2_ = em("scalar", lambda s, j=j: s.activation(
                        gtm[:, j * E:(j + 1) * E], gtm[:, j * E:(j + 1) * E], AF.Exp,
                        bias=d1[:, 0:1]),
                        waits=[t1_], inc="sa")
                    t3_ = em("vector", lambda v, j=j: v.tensor_reduce(
                        d1[:, 0:1], gtm[:, j * E:(j + 1) * E], AXX, ALU.add),
                        waits=[t2_], inc="sv")
                    t4_ = em("vector", lambda v: v.reciprocal(d1[:, 0:1], d1[:, 0:1]),
                             waits=[t3_], inc="sv")
                    t5_ = em("vector", lambda v, j=j: v.tensor_scalar(
                        gtm[:, j * E:(j + 1) * E], gtm[:, j * E:(j + 1) * E],
                        d1[:, 0:1], None, ALU.mult),
                        waits=[t4_], inc="sv")
                    t6_ = em("vector", lambda v: v.memset(g8[:, :], -1e30),
                             waits=[t5_], inc="sv")
                    t7_ = em("vector", lambda v, j=j: v.tensor_copy(
                        g8[:, 0:E], gtm[:, j * E:(j + 1) * E]), waits=[t6_], inc="sv")
                    t8_ = em("vector", lambda v: v.max(m8[:, :], g8[:, :]),
                             waits=[t7_], inc="sv")
                    t9_ = em("vector", lambda v, j=j: v.tensor_scalar(
                        gmask[:, j * E:(j + 1) * E], gtm[:, j * E:(j + 1) * E],
                        m8[:, 1:2], None, ALU.is_ge),
                        waits=[t8_], inc="sv")
                    t10_ = em("vector", lambda v, j=j: v.tensor_mul(
                        wtm[:, j * E:(j + 1) * E], gtm[:, j * E:(j + 1) * E],
                        gmask[:, j * E:(j + 1) * E]),
                        waits=[t9_], inc="sv")
                    t11_ = em("vector", lambda v, j=j: v.tensor_reduce(
                        d1[:, 0:1], wtm[:, j * E:(j + 1) * E], AXX, ALU.add),
                        waits=[t10_], inc="sv")
                    t12_ = em("vector", lambda v: v.tensor_scalar(
                        d1[:, 0:1], d1[:, 0:1], 1e-6, None, ALU.add),
                        waits=[t11_], inc="sv")
                    t13_ = em("vector", lambda v: v.reciprocal(d1[:, 0:1], d1[:, 0:1]),
                              waits=[t12_], inc="sv")
                    gk = em("vector", lambda v, j=j: v.tensor_scalar(
                        wtm[:, j * E:(j + 1) * E], wtm[:, j * E:(j + 1) * E],
                        d1[:, 0:1], None, ALU.mult),
                        waits=[t13_], inc="sv")
                wtr = []
                for j in range(4):
                    tt = em("tensor", lambda t, j=j: t.transpose(
                        ps_wt[0:E, j * 128:(j + 1) * 128], wtm[:, j * E:(j + 1) * E],
                        idenf[:, :]),
                        waits=[gk], inc="st")
                    wtr.append(tt)
                twf = em("vector", lambda v: v.tensor_copy(wfm[0:E, :], ps_wt[0:E, :]),
                         waits=wtr, inc="sv")
                def fwb(t):
                    t.matmul(ps_wb[0:LR, :], g6_sb[0:E, :], wfm[0:E, :],
                             start=True, stop=False)
                    return t.matmul(ps_wb[0:LR, :], shsel_sb[0:1, :], onesb[0:1, :],
                                    start=False, stop=True)
                t_wb = em("tensor", fwb, waits=[twf, L["g6"], L["sh"], t_ones], inc="st")
                t_wre = em("vector", lambda v: v.tensor_copy(wr_sb[0:LR, :], ps_wb[0:LR, :]),
                           waits=[t_wb], inc="sv")
                t_hs = em("vector", lambda v, g4=g4: v.tensor_tensor(
                    hs_sb[0:LR, :], ps_lr[g4 % 2][0:LR, :], wr_sb[0:LR, :], ALU.mult),
                    waits=[t_wre, t_lr], inc="sv")
                prev_hs = t_hs
                prev_hs2[g4 % 2] = t_hs

                for tsl in range(4):
                    ti = 4 * g4 + tsl
                    w = [t_hs, L["baseWT"], L["Bstack"], e_w1b[-1], l1]
                    if prev_ffn is not None:
                        w.append(prev_ffn)
                    def fbase(t, g4=g4, tsl=tsl, sl=sl):
                        ii = None
                        for hh in range(NDH):
                            w0 = hh * 512
                            wd = min(512, D - w0)
                            for k in range(DCH):
                                t.matmul(ps_ffn[:, w0:w0 + wd],
                                         zfm[:, k * OWN + g4 * 512 + tsl * 128:
                                             k * OWN + g4 * 512 + (tsl + 1) * 128],
                                         baseWT_sb[:, k * D + w0:k * D + w0 + wd],
                                         start=(k == 0), stop=False)
                            t.matmul(ps_ffn[:, w0:w0 + wd],
                                     nm4row[sl][0:1, tsl * 128:(tsl + 1) * 128],
                                     w1base[0:1, w0:w0 + wd], start=False, stop=False)
                            ii = t.matmul(ps_ffn[:, w0:w0 + wd],
                                          hs_sb[0:LR, tsl * 128:(tsl + 1) * 128],
                                          BstackT_sb[0:LR, w0:w0 + wd],
                                          start=False, stop=True)
                        return ii
                    t_f = em("tensor", fbase, waits=w, inc="st")
                    tf1 = em("vector", lambda v, ti=ti: v.scalar_tensor_tensor(
                        ffn1[:, :], ps_ffn[:, :], r4[:, ti:ti + 1], bbc_sb[:, :],
                        ALU.mult, ALU.add),
                        waits=[t_f, e_bbc, ln4_r[ti]], inc="sv")
                    prev_ffn = tf1
                    def fs5(v):
                        ii = None
                        for j in range(NST):
                            c0 = j * 512
                            cw = min(512, D - c0)
                            ii = v.bn_stats(st12c[:, 6 * j:6 * j + 6], ffn1[:, c0:c0 + cw])
                        return ii
                    u1 = em("vector", fs5, waits=[tf1], inc="sv")
                    u2 = em("vector", lambda v: v.bn_aggr(mvc[:, :], st12c[:, 0:6 * NST]),
                            waits=[u1, prev_ua2[0]], inc="sv")
                    u3 = em("vector", lambda v: v.tensor_scalar(
                        nmc[:, 0:1], mvc[:, 0:1], -1.0, None, ALU.mult),
                        waits=[u2], inc="sv")
                    ua1 = em("scalar", lambda s: s.activation(
                        rsc[:, 0:1], mvc[:, 1:2], AF.Ln, bias=epst[:, 0:1]),
                        waits=[u2, prev_u4[0]], inc="sa")
                    ua2 = em("scalar", lambda s: s.activation(
                        rsc[:, 0:1], rsc[:, 0:1], AF.Exp, scale=-0.5),
                        waits=[ua1], inc="sa")
                    u4 = em("vector", lambda v: v.tensor_scalar(
                        yln[:, :], ffn1[:, :], nmc[:, 0:1], rsc[:, 0:1],
                        ALU.add, ALU.mult),
                        waits=[u3, ua2], inc="sv")
                    prev_ua2[0] = ua2
                    prev_u4[0] = u4
                    u5 = em("vector", lambda v, ti=ti, tsl=tsl: v.tensor_tensor(
                        outb[tsl % 2][:, :], yln[:, :], z_sb[:, ti * D:(ti + 1) * D],
                        ALU.add),
                        waits=[u4, ("dou%d" % (tsl % 2), P.cnt["dou%d" % (tsl % 2)])],
                        inc="sv")
                    out_final = dmas(lambda s, ti=ti, tsl=tsl: s.dma_start(
                        out=out_t.ap()[ti * 128:(ti + 1) * 128, :],
                        in_=outb[tsl % 2][:, :]), "dou%d" % (tsl % 2), waits=[u5])
                prev_rows = prev_ffn

            wt("sync", [(s, P.cnt[s]) for s in list(P.cnt.keys())
                        if s.startswith("d") or s == "scc"])

    sems = {name: es.enter_context(nc.semaphore(name)) for name in P.cnt}
    SEMS.update(sems)
    nc._keepalive_es = es
    return nc, sems, P


def emit(nc, sems, P):
    with nc.Block() as blk:
        def mk(name):
            def run(eng):
                seen = {}
                for ws, fn, tok, incn in P.ops[name]:
                    for s, v in ws:
                        if v > seen.get(s, -1):
                            eng.wait_ge(sems[s], v)
                            seen[s] = v
                    if fn is None:
                        continue
                    inst = fn(eng)
                    if tok is not None:
                        inst.then_inc(sems[tok[0]], incn)
            return run
        blk.tensor(mk("tensor"))
        blk.vector(mk("vector"))
        blk.scalar(mk("scalar"))
        blk.gpsimd(mk("gpsimd"))
        blk.sync(mk("sync"))
    return nc



def host_consts(cfg: Cfg):
    S, DK = cfg.S, cfg.DK
    rope_dim = DK // 2
    inv = 1.0 / ROPE_BASE ** (np.arange(0, rope_dim, 2, dtype=np.float32) / rope_dim)
    t = np.arange(S, dtype=np.float32)
    fr = np.outer(t, inv)
    emb = np.concatenate([fr, fr], axis=1)
    cos = np.cos(emb).astype(np.float32)
    sin = np.sin(emb).astype(np.float32)
    C = np.ones((128, S), np.float32)
    Sg = np.zeros((128, S), np.float32)
    for blk in (0, 64):
        C[blk:blk + 32] = cos.T
        Sg[blk:blk + 16] = -sin.T[:16]
        Sg[blk + 16:blk + 32] = sin.T[16:32]
    masks = np.zeros((4 * 128, 512), np.float32)
    p = np.arange(128)[:, None]
    c = np.arange(512)[None, :]
    for i in range(4):
        masks[i * 128:(i + 1) * 128] = (c >= 128 * i + p)
    bsel = np.zeros((16, 8 * 128), np.float32)
    for k in range(8):
        for a in range(2):
            row = 2 * (k // 2) + a
            bsel[row, k * 128 + a * 64:k * 128 + (a + 1) * 64] = 1.0
    g6 = np.zeros((cfg.E, cfg.LR), np.float32)
    for e in range(cfg.E):
        g6[e, cfg.R + cfg.R * e:cfg.R + cfg.R * (e + 1)] = 1.0
    sh = np.zeros((1, cfg.LR), np.float32)
    sh[0, :cfg.R] = 1.0
    return {
        "ropeC": C.astype(BF), "ropeS": Sg.astype(BF),
        "masks": masks.astype(BF),
        "iden_bf": np.eye(128, dtype=np.float32).astype(BF),
        "iden_f": np.eye(128, dtype=np.float32),
        "bsel": bsel.astype(BF), "g6sel": g6.astype(BF), "shsel": sh.astype(BF),
    }


def core_inputs(cfg: Cfg, A, c, consts):
    b, hg = c // 4, c % 4
    x = np.asarray(A["L0_attn_x"][b])
    Wqkv = np.asarray(A["L0_attn_W_qkv"])
    hs = [4 * hg + i for i in range(cfg.HPC)]
    qp = []
    for pr in range(2):
        h0, h1 = hs[2 * pr], hs[2 * pr + 1]
        qp.append(np.concatenate([Wqkv[0, h0].T, Wqkv[0, h1].T], axis=1))
        qp.append(np.concatenate([Wqkv[1, h0].T, Wqkv[1, h1].T], axis=1))
    wqkT = np.concatenate([qp[0], qp[1], qp[2], qp[3]], axis=1)
    wvT = np.concatenate([Wqkv[2, h].T for h in hs], axis=1)
    d = {
        "x_fm": np.ascontiguousarray(x.T),
        "x_tm": np.ascontiguousarray(x),
        "x_res": np.ascontiguousarray(x[hg * cfg.OWN:(hg + 1) * cfg.OWN]),
        "rkoff": np.array([[hg * cfg.OWN]], np.uint32),
        "wqkT": np.ascontiguousarray(wqkT),
        "wvT": np.ascontiguousarray(wvT),
        "woT": np.ascontiguousarray(np.asarray(A["L0_attn_W_o"]).T),
        "baseWT": np.ascontiguousarray(np.asarray(A["L0_ffn_base_W"]).T),
        "baseb": np.asarray(A["L0_ffn_base_b"]).reshape(1, -1),
        "lrAT": np.ascontiguousarray(np.concatenate(
            [np.asarray(A["L0_ffn_shared_A"]).T] +
            [np.asarray(A["L0_ffn_routed_A"][e]).T for e in range(cfg.E)], axis=1)),
        "routerWT": np.ascontiguousarray(np.asarray(A["L0_ffn_router_W"]).T),
        "routerb": np.asarray(A["L0_ffn_router_b"]).reshape(1, -1),
        "BstackT": np.ascontiguousarray(np.concatenate(
            [np.asarray(A["L0_ffn_shared_B"]).T] +
            [np.asarray(A["L0_ffn_routed_B"][e]).T for e in range(cfg.E)], axis=0)),
    }
    d.update(consts)
    return d


_BUILD_CACHE = {}


def kernel(**inputs):
    cfg = CFG
    key = cfg.S
    if key not in _BUILD_CACHE:
        nc, sems, P = build(cfg)
        emit(nc, sems, P)
        _BUILD_CACHE[key] = nc
    nc = _BUILD_CACHE[key]
    consts = host_consts(cfg)
    in_maps = [core_inputs(cfg, inputs, c, consts) for c in range(8)]
    res = run_bass_kernel_spmd(nc, in_maps, core_ids=list(range(8)))
    out = np.empty((2, cfg.S, cfg.D), np.float32)
    for c in range(8):
        b, hg = c // 4, c % 4
        out[b, hg * cfg.OWN:(hg + 1) * cfg.OWN] = res.results[c]["out"]
    return out


# version 08
# speedup vs baseline: 1.0394x; 1.0394x over previous
import math
from dataclasses import dataclass

import numpy as np
import ml_dtypes

import concourse.bass as bass
import concourse.mybir as mybir
from concourse.bass_utils import run_bass_kernel_spmd

F32 = mybir.dt.float32
BF16 = mybir.dt.bfloat16
U32 = mybir.dt.uint32
AF = mybir.ActivationFunctionType
ALU = mybir.AluOpType
AXX = mybir.AxisListType.X
BF = ml_dtypes.bfloat16

ROPE_BASE = 6000.0
EPS = 1e-5


@dataclass
class Cfg:
    S: int = 8192
    D: int = 1024
    F: int = 1024
    NH: int = 16
    HPC: int = 4
    DK: int = 64
    E: int = 6
    R: int = 16

    @property
    def DCH(self): return self.D // 128
    @property
    def FCH(self): return self.F // 128
    @property
    def NG(self): return self.S // 512
    @property
    def NQT(self): return self.S // 512
    @property
    def NKT(self): return self.S // 128
    @property
    def OWN(self): return self.S // 4
    @property
    def NOT(self): return self.OWN // 128
    @property
    def NFG(self): return self.OWN // 512
    @property
    def NST(self): return -(-self.D // 512)
    @property
    def LR(self): return self.R * (1 + self.E)
    @property
    def VW(self): return self.HPC * (self.DK + 1)
    @property
    def NTS(self): return self.S // 128


CFG = Cfg()
ENGS = ("tensor", "vector", "scalar", "gpsimd", "sync")
SELF_SEM = {"vector": "sv", "scalar": "sa"}


class Plan:
    def __init__(self):
        import collections
        self.ops = {e: [] for e in ENGS}
        self.cnt = collections.defaultdict(int)

    def em(self, eng, fn, waits=(), inc=None, incn=1):
        ws = [w for w in waits if w is not None]
        if eng in SELF_SEM:
            ws.append((SELF_SEM[eng], self.cnt[SELF_SEM[eng]]))
        tok = None
        if inc is not None:
            self.cnt[inc] += incn
            tok = (inc, self.cnt[inc])
        self.ops[eng].append((ws, fn, tok, incn))
        return tok

    def wait(self, eng, waits):
        self.ops[eng].append(([w for w in waits if w is not None], None, None, 0))


def build(cfg: Cfg):
    SEMS = {}
    nc = bass.Bass("TRN2", target_bir_lowering=False)
    S, D, Fd = cfg.S, cfg.D, cfg.F
    DCH, FCH, NST = cfg.DCH, cfg.FCH, cfg.NST
    NG, NQT, NKT, OWN, NOT = cfg.NG, cfg.NQT, cfg.NKT, cfg.OWN, cfg.NOT
    HPC, DK, LR, E, R, VW, NTS = cfg.HPC, cfg.DK, cfg.LR, cfg.E, cfg.R, cfg.VW, cfg.NTS
    NDH = -(-D // 512)

    din = lambda n, sh, dt=F32: nc.dram_tensor(n, list(sh), dt, kind="ExternalInput")
    x_fm = din("x_fm", [D, S]);           x_tm = din("x_tm", [S, D])
    x_res = din("x_res", [OWN, D]);       rkoff = din("rkoff", [1, 1], U32)
    wqkT = din("wqkT", [D, 512]);         wvT = din("wvT", [D, HPC * DK])
    woT = din("woT", [Fd, D]);            baseWT = din("baseWT", [D, D])
    baseb = din("baseb", [1, D]);         lrAT = din("lrAT", [D, LR])
    routerWT = din("routerWT", [D, E]);   routerb = din("routerb", [1, E])
    BstackT = din("BstackT", [LR, D])
    ropeC = din("ropeC", [128, S], BF16); ropeS = din("ropeS", [128, S], BF16)
    masksI = din("masks", [512, 512], BF16)
    iden_bf = din("iden_bf", [128, 128], BF16)
    iden_f = din("iden_f", [128, 128], F32)
    bselI = din("bsel", [16, 8 * 128], BF16)
    g6I = din("g6sel", [E, LR], BF16);    shI = din("shsel", [1, LR], BF16)
    out_t = nc.dram_tensor("out", [OWN, D], F32, kind="ExternalOutput")

    stats_d = nc.dram_tensor("stats_d", [2, S], F32)
    zstats_d = nc.dram_tensor("zstats_d", [2, OWN], F32)
    xch_in = nc.dram_tensor("xch_in", [HPC * (DK + 1), S], BF16)
    xch_out = [nc.dram_tensor(f"xch_out{p}", [8 * 2 * (DK + 1), S], BF16)
               for p in range(2)]

    P = Plan()
    em, wt = P.em, P.wait

    import contextlib
    es = contextlib.ExitStack()
    sb = lambda n, sh, dt: es.enter_context(nc.sbuf_tensor(n, list(sh), dt))

    onesb = sb("onesb", [1, 512], BF16)
    onescol = sb("onescol", [128, 1], BF16)
    epst = sb("epst", [128, 1], F32)
    idenb = sb("idenb", [128, 128], BF16)
    idenf = sb("idenf", [128, 128], F32)
    bneg = sb("bneg", [128, 1], F32)
    r_cols = sb("r_cols", [128, NTS], F32)
    woT_sb = sb("woT_sb", [128, FCH * D], BF16)
    baseWT_sb = sb("baseWT_sb", [128, DCH * D], BF16)
    lrAT_sb = sb("lrAT_sb", [128, DCH * LR], BF16)
    routerWT_sb = sb("routerWT_sb", [128, DCH * E], BF16)
    BstackT_sb = sb("BstackT_sb", [LR, D], BF16)
    bsel_sb = sb("bsel_sb", [16, 8 * 128], BF16)
    g6_sb = sb("g6_sb", [E, LR], BF16)
    shsel_sb = sb("shsel_sb", [1, LR], BF16)
    rb6_sb = sb("rb6_sb", [128, 4 * E], F32)
    bbc_sb = sb("bbc_sb", [128, D], BF16)
    w1base = sb("w1base", [1, D], BF16)
    w1lr = sb("w1lr", [1, LR], BF16)
    w1ro = sb("w1ro", [1, E], BF16)
    bscb = sb("bscb", [1, 1], BF16)

    dmag = lambda fn, ch, waits=(): em("gpsimd", fn, waits=waits, inc=ch, incn=16)
    dmas = lambda fn, ch, waits=(): em("sync", fn, waits=waits, inc=ch, incn=16)

    def chunked_load(dst, src_t, nchunk, width, ch, q="g"):
        f = lambda g: g.dma_start(
            out=dst[:, :].rearrange("p (c n) -> p c n", c=nchunk),
            in_=src_t.ap().rearrange("(c p) n -> p c n", p=128))
        return (dmag if q == "g" else dmas)(f, ch)

    L = {}
    L["woT"] = chunked_load(woT_sb, woT, FCH, D, "dgw")
    L["baseWT"] = chunked_load(baseWT_sb, baseWT, DCH, D, "dgw")
    L["lrAT"] = chunked_load(lrAT_sb, lrAT, DCH, LR, "dgw")
    L["routerWT"] = chunked_load(routerWT_sb, routerWT, DCH, E, "dgw")
    L["Bstack"] = dmag(lambda g: g.dma_start(out=BstackT_sb[:, :], in_=BstackT.ap()), "dgw")
    L["bbrow"] = dmag(lambda g: g.dma_start(out=bbrow[:, :], in_=baseb.ap()), "dgw")
    L["rbrow"] = dmag(lambda g: g.dma_start(out=rbrow[:, :], in_=routerb.ap()), "dgw")
    L["bsel"] = dmas(lambda s: s.dma_start(out=bsel_sb[:, :], in_=bselI.ap()), "dsw")
    L["g6"] = dmas(lambda s: s.dma_start(out=g6_sb[:, :], in_=g6I.ap()), "dsw")
    L["sh"] = dmas(lambda s: s.dma_start(out=shsel_sb[:, :], in_=shI.ap()), "dsw")
    L["idb"] = dmas(lambda s: s.dma_start(out=idenb[:, :], in_=iden_bf.ap()), "dsw")
    L["idf"] = dmas(lambda s: s.dma_start(out=idenf[:, :], in_=iden_f.ap()), "dsw")

    t_ones = em("vector", lambda v: v.memset(onesb[:, :], 1.0), inc="sv")
    t_onesc = em("vector", lambda v: v.memset(onescol[:, :], 1.0), inc="sv")
    t_eps = em("vector", lambda v: v.memset(epst[:, :], EPS), inc="sv")

    AE = {}

    with contextlib.ExitStack() as esA:
        sba = lambda n, sh, dt: esA.enter_context(nc.sbuf_tensor(n, list(sh), dt))
        rbrow = sba("rbrow", [1, E], BF16)
        bbrow = sba("bbrow", [1, D], BF16)
        Qp = [sba(f"Qp{p}", [128, S], BF16) for p in range(2)]
        Kp = [sba(f"Kp{p}", [128, S], BF16) for p in range(2)]
        Vtm = sba("Vtm", [128, NTS * VW], BF16)
        wqk_sb = sba("wqk_sb", [128, DCH * 512], BF16)
        wv_sb = sba("wv_sb", [128, DCH * HPC * DK], BF16)
        xbuf = [sba(f"xbuf{i}", [128, DCH * 512], BF16) for i in range(2)]
        ropeCb = [sba(f"ropeCb{i}", [128, 512], BF16) for i in range(2)]
        ropeSb = [sba(f"ropeSb{i}", [128, 512], BF16) for i in range(2)]
        nmrow = [sba(f"nmrow{i}", [1, 512], BF16) for i in range(2)]
        rrow = [sba(f"rrow{i}", [1, 512], BF16) for i in range(2)]
        rb_sb = sba("rb_sb", [128, 512], F32)
        ropet = [sba(f"ropet{i}", [128, 512], BF16) for i in range(2)]
        w1qk = sba("w1qk", [1, 512], BF16)
        w1v = sba("w1v", [1, HPC * DK], BF16)
        Pt = [sba(f"Pt{i}", [128, 1024], BF16) for i in range(3)]
        ostg = [sba(f"ostg{i}", [65, 512], BF16) for i in range(2)]
        mask_sb = sba("mask_sb", [128, 4 * 512], BF16)
        qsq = sba("qsq", [128, 512], BF16)
        bcol = sba("bcol", [1, 4 * NG], F32)
        bsc = sba("bsc", [1, 1], F32)

        t_msk = dmas(lambda s: s.dma_start(
            out=mask_sb[:, :].rearrange("p (i n) -> p i n", i=4),
            in_=masksI.ap().rearrange("(i p) n -> p i n", p=128)), "dsw")
        t_wqk = chunked_load(wqk_sb, wqkT, DCH, 512, "dgw")
        t_wv = chunked_load(wv_sb, wvT, DCH, HPC * DK, "dgw")
        _gw = ("dgw", P.cnt["dgw"]); _sw = ("dsw", P.cnt["dsw"])
        for _k in list(L):
            L[_k] = _gw if L[_k][0] == "dgw" else _sw
        t_msk, t_wqk, t_wv = _sw, _gw, _gw

        with contextlib.ExitStack() as esP:
            ps_w1 = esP.enter_context(nc.psum_tensor("ps_w1", [1, 512], F32))
            ps_bb = esP.enter_context(nc.psum_tensor("ps_bb", [128, 512], F32))

            _w1prev = [None]
            def w1row(dst, wsb, width, stride, ldtok, w0=0):
                def fn(t):
                    ii = None
                    for k in range(DCH):
                        ii = t.matmul(ps_w1[0:1, 0:width], onescol[:, 0:1],
                                      wsb[:, k * stride + w0:k * stride + w0 + width],
                                      start=(k == 0), stop=(k == DCH - 1))
                    return ii
                tok = em("tensor", fn, waits=[ldtok, t_onesc, _w1prev[0]], inc="st")
                ev = em("vector", lambda v: v.tensor_copy(dst, ps_w1[0:1, 0:width]),
                        waits=[tok], inc="sv")
                _w1prev[0] = ev
                return ev

            e_w1qk = w1row(w1qk[0:1, :], wqk_sb, 512, 512, t_wqk)
            e_w1v = w1row(w1v[0:1, :], wv_sb, HPC * DK, HPC * DK, t_wv)
            e_w1b = []
            for h in range(NDH):
                w0 = h * 512
                wd = min(512, D - w0)
                e_w1b.append(w1row(w1base[0:1, w0:w0 + wd], baseWT_sb, wd, D,
                                   L["baseWT"], w0=w0))
            e_w1lr = w1row(w1lr[0:1, :], lrAT_sb, LR, LR, L["lrAT"])
            e_w1ro = w1row(w1ro[0:1, :], routerWT_sb, E, E, L["routerWT"])

            t1 = em("tensor", lambda t: t.matmul(ps_bb[:, 0:E], onesb[0:1, 0:128],
                                                 rbrow[0:1, :], start=True, stop=True),
                    waits=[L["rbrow"], t_ones, e_w1ro], inc="st")
            e_rb6 = em("vector", lambda v: v.tensor_copy(
                rb6_sb[:, :].rearrange("p (j e) -> p j e", j=4),
                ps_bb[:, 0:E].unsqueeze(1).broadcast_to([128, 4, E])),
                waits=[t1], inc="sv")
            e_bbc = None
            for h in range(NDH):
                w0 = h * 512
                wd = min(512, D - w0)
                t2 = em("tensor", lambda t, w0=w0, wd=wd: t.matmul(
                    ps_bb[:, 0:wd], onesb[0:1, 0:128], bbrow[0:1, w0:w0 + wd],
                    start=True, stop=True),
                    waits=[L["bbrow"], t_ones, e_rb6 if h == 0 else e_bbc], inc="st")
                e_bbc = em("vector", lambda v, w0=w0, wd=wd: v.tensor_copy(
                    bbc_sb[:, w0:w0 + wd], ps_bb[:, 0:wd]), waits=[t2], inc="sv")

        with contextlib.ExitStack() as esS, contextlib.ExitStack() as esP:
            sbs = lambda n, sh, dt: esS.enter_context(nc.sbuf_tensor(n, list(sh), dt))
            xtbuf = [sbs(f"xtbuf{i}", [128, D], F32) for i in range(2)]
            stat12 = sbs("stat12", [128, 6 * NST], F32)
            mvall = sbs("mvall", [128, 2 * NTS], F32)
            negmu_c = sbs("negmu_c", [128, NTS], F32)
            lnv_c = sbs("lnv_c", [128, NTS], F32)
            rowtr = sbs("rowtr", [128, 128], F32)
            ps_tr = esP.enter_context(nc.psum_tensor("ps_tr", [128, 128], F32))

            prev_aggr = [None, None]
            for ts in range(NTS):
                sl = ts % 2
                ld = dmas(lambda s, ts=ts, sl=sl: s.dma_start(
                    out=xtbuf[sl][:, :], in_=x_tm.ap()[ts * 128:(ts + 1) * 128, :]),
                    f"dsx{sl}", waits=[prev_aggr[sl]])
                def fstat(v, sl=sl):
                    ii = None
                    for j in range(NST):
                        c0 = j * 512
                        cw = min(512, D - c0)
                        ii = v.bn_stats(stat12[:, 6 * j:6 * j + 6], xtbuf[sl][:, c0:c0 + cw])
                    return ii
                tstat = em("vector", fstat, waits=[ld], inc="sv")
                taggr = em("vector", lambda v, ts=ts: v.bn_aggr(
                    mvall[:, 2 * ts:2 * ts + 2], stat12[:, 0:6 * NST]),
                    waits=[tstat], inc="sv")
                prev_aggr[sl] = taggr
            allstats = prev_aggr[(NTS - 1) % 2]
            t_nm = em("vector", lambda v: v.tensor_scalar(
                negmu_c[:, :], mvall[:, 0:2 * NTS:2], -1.0, None, ALU.mult),
                waits=[allstats], inc="sv")
            t_lv = em("scalar", lambda s: s.activation(
                lnv_c[:, :], mvall[:, 1:2 * NTS:2], AF.Ln, bias=epst[:, 0:1]),
                waits=[allstats, t_eps], inc="sa")
            t_rr = em("scalar", lambda s: s.activation(
                r_cols[:, 0:NTS], lnv_c[:, :], AF.Exp, scale=-0.5),
                waits=[t_lv], inc="sa")
            assert NTS <= 128
            tk1 = em("tensor", lambda t: t.transpose(ps_tr[0:NTS, 0:128], negmu_c[:, 0:NTS],
                                                     idenf[:, :]),
                     waits=[t_nm, L["idf"]], inc="st")
            ck1 = em("vector", lambda v: v.tensor_copy(rowtr[0:NTS, 0:128],
                                                       ps_tr[0:NTS, 0:128]),
                     waits=[tk1], inc="sv")
            st1 = dmas(lambda s: s.dma_start(
                out=stats_d.ap()[0:1, :].rearrange("o (a b) -> (o a) b", b=128),
                in_=rowtr[0:NTS, 0:128]), "dst", waits=[ck1])
            tk2 = em("tensor", lambda t: t.transpose(ps_tr[0:NTS, 0:128], r_cols[:, 0:NTS],
                                                     idenf[:, :]),
                     waits=[t_rr, ck1], inc="st")
            ck2 = em("vector", lambda v: v.tensor_copy(rowtr[0:NTS, 0:128],
                                                       ps_tr[0:NTS, 0:128]),
                     waits=[tk2, st1], inc="sv")
            st2 = dmas(lambda s: s.dma_start(
                out=stats_d.ap()[1:2, :].rearrange("o (a b) -> (o a) b", b=128),
                in_=rowtr[0:NTS, 0:128]), "dst", waits=[ck2])

        rope_done = []
        with contextlib.ExitStack() as esP:
            ps_qkv = [esP.enter_context(nc.psum_tensor(f"ps_qkv{m}", [128, 512], F32))
                      for m in range(4)]
            ps_V = [esP.enter_context(nc.psum_tensor(f"ps_V{i}", [128, HPC * DK], F32))
                    for i in range(2)]
            ps_rb = esP.enter_context(nc.psum_tensor("ps_rb", [128, 512], F32))
            ps_b = esP.enter_context(nc.psum_tensor("ps_b", [1, 512], F32))

            t_vms = em("vector", lambda v: v.memset(Vtm[:, :], 1.0), inc="sv")
            qk_evac, v_evac = {}, {}
            rb_prev = None
            bnd_mm_prev = None
            bnd_red_last = None
            for g in range(NG):
                sl = g % 2
                wx = []
                if g >= 2:
                    wx = [qk_evac[(g - 2, 3)], v_evac[4 * (g - 2) + 3],
                          rope_done[4 * (g - 1) - 1]]
                xl = dmag(lambda gg, g=g, sl=sl: gg.dma_start(
                    out=xbuf[sl][:, :].rearrange("p (c n) -> p c n", c=DCH),
                    in_=x_fm.ap().rearrange("(c p) s -> p c s", p=128)
                        [:, :, g * 512:(g + 1) * 512]), f"dgx{sl}", waits=wx)
                rl1 = dmag(lambda gg, g=g, sl=sl: gg.dma_start(
                    out=nmrow[sl][:, :], in_=stats_d.ap()[0:1, g * 512:(g + 1) * 512]),
                    f"dgr{sl}", waits=wx + [st1])
                rl2 = dmag(lambda gg, g=g, sl=sl: gg.dma_start(
                    out=rrow[sl][:, :], in_=stats_d.ap()[1:2, g * 512:(g + 1) * 512]),
                    f"dgs{sl}", waits=[st2, rb_prev])
                wct = [rope_done[4 * (g - 1) - 1]] if g >= 2 else []
                ct = dmas(lambda s, g=g, sl=sl: s.dma_start(
                    out=ropeCb[sl][:, :], in_=ropeC.ap()[:, g * 512:(g + 1) * 512]),
                    f"dtc{sl}", waits=wct)
                st_ = dmas(lambda s, g=g, sl=sl: s.dma_start(
                    out=ropeSb[sl][:, :], in_=ropeS.ap()[:, g * 512:(g + 1) * 512]),
                    f"dts{sl}", waits=wct)

                trb = em("tensor", lambda t, sl=sl: t.matmul(
                    ps_rb[:, :], onesb[0:1, 0:128], rrow[sl][0:1, :],
                    start=True, stop=True),
                    waits=[rl2, t_ones, rb_prev], inc="st")
                tre = em("vector", lambda v: v.tensor_copy(rb_sb[:, :], ps_rb[:, :]),
                         waits=[trb], inc="sv")
                rb_prev = tre
                tcC = em("vector", lambda v, sl=sl: v.tensor_tensor(
                    ropeCb[sl][:, :], ropeCb[sl][:, :], rb_sb[:, :], ALU.mult),
                    waits=[ct, tre], inc="sv")
                tcS = em("vector", lambda v, sl=sl: v.tensor_tensor(
                    ropeSb[sl][:, :], ropeSb[sl][:, :], rb_sb[:, :], ALU.mult),
                    waits=[st_, tcC], inc="sv")

                for m in range(4):
                    wqkv = [xl, rl1, t_wqk, e_w1qk]
                    if g >= 1:
                        wqkv.append(qk_evac[(g - 1, m)])
                    def fqk(t, m=m, sl=sl):
                        for k in range(DCH):
                            t.matmul(ps_qkv[m][:, :],
                                     wqk_sb[:, k * 512 + m * 128:k * 512 + (m + 1) * 128],
                                     xbuf[sl][:, k * 512:(k + 1) * 512],
                                     start=(k == 0), stop=False)
                        return t.matmul(ps_qkv[m][:, :], w1qk[0:1, m * 128:(m + 1) * 128],
                                        nmrow[sl][0:1, :], start=False, stop=True)
                    tmm = em("tensor", fqk, waits=wqkv, inc="st")
                    pair, isK = m // 2, (m % 2 == 1)
                    dst = (Kp if isK else Qp)[pair]
                    qk_evac[(g, m)] = em("scalar", lambda s, dst=dst, m=m, g=g: s.activation(
                        dst[:, g * 512:(g + 1) * 512], ps_qkv[m][:, :], AF.Copy),
                        waits=[tmm], inc="sa")

                for tsl in range(4):
                    ts = 4 * g + tsl
                    wv_ = [xl, rl1, t_wv, e_w1v, t_vms]
                    if ts >= 2:
                        wv_.append(v_evac[ts - 2])
                    def fv(t, tsl=tsl, sl=sl):
                        for k in range(DCH):
                            t.matmul(ps_V[tsl % 2][:, :],
                                     xbuf[sl][:, k * 512 + tsl * 128:k * 512 + (tsl + 1) * 128],
                                     wv_sb[:, k * HPC * DK:(k + 1) * HPC * DK],
                                     start=(k == 0), stop=False)
                        return t.matmul(ps_V[tsl % 2][:, :],
                                        nmrow[sl][0:1, tsl * 128:(tsl + 1) * 128],
                                        w1v[0:1, :], start=False, stop=True)
                    tmv = em("tensor", fv, waits=wv_, inc="st")
                    v_evac[ts] = em("scalar", lambda s, ts=ts, tsl=tsl: s.activation(
                        Vtm[:, ts * VW:(ts + 1) * VW]
                            .rearrange("p (h d) -> p h d", h=HPC)[:, :, 0:DK],
                        ps_V[tsl % 2][:, :].rearrange("p (h d) -> p h d", h=HPC),
                        AF.Copy, scale=r_cols[:, ts:ts + 1]),
                        waits=[tmv, t_rr, t_vms], inc="sa")

                shmask = list(range(16, 32)) + list(range(16))
                for pair in range(2):
                    for which, base in ((0, Qp[pair]), (1, Kp[pair])):
                        wdep = qk_evac[(g, pair * 2 + which)]
                        t1_ = em("vector", lambda v, base=base, g=g: v.stream_shuffle(
                            ropet[0][:, :], base[:, g * 512:(g + 1) * 512], shmask),
                            waits=[wdep, tcS], inc="sv")
                        t2_ = em("vector", lambda v, sl=sl: v.tensor_tensor(
                            ropet[0][:, :], ropet[0][:, :], ropeSb[sl][:, :], ALU.mult),
                            waits=[t1_], inc="sv")
                        t3_ = em("vector", lambda v, base=base, g=g, sl=sl: v.tensor_tensor(
                            ropet[1][:, :], base[:, g * 512:(g + 1) * 512],
                            ropeCb[sl][:, :], ALU.mult),
                            waits=[t2_], inc="sv")
                        t4_ = em("vector", lambda v, base=base, g=g: v.tensor_tensor(
                            base[:, g * 512:(g + 1) * 512], ropet[0][:, :],
                            ropet[1][:, :], ALU.add),
                            waits=[t3_], inc="sv")
                        rope_done.append(t4_)

                for m in range(4):
                    pair, isK = m // 2, (m % 2 == 1)
                    srcT = (Kp if isK else Qp)[pair]
                    tsq = em("vector", lambda v, srcT=srcT, g=g: v.tensor_tensor(
                        qsq[:, :], srcT[:, g * 512:(g + 1) * 512],
                        srcT[:, g * 512:(g + 1) * 512], ALU.mult),
                        waits=[rope_done[4 * g + m], bnd_mm_prev], inc="sv")
                    bnd_mm = em("tensor", lambda t: t.matmul(
                        ps_b[0:1, :], onescol[:, 0:1], qsq[:, :], start=True, stop=True),
                        waits=[tsq, bnd_red_last, t_onesc], inc="st")
                    bnd_mm_prev = bnd_mm
                    bnd_red_last = em("vector", lambda v, g=g, m=m: v.tensor_reduce(
                        bcol[0:1, 4 * g + m:4 * g + m + 1], ps_b[0:1, :], AXX, ALU.max),
                        waits=[bnd_mm], inc="sv")
        qkv_sv = ("sv", P.cnt["sv"])
        qkv_sa = ("sa", P.cnt["sa"])
        qkv_st = ("st", P.cnt["st"])

        with contextlib.ExitStack() as esP:
            ps_bc = esP.enter_context(nc.psum_tensor("ps_bc", [128, 1], F32))
            tq = em("vector", lambda v: v.tensor_reduce(
                bsc[0:1, 0:1], bcol[0:1, 0:4 * NG:2], AXX, ALU.max),
                waits=[bnd_red_last, qkv_sv], inc="sv")
            tk = em("vector", lambda v: v.tensor_reduce(
                bcol[0:1, 0:1], bcol[0:1, 1:4 * NG:2], AXX, ALU.max),
                waits=[tq], inc="sv")
            tm = em("vector", lambda v: v.tensor_mul(bsc[0:1, 0:1], bsc[0:1, 0:1],
                                                     bcol[0:1, 0:1]), waits=[tk], inc="sv")
            tl = em("scalar", lambda s: s.activation(bsc[0:1, 0:1], bsc[0:1, 0:1], AF.Ln),
                    waits=[tm], inc="sa")
            te = em("scalar", lambda s: s.activation(bsc[0:1, 0:1], bsc[0:1, 0:1],
                                                     AF.Exp, scale=0.5),
                    waits=[tl], inc="sa")
            tcst = em("vector", lambda v: v.tensor_copy(bscb[0:1, 0:1], bsc[0:1, 0:1]),
                      waits=[te], inc="sv")
            tbc = em("tensor", lambda t: t.matmul(ps_bc[:, 0:1], onesb[0:1, 0:128],
                                                  bscb[0:1, 0:1], start=True, stop=True),
                     waits=[tcst], inc="st")
            bound_sv = em("vector", lambda v: v.tensor_scalar(
                bneg[:, 0:1], ps_bc[:, 0:1], -0.125, None, ALU.mult),
                waits=[tbc], inc="sv")

        with contextlib.ExitStack() as esP:
            ps_S = [esP.enter_context(nc.psum_tensor(f"ps_S{i}", [128, 1024], F32))
                    for i in range(3)]
            ps_O = [esP.enter_context(nc.psum_tensor(f"ps_O{h}", [65, 512], F32))
                    for h in range(2)]

            i_glob = 0
            seq = 0
            score_tok, exp_tok, mask_tok, pv_tok = {}, {}, {}, {}
            evac_toks, store_toks = [], []
            cc_toks = []
            for pair in range(2):
                for qt in range(NQT):
                    nkt = 4 * (qt + 1)
                    base_i = i_glob

                    def emit_sc(j):
                        kt = j
                        i = base_i + j
                        w = [bound_sv]
                        if i == 0:
                            w += [rope_done[-1], qkv_sv, qkv_sa]
                        if i - 3 >= 0:
                            w.append(exp_tok[i - 3])
                        def fsc(t, pair=pair, qt=qt, kt=kt, i=i):
                            ii = None
                            for h in range(2):
                                ii = t.matmul(
                                    ps_S[i % 3][:, h * 512:(h + 1) * 512],
                                    Kp[pair][h * 64:(h + 1) * 64,
                                             kt * 128:(kt + 1) * 128],
                                    Qp[pair][h * 64:(h + 1) * 64,
                                             qt * 512:(qt + 1) * 512],
                                    start=True, stop=True)
                            return ii
                        score_tok[i] = em("tensor", fsc, waits=w, inc="st")
                        w = [score_tok[i]]
                        if i - 3 >= 0:
                            w.append(pv_tok[i - 3])
                        exp_tok[i] = em("scalar", lambda s, i=i: s.activation(
                            Pt[i % 3][:, :], ps_S[i % 3][:, :], AF.Exp,
                            bias=bneg[:, 0:1], scale=0.125),
                            waits=w, inc="sa")
                        if kt >= 4 * qt:
                            mi = kt - 4 * qt
                            def fmk(v, i=i, mi=mi):
                                v.tensor_tensor(Pt[i % 3][:, 0:512], Pt[i % 3][:, 0:512],
                                                mask_sb[:, mi * 512:(mi + 1) * 512],
                                                ALU.mult)
                                return v.tensor_tensor(
                                    Pt[i % 3][:, 512:1024], Pt[i % 3][:, 512:1024],
                                    mask_sb[:, mi * 512:(mi + 1) * 512], ALU.mult)
                            mask_tok[i] = em("vector", fmk,
                                             waits=[exp_tok[i], t_msk], inc="sv")

                    def emit_pv(j):
                        kt = j
                        i = base_i + j
                        w = [mask_tok[i] if i in mask_tok else exp_tok[i]]
                        if j == 0:
                            w.append(evac_toks[-1] if evac_toks else qkv_sa)
                        def fpv(t, pair=pair, kt=kt, i=i, first=(j == 0),
                                last=(j == nkt - 1)):
                            ii = None
                            for h in range(2):
                                hl = 2 * pair + h
                                ii = t.matmul(
                                    ps_O[h][0:65, :],
                                    Vtm[:, kt * VW + hl * (DK + 1):
                                        kt * VW + hl * (DK + 1) + 65],
                                    Pt[i % 3][:, h * 512:(h + 1) * 512],
                                    start=first, stop=last)
                            return ii
                        pv_tok[i] = em("tensor", fpv, waits=w, inc="st")

                    for j in range(nkt):
                        emit_sc(j)
                        if j >= 2:
                            emit_pv(j - 2)
                    emit_pv(nkt - 2)
                    emit_pv(nkt - 1)
                    i_glob = base_i + nkt

                    for h in range(2):
                        hl = 2 * pair + h
                        w = [pv_tok[i_glob - 1]]
                        if seq >= 1:
                            w.append(store_toks[2 * (seq - 1) + h])
                        tev = em("vector", lambda v, h=h: v.tensor_copy(
                            ostg[h][0:65, :], ps_O[h][0:65, :]), waits=w, inc="sv")
                        evac_toks.append(tev)
                        store_toks.append(dmas(
                            lambda s, h=h, hl=hl, qt=qt: s.dma_start(
                                out=xch_in.ap()[hl * 65:(hl + 1) * 65,
                                                qt * 512:(qt + 1) * 512],
                                in_=ostg[h][0:65, :]),
                            f"dso{h}", waits=[tev]))
                    seq += 1
                cc_toks.append(em(
                    "gpsimd",
                    lambda g, pair=pair: g.collective_compute(
                        "AllGather", ALU.bypass,
                        replica_groups=[list(range(8))],
                        ins=[xch_in.ap()[pair * 130:(pair + 1) * 130, :]],
                        outs=[xch_out[pair].ap()]),
                    waits=[("dso0", P.cnt["dso0"]), ("dso1", P.cnt["dso1"])],
                    inc="scc"))

        for s in ("st", "sv", "sa", "dg", "ds"):
            AE[s] = P.cnt[s]

    with contextlib.ExitStack() as esC:
        sbc = lambda n, sh, dt: esC.enter_context(nc.sbuf_tensor(n, list(sh), dt))
        z_sb = sbc("z_sb", [128, NOT * D], F32)
        xres_b = [sbc(f"xres{i}", [128, D], F32) for i in range(2)]
        yln = sbc("yln", [128, D], F32)
        st12c = sbc("st12c", [128, 6 * NST], F32)
        mvc = sbc("mvc", [128, 2], F32)
        nmc = sbc("nmc", [128, 1], F32)
        rsc = sbc("rsc", [128, 1], F32)
        zbf = sbc("zbf", [128, D], BF16)
        negmu4 = sbc("negmu4", [128, NOT], F32)
        lnv4 = sbc("lnv4", [128, NOT], F32)
        r4 = sbc("r4", [128, NOT], F32)
        rowtr2 = sbc("rowtr2", [128, 128], F32)
        nm4row = [sbc(f"nm4row{i}", [1, 512], BF16) for i in range(2)]
        ro_sb = sbc("ro_sb", [E, 512], F32)
        gtm = sbc("gtm", [128, 4 * E], F32)
        g8 = sbc("g8", [128, 8], F32)
        m8 = sbc("m8", [128, 8], F32)
        gmask = sbc("gmask", [128, 4 * E], F32)
        wtm = sbc("wtm", [128, 4 * E], F32)
        wfm = sbc("wfm", [E, 512], BF16)
        wr_sb = sbc("wr_sb", [LR, 512], F32)
        hs_sb = sbc("hs_sb", [LR, 512], BF16)
        ffn1 = sbc("ffn1", [128, D], F32)
        outb = [sbc(f"outb{i}", [128, D], F32) for i in range(2)]
        d1 = sbc("d1", [128, 1], F32)
        rk_sb = sbc("rk_sb", [1, 1], U32)

        AEw = [("st", AE["st"]), ("sv", AE["sv"]), ("sa", AE["sa"])]

        with contextlib.ExitStack() as esC1:
            sbc1 = lambda n, sh, dt: esC1.enter_context(nc.sbuf_tensor(n, list(sh), dt))
            Och = sbc1("Och", [128, FCH * OWN], BF16)
            sums_sb = [sbc1(f"sums_sb{p}", [8, OWN], BF16) for p in range(2)]
            recf = sbc1("recf", [8, 512], F32)
            esP = esC1.enter_context(contextlib.ExitStack())
            ps_y = [esP.enter_context(nc.psum_tensor(f"ps_y{i}", [128, D], F32))
                    for i in range(2)]
            ps_rc = [esP.enter_context(nc.psum_tensor(f"ps_rc{i}", [128, 512], F32))
                     for i in range(2)]

            roff = nc.alloc_register(mybir.EngineType.SP, "roff")
            rtmp = nc.alloc_register(mybir.EngineType.SP, "rtmp")
            t_rk = dmas(lambda s: s.dma_start(out=rk_sb[:, :], in_=rkoff.ap()), "dsv0")

            def mk_recv(p):
                def f_recv(s):
                    if p == 0:
                        s.reg_load(roff, rk_sb[0:1, 0:1])
                    for k in range(p, FCH, 2):
                        rowbase = 2 * (DK + 1) * (k // 2)
                        s.reg_mov(rtmp, rowbase * S)
                        s.reg_add(rtmp, rtmp, roff)
                        s.dma_start(
                            out=Och[:, k * OWN:(k + 1) * OWN],
                            in_=bass.AP(xch_out[p], rtmp,
                                        [[(DK + 1) * S, 2], [S, DK], [1, OWN]])
                        ).then_inc(SEMS[f"dsv{p}"], 16)
                    s.reg_mov(rtmp, DK * S)
                    s.reg_add(rtmp, rtmp, roff)
                    return s.dma_start(
                        out=sums_sb[p][:, :],
                        in_=bass.AP(xch_out[p], rtmp,
                                    [[2 * (DK + 1) * S, 4], [(DK + 1) * S, 2], [1, OWN]]))
                return f_recv
            t_recv = [None, None]
            for p in range(2):
                P.cnt[f"dsv{p}"] += 16 * (FCH // 2)
                t_recv[p] = em("sync", mk_recv(p),
                               waits=[("scc", p + 1), t_rk], inc=f"dsv{p}", incn=16)

            t_rcb = [None, None]
            for p in range(2):
                for _c in range(OWN // 512):
                    a_ = em("vector", lambda v, p=p, _c=_c: v.tensor_copy(
                        recf[:, :], sums_sb[p][:, _c * 512:(_c + 1) * 512]),
                        waits=[t_recv[p]] + AEw, inc="sv")
                    b_ = em("vector", lambda v: v.reciprocal(recf[:, :], recf[:, :]),
                            waits=[a_], inc="sv")
                    t_rcb[p] = em("vector", lambda v, p=p, _c=_c: v.tensor_copy(
                        sums_sb[p][:, _c * 512:(_c + 1) * 512], recf[:, :]),
                        waits=[b_], inc="sv")

            div_last = None
            div_tts = {}
            di = 0
            for p in range(2):
                for k in range(p, FCH, 2):
                    for nt in range(OWN // 512):
                        w = [t_rcb[p], ("sa", AE["sa"]), L["bsel"], div_tts.get(di - 2)]
                        tmm = em("tensor", lambda t, p=p, k=k, nt=nt, di=di: t.matmul(
                            ps_rc[di % 2][:, :], bsel_sb[0:8, k * 128:(k + 1) * 128],
                            sums_sb[p][:, nt * 512:(nt + 1) * 512],
                            start=True, stop=True),
                            waits=w, inc="st")
                        div_last = em("vector", lambda v, k=k, nt=nt, di=di: v.tensor_tensor(
                            Och[:, k * OWN + nt * 512:k * OWN + (nt + 1) * 512],
                            Och[:, k * OWN + nt * 512:k * OWN + (nt + 1) * 512],
                            ps_rc[di % 2][:, :], ALU.mult),
                            waits=[tmm], inc="sv")
                        div_tts[di] = div_last
                        di += 1

            z_toks = []
            prev_ln = None
            prev_ln2 = [None, None]
            prev_ta3 = None
            xr_war = [None, None]
            for ti in range(NOT):
                sl = ti % 2
                t_xr = dmas(lambda s, ti=ti, sl=sl: s.dma_start(
                    out=xres_b[sl][:, :], in_=x_res.ap()[ti * 128:(ti + 1) * 128, :]),
                    f"dsr{sl}", waits=[xr_war[sl]])
                w = [div_last, L["woT"], prev_ln2[ti % 2]]
                def fwo(t, ti=ti):
                    ii = None
                    for hh in range(NDH):
                        w0 = hh * 512
                        wd = min(512, D - w0)
                        for k in range(FCH):
                            ii = t.matmul(ps_y[ti % 2][:, w0:w0 + wd],
                                          Och[:, k * OWN + ti * 128:k * OWN + (ti + 1) * 128],
                                          woT_sb[:, k * D + w0:k * D + w0 + wd],
                                          start=(k == 0), stop=(k == FCH - 1))
                    return ii
                tmy = em("tensor", fwo, waits=w, inc="st")
                def fst3(v, ti=ti):
                    ii = None
                    for j in range(NST):
                        c0 = j * 512
                        cw = min(512, D - c0)
                        ii = v.bn_stats(st12c[:, 6 * j:6 * j + 6],
                                        ps_y[ti % 2][:, c0:c0 + cw])
                    return ii
                ts1 = em("vector", fst3, waits=[tmy], inc="sv")
                ts2 = em("vector", lambda v: v.bn_aggr(mvc[:, :], st12c[:, 0:6 * NST]),
                         waits=[ts1, prev_ta3], inc="sv")
                ts3 = em("vector", lambda v: v.tensor_scalar(
                    nmc[:, 0:1], mvc[:, 0:1], -1.0, None, ALU.mult), waits=[ts2], inc="sv")
                ta1 = em("scalar", lambda s: s.activation(
                    rsc[:, 0:1], mvc[:, 1:2], AF.Ln, bias=epst[:, 0:1]),
                    waits=[ts2, prev_ln], inc="sa")
                ta2 = em("scalar", lambda s: s.activation(
                    rsc[:, 0:1], rsc[:, 0:1], AF.Exp, scale=-0.5), waits=[ta1], inc="sa")
                ts4 = em("vector", lambda v, ti=ti: v.tensor_scalar(
                    yln[:, :], ps_y[ti % 2][:, :], nmc[:, 0:1], rsc[:, 0:1],
                    ALU.add, ALU.mult),
                    waits=[ts3, ta2], inc="sv")
                prev_ln = ts4
                prev_ln2[ti % 2] = ts4
                prev_ta3 = ta2
                tz = em("vector", lambda v, ti=ti, sl=sl: v.tensor_tensor(
                    z_sb[:, ti * D:(ti + 1) * D], yln[:, :], xres_b[sl][:, :], ALU.add),
                    waits=[ts4, t_xr], inc="sv")
                xr_war[sl] = tz
                z_toks.append(tz)

        zfm = sbc("zfm", [128, DCH * OWN], BF16)
        wo_done_st = ("st", P.cnt["st"])
        with contextlib.ExitStack() as esP:
            ps_zt = [esP.enter_context(nc.psum_tensor(f"ps_zt{i}", [128, 128], BF16))
                     for i in range(2)]
            ps_tr2 = esP.enter_context(nc.psum_tensor("ps_tr2", [128, 128], F32))
            ln4_nm, ln4_r = [], []
            prev_tr = None
            ztev = {}
            nzt = 0
            for ti in range(NOT):
                tc = em("vector", lambda v, ti=ti: v.tensor_copy(
                    zbf[:, :], z_sb[:, ti * D:(ti + 1) * D]),
                    waits=[z_toks[ti], prev_tr, wo_done_st], inc="sv")
                for k in range(DCH):
                    tt = em("tensor", lambda t, k=k, nzt=nzt: t.transpose(
                        ps_zt[nzt % 2][:, :], zbf[:, k * 128:(k + 1) * 128], idenb[:, :]),
                        waits=[tc, L["idb"], ztev.get(nzt - 2)], inc="st")
                    prev_tr = em("vector", lambda v, k=k, ti=ti, nzt=nzt: v.tensor_copy(
                        zfm[:, k * OWN + ti * 128:k * OWN + (ti + 1) * 128],
                        ps_zt[nzt % 2][:, :]),
                        waits=[tt], inc="sv")
                    ztev[nzt] = prev_tr
                    nzt += 1
                def fs4(v, ti=ti):
                    ii = None
                    for j in range(NST):
                        c0 = j * 512
                        cw = min(512, D - c0)
                        ii = v.bn_stats(st12c[:, 6 * j:6 * j + 6],
                                        z_sb[:, ti * D + c0:ti * D + c0 + cw])
                    return ii
                t41 = em("vector", fs4, waits=[z_toks[ti]], inc="sv")
                t42 = em("vector", lambda v: v.bn_aggr(mvc[:, :], st12c[:, 0:6 * NST]),
                         waits=[t41, prev_ta3, ln4_r[-1] if ln4_r else None], inc="sv")
                t43 = em("vector", lambda v, ti=ti: v.tensor_scalar(
                    negmu4[:, ti:ti + 1], mvc[:, 0:1], -1.0, None, ALU.mult),
                    waits=[t42], inc="sv")
                t44 = em("scalar", lambda s, ti=ti: s.activation(
                    lnv4[:, ti:ti + 1], mvc[:, 1:2], AF.Ln, bias=epst[:, 0:1]),
                    waits=[t42], inc="sa")
                t45 = em("scalar", lambda s, ti=ti: s.activation(
                    r4[:, ti:ti + 1], lnv4[:, ti:ti + 1], AF.Exp, scale=-0.5),
                    waits=[t44], inc="sa")
                ln4_nm.append(t43)
                ln4_r.append(t45)
            ztr_sv = ("sv", P.cnt["sv"])

            assert NOT <= 128
            tz1 = em("tensor", lambda t: t.transpose(
                ps_tr2[0:NOT, 0:128], negmu4[:, 0:NOT], idenf[:, :]),
                waits=ln4_nm + [prev_tr], inc="st")
            cz1 = em("vector", lambda v: v.tensor_copy(rowtr2[0:NOT, 0:128],
                                                       ps_tr2[0:NOT, 0:128]),
                     waits=[tz1], inc="sv")
            sz1 = dmas(lambda s: s.dma_start(
                out=zstats_d.ap()[0:1, :].rearrange("o (a b) -> (o a) b", b=128),
                in_=rowtr2[0:NOT, 0:128]), "dst", waits=[cz1])
            tz2 = em("tensor", lambda t: t.transpose(
                ps_tr2[0:NOT, 0:128], r4[:, 0:NOT], idenf[:, :]),
                waits=ln4_r + [cz1], inc="st")
            cz2 = em("vector", lambda v: v.tensor_copy(rowtr2[0:NOT, 0:128],
                                                       ps_tr2[0:NOT, 0:128]),
                     waits=[tz2, sz1], inc="sv")
            sz2 = dmas(lambda s: s.dma_start(
                out=zstats_d.ap()[1:2, :].rearrange("o (a b) -> (o a) b", b=128),
                in_=rowtr2[0:NOT, 0:128]), "dst", waits=[cz2])
            c_mid_st = ("st", P.cnt["st"])

        with contextlib.ExitStack() as esP:
            ps_ffn = esP.enter_context(nc.psum_tensor("ps_ffn", [128, D], F32))
            ps_lr = [esP.enter_context(nc.psum_tensor(f"ps_lr{i}", [LR, 512], F32))
                     for i in range(2)]
            ps_ro = esP.enter_context(nc.psum_tensor("ps_ro", [E, 512], F32))
            ps_g = esP.enter_context(nc.psum_tensor("ps_g", [128, 4 * E], F32))
            ps_wt = esP.enter_context(nc.psum_tensor("ps_wt", [E, 512], F32))
            ps_wb = esP.enter_context(nc.psum_tensor("ps_wb", [LR, 512], F32))

            prev_rows = None
            prev_hs = None
            prev_hs2 = [None, None]
            prev_ffn = None
            out_final = None
            prev_ua2 = [None]
            prev_u4 = [None]
            for g4 in range(cfg.NFG):
                sl = g4 % 2
                l1 = dmag(lambda g, g4=g4, sl=sl: g.dma_start(
                    out=nm4row[sl][:, :],
                    in_=zstats_d.ap()[0:1, g4 * 512:(g4 + 1) * 512]),
                    f"dgz{sl}", waits=[sz1, sz2, prev_rows])

                w = [ztr_sv, l1, L["lrAT"], e_w1lr, c_mid_st, prev_hs2[g4 % 2]]
                def flr(t, g4=g4, sl=sl):
                    for k in range(DCH):
                        t.matmul(ps_lr[g4 % 2][0:LR, :], lrAT_sb[:, k * LR:(k + 1) * LR],
                                 zfm[:, k * OWN + g4 * 512:k * OWN + (g4 + 1) * 512],
                                 start=(k == 0), stop=False)
                    return t.matmul(ps_lr[g4 % 2][0:LR, :], w1lr[0:1, :],
                                    nm4row[sl][0:1, :], start=False, stop=True)
                t_lr = em("tensor", flr, waits=w, inc="st")

                wro = [L["routerWT"], e_w1ro]
                def fro(t, g4=g4, sl=sl):
                    for k in range(DCH):
                        t.matmul(ps_ro[0:E, :], routerWT_sb[:, k * E:(k + 1) * E],
                                 zfm[:, k * OWN + g4 * 512:k * OWN + (g4 + 1) * 512],
                                 start=(k == 0), stop=False)
                    return t.matmul(ps_ro[0:E, :], w1ro[0:1, :], nm4row[sl][0:1, :],
                                    start=False, stop=True)
                t_ro = em("tensor", fro, waits=wro, inc="st")
                t_roe = em("vector", lambda v: v.tensor_copy(ro_sb[0:E, :], ps_ro[0:E, :]),
                           waits=[t_ro], inc="sv")

                tr_toks = []
                for j in range(4):
                    tt = em("tensor", lambda t, j=j: t.transpose(
                        ps_g[:, j * E:(j + 1) * E], ro_sb[0:E, j * 128:(j + 1) * 128],
                        idenf[0:E, 0:E]),
                        waits=[t_roe], inc="st")
                    tr_toks.append(tt)
                tg0 = em("vector", lambda v: v.tensor_copy(gtm[:, :], ps_g[:, :]),
                         waits=tr_toks, inc="sv")
                gk = tg0
                for j in range(4):
                    ti4 = 4 * g4 + j
                    gk = em("vector", lambda v, j=j, ti4=ti4: v.tensor_scalar(
                        gtm[:, j * E:(j + 1) * E], gtm[:, j * E:(j + 1) * E],
                        r4[:, ti4:ti4 + 1], None, ALU.mult),
                        waits=[gk, ln4_r[ti4]], inc="sv")
                gk = em("vector", lambda v: v.tensor_add(gtm[:, :], gtm[:, :], rb6_sb[:, :]),
                        waits=[gk, e_rb6], inc="sv")
                for j in range(4):
                    t1_ = em("vector", lambda v, j=j: v.tensor_reduce(
                        d1[:, 0:1], gtm[:, j * E:(j + 1) * E], AXX, ALU.max, negate=True),
                        waits=[gk], inc="sv")
                    t2_ = em("scalar", lambda s, j=j: s.activation(
                        gtm[:, j * E:(j + 1) * E], gtm[:, j * E:(j + 1) * E], AF.Exp,
                        bias=d1[:, 0:1]),
                        waits=[t1_], inc="sa")
                    t3_ = em("vector", lambda v, j=j: v.tensor_reduce(
                        d1[:, 0:1], gtm[:, j * E:(j + 1) * E], AXX, ALU.add),
                        waits=[t2_], inc="sv")
                    t4_ = em("vector", lambda v: v.reciprocal(d1[:, 0:1], d1[:, 0:1]),
                             waits=[t3_], inc="sv")
                    t5_ = em("vector", lambda v, j=j: v.tensor_scalar(
                        gtm[:, j * E:(j + 1) * E], gtm[:, j * E:(j + 1) * E],
                        d1[:, 0:1], None, ALU.mult),
                        waits=[t4_], inc="sv")
                    t6_ = em("vector", lambda v: v.memset(g8[:, :], -1e30),
                             waits=[t5_], inc="sv")
                    t7_ = em("vector", lambda v, j=j: v.tensor_copy(
                        g8[:, 0:E], gtm[:, j * E:(j + 1) * E]), waits=[t6_], inc="sv")
                    t8_ = em("vector", lambda v: v.max(m8[:, :], g8[:, :]),
                             waits=[t7_], inc="sv")
                    t9_ = em("vector", lambda v, j=j: v.tensor_scalar(
                        gmask[:, j * E:(j + 1) * E], gtm[:, j * E:(j + 1) * E],
                        m8[:, 1:2], None, ALU.is_ge),
                        waits=[t8_], inc="sv")
                    t10_ = em("vector", lambda v, j=j: v.tensor_mul(
                        wtm[:, j * E:(j + 1) * E], gtm[:, j * E:(j + 1) * E],
                        gmask[:, j * E:(j + 1) * E]),
                        waits=[t9_], inc="sv")
                    t11_ = em("vector", lambda v, j=j: v.tensor_reduce(
                        d1[:, 0:1], wtm[:, j * E:(j + 1) * E], AXX, ALU.add),
                        waits=[t10_], inc="sv")
                    t12_ = em("vector", lambda v: v.tensor_scalar(
                        d1[:, 0:1], d1[:, 0:1], 1e-6, None, ALU.add),
                        waits=[t11_], inc="sv")
                    t13_ = em("vector", lambda v: v.reciprocal(d1[:, 0:1], d1[:, 0:1]),
                              waits=[t12_], inc="sv")
                    gk = em("vector", lambda v, j=j: v.tensor_scalar(
                        wtm[:, j * E:(j + 1) * E], wtm[:, j * E:(j + 1) * E],
                        d1[:, 0:1], None, ALU.mult),
                        waits=[t13_], inc="sv")
                wtr = []
                for j in range(4):
                    tt = em("tensor", lambda t, j=j: t.transpose(
                        ps_wt[0:E, j * 128:(j + 1) * 128], wtm[:, j * E:(j + 1) * E],
                        idenf[:, :]),
                        waits=[gk], inc="st")
                    wtr.append(tt)
                twf = em("vector", lambda v: v.tensor_copy(wfm[0:E, :], ps_wt[0:E, :]),
                         waits=wtr, inc="sv")
                def fwb(t):
                    t.matmul(ps_wb[0:LR, :], g6_sb[0:E, :], wfm[0:E, :],
                             start=True, stop=False)
                    return t.matmul(ps_wb[0:LR, :], shsel_sb[0:1, :], onesb[0:1, :],
                                    start=False, stop=True)
                t_wb = em("tensor", fwb, waits=[twf, L["g6"], L["sh"], t_ones], inc="st")
                t_wre = em("vector", lambda v: v.tensor_copy(wr_sb[0:LR, :], ps_wb[0:LR, :]),
                           waits=[t_wb], inc="sv")
                t_hs = em("vector", lambda v, g4=g4: v.tensor_tensor(
                    hs_sb[0:LR, :], ps_lr[g4 % 2][0:LR, :], wr_sb[0:LR, :], ALU.mult),
                    waits=[t_wre, t_lr], inc="sv")
                prev_hs = t_hs
                prev_hs2[g4 % 2] = t_hs

                for tsl in range(4):
                    ti = 4 * g4 + tsl
                    w = [t_hs, L["baseWT"], L["Bstack"], e_w1b[-1], l1]
                    if prev_ffn is not None:
                        w.append(prev_ffn)
                    def fbase(t, g4=g4, tsl=tsl, sl=sl):
                        ii = None
                        for hh in range(NDH):
                            w0 = hh * 512
                            wd = min(512, D - w0)
                            for k in range(DCH):
                                t.matmul(ps_ffn[:, w0:w0 + wd],
                                         zfm[:, k * OWN + g4 * 512 + tsl * 128:
                                             k * OWN + g4 * 512 + (tsl + 1) * 128],
                                         baseWT_sb[:, k * D + w0:k * D + w0 + wd],
                                         start=(k == 0), stop=False)
                            t.matmul(ps_ffn[:, w0:w0 + wd],
                                     nm4row[sl][0:1, tsl * 128:(tsl + 1) * 128],
                                     w1base[0:1, w0:w0 + wd], start=False, stop=False)
                            ii = t.matmul(ps_ffn[:, w0:w0 + wd],
                                          hs_sb[0:LR, tsl * 128:(tsl + 1) * 128],
                                          BstackT_sb[0:LR, w0:w0 + wd],
                                          start=False, stop=True)
                        return ii
                    t_f = em("tensor", fbase, waits=w, inc="st")
                    tf1 = em("vector", lambda v, ti=ti: v.scalar_tensor_tensor(
                        ffn1[:, :], ps_ffn[:, :], r4[:, ti:ti + 1], bbc_sb[:, :],
                        ALU.mult, ALU.add),
                        waits=[t_f, e_bbc, ln4_r[ti]], inc="sv")
                    prev_ffn = tf1
                    def fs5(v):
                        ii = None
                        for j in range(NST):
                            c0 = j * 512
                            cw = min(512, D - c0)
                            ii = v.bn_stats(st12c[:, 6 * j:6 * j + 6], ffn1[:, c0:c0 + cw])
                        return ii
                    u1 = em("vector", fs5, waits=[tf1], inc="sv")
                    u2 = em("vector", lambda v: v.bn_aggr(mvc[:, :], st12c[:, 0:6 * NST]),
                            waits=[u1, prev_ua2[0]], inc="sv")
                    u3 = em("vector", lambda v: v.tensor_scalar(
                        nmc[:, 0:1], mvc[:, 0:1], -1.0, None, ALU.mult),
                        waits=[u2], inc="sv")
                    ua1 = em("scalar", lambda s: s.activation(
                        rsc[:, 0:1], mvc[:, 1:2], AF.Ln, bias=epst[:, 0:1]),
                        waits=[u2, prev_u4[0]], inc="sa")
                    ua2 = em("scalar", lambda s: s.activation(
                        rsc[:, 0:1], rsc[:, 0:1], AF.Exp, scale=-0.5),
                        waits=[ua1], inc="sa")
                    u4 = em("vector", lambda v: v.tensor_scalar(
                        yln[:, :], ffn1[:, :], nmc[:, 0:1], rsc[:, 0:1],
                        ALU.add, ALU.mult),
                        waits=[u3, ua2], inc="sv")
                    prev_ua2[0] = ua2
                    prev_u4[0] = u4
                    u5 = em("vector", lambda v, ti=ti, tsl=tsl: v.tensor_tensor(
                        outb[tsl % 2][:, :], yln[:, :], z_sb[:, ti * D:(ti + 1) * D],
                        ALU.add),
                        waits=[u4, ("dou%d" % (tsl % 2), P.cnt["dou%d" % (tsl % 2)])],
                        inc="sv")
                    out_final = dmas(lambda s, ti=ti, tsl=tsl: s.dma_start(
                        out=out_t.ap()[ti * 128:(ti + 1) * 128, :],
                        in_=outb[tsl % 2][:, :]), "dou%d" % (tsl % 2), waits=[u5])
                prev_rows = prev_ffn

            wt("sync", [(s, P.cnt[s]) for s in list(P.cnt.keys())
                        if s.startswith("d") or s == "scc"])

    sems = {name: es.enter_context(nc.semaphore(name)) for name in P.cnt}
    SEMS.update(sems)
    nc._keepalive_es = es
    return nc, sems, P


def emit(nc, sems, P):
    with nc.Block() as blk:
        def mk(name):
            def run(eng):
                seen = {}
                for ws, fn, tok, incn in P.ops[name]:
                    for s, v in ws:
                        if v > seen.get(s, -1):
                            eng.wait_ge(sems[s], v)
                            seen[s] = v
                    if fn is None:
                        continue
                    inst = fn(eng)
                    if tok is not None:
                        inst.then_inc(sems[tok[0]], incn)
            return run
        blk.tensor(mk("tensor"))
        blk.vector(mk("vector"))
        blk.scalar(mk("scalar"))
        blk.gpsimd(mk("gpsimd"))
        blk.sync(mk("sync"))
    return nc



def host_consts(cfg: Cfg):
    S, DK = cfg.S, cfg.DK
    rope_dim = DK // 2
    inv = 1.0 / ROPE_BASE ** (np.arange(0, rope_dim, 2, dtype=np.float32) / rope_dim)
    t = np.arange(S, dtype=np.float32)
    fr = np.outer(t, inv)
    emb = np.concatenate([fr, fr], axis=1)
    cos = np.cos(emb).astype(np.float32)
    sin = np.sin(emb).astype(np.float32)
    C = np.ones((128, S), np.float32)
    Sg = np.zeros((128, S), np.float32)
    for blk in (0, 64):
        C[blk:blk + 32] = cos.T
        Sg[blk:blk + 16] = -sin.T[:16]
        Sg[blk + 16:blk + 32] = sin.T[16:32]
    masks = np.zeros((4 * 128, 512), np.float32)
    p = np.arange(128)[:, None]
    c = np.arange(512)[None, :]
    for i in range(4):
        masks[i * 128:(i + 1) * 128] = (c >= 128 * i + p)
    bsel = np.zeros((16, 8 * 128), np.float32)
    for k in range(8):
        for a in range(2):
            row = 2 * (k // 2) + a
            bsel[row, k * 128 + a * 64:k * 128 + (a + 1) * 64] = 1.0
    g6 = np.zeros((cfg.E, cfg.LR), np.float32)
    for e in range(cfg.E):
        g6[e, cfg.R + cfg.R * e:cfg.R + cfg.R * (e + 1)] = 1.0
    sh = np.zeros((1, cfg.LR), np.float32)
    sh[0, :cfg.R] = 1.0
    return {
        "ropeC": C.astype(BF), "ropeS": Sg.astype(BF),
        "masks": masks.astype(BF),
        "iden_bf": np.eye(128, dtype=np.float32).astype(BF),
        "iden_f": np.eye(128, dtype=np.float32),
        "bsel": bsel.astype(BF), "g6sel": g6.astype(BF), "shsel": sh.astype(BF),
    }


def core_inputs(cfg: Cfg, A, c, consts):
    b, hg = c // 4, c % 4
    x = np.asarray(A["L0_attn_x"][b])
    Wqkv = np.asarray(A["L0_attn_W_qkv"])
    hs = [4 * hg + i for i in range(cfg.HPC)]
    qp = []
    for pr in range(2):
        h0, h1 = hs[2 * pr], hs[2 * pr + 1]
        qp.append(np.concatenate([Wqkv[0, h0].T, Wqkv[0, h1].T], axis=1))
        qp.append(np.concatenate([Wqkv[1, h0].T, Wqkv[1, h1].T], axis=1))
    wqkT = np.concatenate([qp[0], qp[1], qp[2], qp[3]], axis=1)
    wvT = np.concatenate([Wqkv[2, h].T for h in hs], axis=1)
    d = {
        "x_fm": np.ascontiguousarray(x.T),
        "x_tm": np.ascontiguousarray(x),
        "x_res": np.ascontiguousarray(x[hg * cfg.OWN:(hg + 1) * cfg.OWN]),
        "rkoff": np.array([[b * 4 * 2 * (cfg.DK + 1) * cfg.S + hg * cfg.OWN]],
                          np.uint32),
        "wqkT": np.ascontiguousarray(wqkT),
        "wvT": np.ascontiguousarray(wvT),
        "woT": np.ascontiguousarray(np.asarray(A["L0_attn_W_o"]).T),
        "baseWT": np.ascontiguousarray(np.asarray(A["L0_ffn_base_W"]).T),
        "baseb": np.asarray(A["L0_ffn_base_b"]).reshape(1, -1),
        "lrAT": np.ascontiguousarray(np.concatenate(
            [np.asarray(A["L0_ffn_shared_A"]).T] +
            [np.asarray(A["L0_ffn_routed_A"][e]).T for e in range(cfg.E)], axis=1)),
        "routerWT": np.ascontiguousarray(np.asarray(A["L0_ffn_router_W"]).T),
        "routerb": np.asarray(A["L0_ffn_router_b"]).reshape(1, -1),
        "BstackT": np.ascontiguousarray(np.concatenate(
            [np.asarray(A["L0_ffn_shared_B"]).T] +
            [np.asarray(A["L0_ffn_routed_B"][e]).T for e in range(cfg.E)], axis=0)),
    }
    d.update(consts)
    return d


_BUILD_CACHE = {}


def kernel(**inputs):
    cfg = CFG
    key = cfg.S
    if key not in _BUILD_CACHE:
        nc, sems, P = build(cfg)
        emit(nc, sems, P)
        _BUILD_CACHE[key] = nc
    nc = _BUILD_CACHE[key]
    consts = host_consts(cfg)
    in_maps = [core_inputs(cfg, inputs, c, consts) for c in range(8)]
    res = run_bass_kernel_spmd(nc, in_maps, core_ids=list(range(8)))
    out = np.empty((2, cfg.S, cfg.D), np.float32)
    for c in range(8):
        b, hg = c // 4, c % 4
        out[b, hg * cfg.OWN:(hg + 1) * cfg.OWN] = res.results[c]["out"]
    return out


# version 09
# speedup vs baseline: 1.0659x; 1.0255x over previous
import math
from dataclasses import dataclass

import numpy as np
import ml_dtypes

import concourse.bass as bass
import concourse.mybir as mybir
from concourse.bass_utils import run_bass_kernel_spmd

F32 = mybir.dt.float32
BF16 = mybir.dt.bfloat16
U32 = mybir.dt.uint32
AF = mybir.ActivationFunctionType
ALU = mybir.AluOpType
AXX = mybir.AxisListType.X
BF = ml_dtypes.bfloat16

ROPE_BASE = 6000.0
EPS = 1e-5


@dataclass
class Cfg:
    S: int = 8192
    D: int = 1024
    F: int = 1024
    NH: int = 16
    HPC: int = 4
    DK: int = 64
    E: int = 6
    R: int = 16

    @property
    def DCH(self): return self.D // 128
    @property
    def FCH(self): return self.F // 128
    @property
    def NG(self): return self.S // 512
    @property
    def NQT(self): return self.S // 512
    @property
    def NKT(self): return self.S // 128
    @property
    def OWN(self): return self.S // 4
    @property
    def NOT(self): return self.OWN // 128
    @property
    def NFG(self): return self.OWN // 512
    @property
    def NST(self): return -(-self.D // 512)
    @property
    def LR(self): return self.R * (1 + self.E)
    @property
    def VW(self): return self.HPC * (self.DK + 1)
    @property
    def NTS(self): return self.S // 128


CFG = Cfg()
ENGS = ("tensor", "vector", "scalar", "gpsimd", "sync")
SELF_SEM = {"vector": "sv", "scalar": "sa"}


class Plan:
    def __init__(self):
        import collections
        self.ops = {e: [] for e in ENGS}
        self.cnt = collections.defaultdict(int)

    def em(self, eng, fn, waits=(), inc=None, incn=1):
        ws = [w for w in waits if w is not None]
        if eng in SELF_SEM:
            ws.append((SELF_SEM[eng], self.cnt[SELF_SEM[eng]]))
        tok = None
        if inc is not None:
            self.cnt[inc] += incn
            tok = (inc, self.cnt[inc])
        self.ops[eng].append((ws, fn, tok, incn))
        return tok

    def wait(self, eng, waits):
        self.ops[eng].append(([w for w in waits if w is not None], None, None, 0))


def build(cfg: Cfg):
    SEMS = {}
    nc = bass.Bass("TRN2", target_bir_lowering=False)
    S, D, Fd = cfg.S, cfg.D, cfg.F
    DCH, FCH, NST = cfg.DCH, cfg.FCH, cfg.NST
    NG, NQT, NKT, OWN, NOT = cfg.NG, cfg.NQT, cfg.NKT, cfg.OWN, cfg.NOT
    HPC, DK, LR, E, R, VW, NTS = cfg.HPC, cfg.DK, cfg.LR, cfg.E, cfg.R, cfg.VW, cfg.NTS
    NDH = -(-D // 512)

    din = lambda n, sh, dt=F32: nc.dram_tensor(n, list(sh), dt, kind="ExternalInput")
    x_fm = din("x_fm", [D, S]);           x_tm = din("x_tm", [S, D])
    x_res = din("x_res", [OWN, D]);       rkoff = din("rkoff", [1, 1], U32)
    wqkT = din("wqkT", [D, 512]);         wvT = din("wvT", [D, HPC * DK])
    woT = din("woT", [Fd, D]);            baseWT = din("baseWT", [D, D])
    baseb = din("baseb", [1, D]);         lrAT = din("lrAT", [D, LR])
    routerWT = din("routerWT", [D, E]);   routerb = din("routerb", [1, E])
    BstackT = din("BstackT", [LR, D])
    ropeC = din("ropeC", [128, S], BF16); ropeS = din("ropeS", [128, S], BF16)
    masksI = din("masks", [512, 512], BF16)
    iden_bf = din("iden_bf", [128, 128], BF16)
    iden_f = din("iden_f", [128, 128], F32)
    bselI = din("bsel", [16, 8 * 128], BF16)
    g6I = din("g6sel", [E, LR], BF16);    shI = din("shsel", [1, LR], BF16)
    out_t = nc.dram_tensor("out", [OWN, D], F32, kind="ExternalOutput")

    stats_d = nc.dram_tensor("stats_d", [2, S], F32)
    zstats_d = nc.dram_tensor("zstats_d", [1, OWN], BF16)
    xch_in = nc.dram_tensor("xch_in", [HPC * (DK + 1), S], BF16)
    xch_out = [nc.dram_tensor(f"xch_out{p}", [8 * 2 * (DK + 1), S], BF16)
               for p in range(2)]

    P = Plan()
    em, wt = P.em, P.wait

    import contextlib
    es = contextlib.ExitStack()
    sb = lambda n, sh, dt: es.enter_context(nc.sbuf_tensor(n, list(sh), dt))

    onesb = sb("onesb", [1, 512], BF16)
    onescol = sb("onescol", [128, 1], BF16)
    epst = sb("epst", [128, 1], F32)
    idenb = sb("idenb", [128, 128], BF16)
    idenf = sb("idenf", [128, 128], F32)
    bneg = sb("bneg", [128, 1], F32)
    r_cols = sb("r_cols", [128, NTS], F32)
    woT_sb = sb("woT_sb", [128, FCH * D], BF16)
    baseWT_sb = sb("baseWT_sb", [128, DCH * D], BF16)
    lrAT_sb = sb("lrAT_sb", [128, DCH * LR], BF16)
    routerWT_sb = sb("routerWT_sb", [128, DCH * E], BF16)
    BstackT_sb = sb("BstackT_sb", [LR, D], BF16)
    bsel_sb = sb("bsel_sb", [16, 8 * 128], BF16)
    g6_sb = sb("g6_sb", [E, LR], BF16)
    shsel_sb = sb("shsel_sb", [1, LR], BF16)
    rb6_sb = sb("rb6_sb", [128, 4 * E], F32)
    bbc_sb = sb("bbc_sb", [128, D], BF16)
    w1base = sb("w1base", [1, D], BF16)
    w1lr = sb("w1lr", [1, LR], BF16)
    w1ro = sb("w1ro", [1, E], BF16)
    bscb = sb("bscb", [1, 1], BF16)

    dmag = lambda fn, ch, waits=(): em("gpsimd", fn, waits=waits, inc=ch, incn=16)
    dmas = lambda fn, ch, waits=(): em("sync", fn, waits=waits, inc=ch, incn=16)

    def chunked_load(dst, src_t, nchunk, width, ch, q="g"):
        f = lambda g: g.dma_start(
            out=dst[:, :].rearrange("p (c n) -> p c n", c=nchunk),
            in_=src_t.ap().rearrange("(c p) n -> p c n", p=128))
        return (dmag if q == "g" else dmas)(f, ch)

    L = {}
    L["woT"] = chunked_load(woT_sb, woT, FCH, D, "dgw")
    L["baseWT"] = chunked_load(baseWT_sb, baseWT, DCH, D, "dgw")
    L["lrAT"] = chunked_load(lrAT_sb, lrAT, DCH, LR, "dgw")
    L["routerWT"] = chunked_load(routerWT_sb, routerWT, DCH, E, "dgw")
    L["Bstack"] = dmag(lambda g: g.dma_start(out=BstackT_sb[:, :], in_=BstackT.ap()), "dgw")
    L["bbrow"] = dmag(lambda g: g.dma_start(out=bbrow[:, :], in_=baseb.ap()), "dgw")
    L["rbrow"] = dmag(lambda g: g.dma_start(out=rbrow[:, :], in_=routerb.ap()), "dgw")
    L["bsel"] = dmas(lambda s: s.dma_start(out=bsel_sb[:, :], in_=bselI.ap()), "dsw")
    L["g6"] = dmas(lambda s: s.dma_start(out=g6_sb[:, :], in_=g6I.ap()), "dsw")
    L["sh"] = dmas(lambda s: s.dma_start(out=shsel_sb[:, :], in_=shI.ap()), "dsw")
    L["idb"] = dmas(lambda s: s.dma_start(out=idenb[:, :], in_=iden_bf.ap()), "dsw")
    L["idf"] = dmas(lambda s: s.dma_start(out=idenf[:, :], in_=iden_f.ap()), "dsw")

    t_ones = em("vector", lambda v: v.memset(onesb[:, :], 1.0), inc="sv")
    t_onesc = em("vector", lambda v: v.memset(onescol[:, :], 1.0), inc="sv")
    t_eps = em("vector", lambda v: v.memset(epst[:, :], EPS), inc="sv")

    AE = {}

    with contextlib.ExitStack() as esA:
        sba = lambda n, sh, dt: esA.enter_context(nc.sbuf_tensor(n, list(sh), dt))
        rbrow = sba("rbrow", [1, E], BF16)
        bbrow = sba("bbrow", [1, D], BF16)
        Qp = [sba(f"Qp{p}", [128, S], BF16) for p in range(2)]
        Kp = [sba(f"Kp{p}", [128, S], BF16) for p in range(2)]
        Vtm = sba("Vtm", [128, NTS * VW], BF16)
        wqk_sb = sba("wqk_sb", [128, DCH * 512], BF16)
        wv_sb = sba("wv_sb", [128, DCH * HPC * DK], BF16)
        xbuf = [sba(f"xbuf{i}", [128, DCH * 512], BF16) for i in range(2)]
        ropeCb = [sba(f"ropeCb{i}", [128, 512], BF16) for i in range(2)]
        ropeSb = [sba(f"ropeSb{i}", [128, 512], BF16) for i in range(2)]
        nmrow = [sba(f"nmrow{i}", [1, 512], BF16) for i in range(2)]
        rrow = [sba(f"rrow{i}", [1, 512], BF16) for i in range(2)]
        rb_sb = sba("rb_sb", [128, 512], F32)
        ropet = [sba(f"ropet{i}", [128, 512], BF16) for i in range(2)]
        w1qk = sba("w1qk", [1, 512], BF16)
        w1v = sba("w1v", [1, HPC * DK], BF16)
        Pt = [sba(f"Pt{i}", [128, 1024], BF16) for i in range(3)]
        ostg = [sba(f"ostg{i}", [65, 512], BF16) for i in range(2)]
        mask_sb = sba("mask_sb", [128, 4 * 512], BF16)
        qsq = sba("qsq", [128, 512], BF16)
        bcol = sba("bcol", [1, 4 * NG], F32)
        bsc = sba("bsc", [1, 1], F32)

        t_msk = dmas(lambda s: s.dma_start(
            out=mask_sb[:, :].rearrange("p (i n) -> p i n", i=4),
            in_=masksI.ap().rearrange("(i p) n -> p i n", p=128)), "dsw")
        t_wqk = chunked_load(wqk_sb, wqkT, DCH, 512, "dgw")
        t_wv = chunked_load(wv_sb, wvT, DCH, HPC * DK, "dgw")
        _gw = ("dgw", P.cnt["dgw"]); _sw = ("dsw", P.cnt["dsw"])
        for _k in list(L):
            L[_k] = _gw if L[_k][0] == "dgw" else _sw
        t_msk, t_wqk, t_wv = _sw, _gw, _gw

        with contextlib.ExitStack() as esP:
            ps_w1 = esP.enter_context(nc.psum_tensor("ps_w1", [1, 512], F32))
            ps_bb = esP.enter_context(nc.psum_tensor("ps_bb", [128, 512], F32))

            _w1prev = [None]
            def w1row(dst, wsb, width, stride, ldtok, w0=0):
                def fn(t):
                    ii = None
                    for k in range(DCH):
                        ii = t.matmul(ps_w1[0:1, 0:width], onescol[:, 0:1],
                                      wsb[:, k * stride + w0:k * stride + w0 + width],
                                      start=(k == 0), stop=(k == DCH - 1))
                    return ii
                tok = em("tensor", fn, waits=[ldtok, t_onesc, _w1prev[0]], inc="st")
                ev = em("vector", lambda v: v.tensor_copy(dst, ps_w1[0:1, 0:width]),
                        waits=[tok], inc="sv")
                _w1prev[0] = ev
                return ev

            e_w1qk = w1row(w1qk[0:1, :], wqk_sb, 512, 512, t_wqk)
            e_w1v = w1row(w1v[0:1, :], wv_sb, HPC * DK, HPC * DK, t_wv)
            e_w1b = []
            for h in range(NDH):
                w0 = h * 512
                wd = min(512, D - w0)
                e_w1b.append(w1row(w1base[0:1, w0:w0 + wd], baseWT_sb, wd, D,
                                   L["baseWT"], w0=w0))
            e_w1lr = w1row(w1lr[0:1, :], lrAT_sb, LR, LR, L["lrAT"])
            e_w1ro = w1row(w1ro[0:1, :], routerWT_sb, E, E, L["routerWT"])

            t1 = em("tensor", lambda t: t.matmul(ps_bb[:, 0:E], onesb[0:1, 0:128],
                                                 rbrow[0:1, :], start=True, stop=True),
                    waits=[L["rbrow"], t_ones, e_w1ro], inc="st")
            e_rb6 = em("vector", lambda v: v.tensor_copy(
                rb6_sb[:, :].rearrange("p (j e) -> p j e", j=4),
                ps_bb[:, 0:E].unsqueeze(1).broadcast_to([128, 4, E])),
                waits=[t1], inc="sv")
            e_bbc = None
            for h in range(NDH):
                w0 = h * 512
                wd = min(512, D - w0)
                t2 = em("tensor", lambda t, w0=w0, wd=wd: t.matmul(
                    ps_bb[:, 0:wd], onesb[0:1, 0:128], bbrow[0:1, w0:w0 + wd],
                    start=True, stop=True),
                    waits=[L["bbrow"], t_ones, e_rb6 if h == 0 else e_bbc], inc="st")
                e_bbc = em("vector", lambda v, w0=w0, wd=wd: v.tensor_copy(
                    bbc_sb[:, w0:w0 + wd], ps_bb[:, 0:wd]), waits=[t2], inc="sv")

        with contextlib.ExitStack() as esS, contextlib.ExitStack() as esP:
            sbs = lambda n, sh, dt: esS.enter_context(nc.sbuf_tensor(n, list(sh), dt))
            xtbuf = [sbs(f"xtbuf{i}", [128, D], F32) for i in range(2)]
            stat12 = sbs("stat12", [128, 6 * NST], F32)
            mvall = sbs("mvall", [128, 2 * NTS], F32)
            negmu_c = sbs("negmu_c", [128, NTS], F32)
            lnv_c = sbs("lnv_c", [128, NTS], F32)
            rowtr = sbs("rowtr", [128, 128], F32)
            ps_tr = esP.enter_context(nc.psum_tensor("ps_tr", [128, 128], F32))

            prev_aggr = [None, None]
            for ts in range(NTS):
                sl = ts % 2
                ld = dmas(lambda s, ts=ts, sl=sl: s.dma_start(
                    out=xtbuf[sl][:, :], in_=x_tm.ap()[ts * 128:(ts + 1) * 128, :]),
                    f"dsx{sl}", waits=[prev_aggr[sl]])
                def fstat(v, sl=sl):
                    ii = None
                    for j in range(NST):
                        c0 = j * 512
                        cw = min(512, D - c0)
                        ii = v.bn_stats(stat12[:, 6 * j:6 * j + 6], xtbuf[sl][:, c0:c0 + cw])
                    return ii
                tstat = em("vector", fstat, waits=[ld], inc="sv")
                taggr = em("vector", lambda v, ts=ts: v.bn_aggr(
                    mvall[:, 2 * ts:2 * ts + 2], stat12[:, 0:6 * NST]),
                    waits=[tstat], inc="sv")
                prev_aggr[sl] = taggr
            allstats = prev_aggr[(NTS - 1) % 2]
            t_nm = em("vector", lambda v: v.tensor_scalar(
                negmu_c[:, :], mvall[:, 0:2 * NTS:2], -1.0, None, ALU.mult),
                waits=[allstats], inc="sv")
            t_lv = em("scalar", lambda s: s.activation(
                lnv_c[:, :], mvall[:, 1:2 * NTS:2], AF.Ln, bias=epst[:, 0:1]),
                waits=[allstats, t_eps], inc="sa")
            t_rr = em("scalar", lambda s: s.activation(
                r_cols[:, 0:NTS], lnv_c[:, :], AF.Exp, scale=-0.5),
                waits=[t_lv], inc="sa")
            assert NTS <= 128
            tk1 = em("tensor", lambda t: t.transpose(ps_tr[0:NTS, 0:128], negmu_c[:, 0:NTS],
                                                     idenf[:, :]),
                     waits=[t_nm, L["idf"]], inc="st")
            ck1 = em("vector", lambda v: v.tensor_copy(rowtr[0:NTS, 0:128],
                                                       ps_tr[0:NTS, 0:128]),
                     waits=[tk1], inc="sv")
            st1 = dmas(lambda s: s.dma_start(
                out=stats_d.ap()[0:1, :].rearrange("o (a b) -> (o a) b", b=128),
                in_=rowtr[0:NTS, 0:128]), "dst", waits=[ck1])
            tk2 = em("tensor", lambda t: t.transpose(ps_tr[0:NTS, 0:128], r_cols[:, 0:NTS],
                                                     idenf[:, :]),
                     waits=[t_rr, ck1], inc="st")
            ck2 = em("vector", lambda v: v.tensor_copy(rowtr[0:NTS, 0:128],
                                                       ps_tr[0:NTS, 0:128]),
                     waits=[tk2, st1], inc="sv")
            st2 = dmas(lambda s: s.dma_start(
                out=stats_d.ap()[1:2, :].rearrange("o (a b) -> (o a) b", b=128),
                in_=rowtr[0:NTS, 0:128]), "dst", waits=[ck2])

        rope_done = []
        with contextlib.ExitStack() as esP:
            ps_qkv = [esP.enter_context(nc.psum_tensor(f"ps_qkv{m}", [128, 512], F32))
                      for m in range(4)]
            ps_V = [esP.enter_context(nc.psum_tensor(f"ps_V{i}", [128, HPC * DK], F32))
                    for i in range(2)]
            ps_rb = esP.enter_context(nc.psum_tensor("ps_rb", [128, 512], F32))
            ps_b = esP.enter_context(nc.psum_tensor("ps_b", [1, 512], F32))

            t_vms = em("vector", lambda v: v.memset(Vtm[:, :], 1.0), inc="sv")
            qk_evac, v_evac = {}, {}
            rb_prev = None
            bnd_mm_prev = None
            bnd_red_last = None
            for g in range(NG):
                sl = g % 2
                wx = []
                if g >= 2:
                    wx = [qk_evac[(g - 2, 3)], v_evac[4 * (g - 2) + 3],
                          rope_done[4 * (g - 1) - 1]]
                xl = dmag(lambda gg, g=g, sl=sl: gg.dma_start(
                    out=xbuf[sl][:, :].rearrange("p (c n) -> p c n", c=DCH),
                    in_=x_fm.ap().rearrange("(c p) s -> p c s", p=128)
                        [:, :, g * 512:(g + 1) * 512]), f"dgx{sl}", waits=wx)
                rl1 = dmag(lambda gg, g=g, sl=sl: gg.dma_start(
                    out=nmrow[sl][:, :], in_=stats_d.ap()[0:1, g * 512:(g + 1) * 512]),
                    f"dgr{sl}", waits=wx + [st1])
                rl2 = dmag(lambda gg, g=g, sl=sl: gg.dma_start(
                    out=rrow[sl][:, :], in_=stats_d.ap()[1:2, g * 512:(g + 1) * 512]),
                    f"dgs{sl}", waits=[st2, rb_prev])
                wct = [rope_done[4 * (g - 1) - 1]] if g >= 2 else []
                ct = dmas(lambda s, g=g, sl=sl: s.dma_start(
                    out=ropeCb[sl][:, :], in_=ropeC.ap()[:, g * 512:(g + 1) * 512]),
                    f"dtc{sl}", waits=wct)
                st_ = dmas(lambda s, g=g, sl=sl: s.dma_start(
                    out=ropeSb[sl][:, :], in_=ropeS.ap()[:, g * 512:(g + 1) * 512]),
                    f"dts{sl}", waits=wct)

                trb = em("tensor", lambda t, sl=sl: t.matmul(
                    ps_rb[:, :], onesb[0:1, 0:128], rrow[sl][0:1, :],
                    start=True, stop=True),
                    waits=[rl2, t_ones, rb_prev], inc="st")
                tre = em("vector", lambda v: v.tensor_copy(rb_sb[:, :], ps_rb[:, :]),
                         waits=[trb], inc="sv")
                rb_prev = tre
                tcC = em("vector", lambda v, sl=sl: v.tensor_tensor(
                    ropeCb[sl][:, :], ropeCb[sl][:, :], rb_sb[:, :], ALU.mult),
                    waits=[ct, tre], inc="sv")
                tcS = em("vector", lambda v, sl=sl: v.tensor_tensor(
                    ropeSb[sl][:, :], ropeSb[sl][:, :], rb_sb[:, :], ALU.mult),
                    waits=[st_, tcC], inc="sv")

                for m in range(4):
                    wqkv = [xl, rl1, t_wqk, e_w1qk]
                    if g >= 1:
                        wqkv.append(qk_evac[(g - 1, m)])
                    def fqk(t, m=m, sl=sl):
                        for k in range(DCH):
                            t.matmul(ps_qkv[m][:, :],
                                     wqk_sb[:, k * 512 + m * 128:k * 512 + (m + 1) * 128],
                                     xbuf[sl][:, k * 512:(k + 1) * 512],
                                     start=(k == 0), stop=False)
                        return t.matmul(ps_qkv[m][:, :], w1qk[0:1, m * 128:(m + 1) * 128],
                                        nmrow[sl][0:1, :], start=False, stop=True)
                    tmm = em("tensor", fqk, waits=wqkv, inc="st")
                    pair, isK = m // 2, (m % 2 == 1)
                    dst = (Kp if isK else Qp)[pair]
                    qk_evac[(g, m)] = em("scalar", lambda s, dst=dst, m=m, g=g: s.activation(
                        dst[:, g * 512:(g + 1) * 512], ps_qkv[m][:, :], AF.Copy),
                        waits=[tmm], inc="sa")

                for tsl in range(4):
                    ts = 4 * g + tsl
                    wv_ = [xl, rl1, t_wv, e_w1v, t_vms]
                    if ts >= 2:
                        wv_.append(v_evac[ts - 2])
                    def fv(t, tsl=tsl, sl=sl):
                        for k in range(DCH):
                            t.matmul(ps_V[tsl % 2][:, :],
                                     xbuf[sl][:, k * 512 + tsl * 128:k * 512 + (tsl + 1) * 128],
                                     wv_sb[:, k * HPC * DK:(k + 1) * HPC * DK],
                                     start=(k == 0), stop=False)
                        return t.matmul(ps_V[tsl % 2][:, :],
                                        nmrow[sl][0:1, tsl * 128:(tsl + 1) * 128],
                                        w1v[0:1, :], start=False, stop=True)
                    tmv = em("tensor", fv, waits=wv_, inc="st")
                    v_evac[ts] = em("scalar", lambda s, ts=ts, tsl=tsl: s.activation(
                        Vtm[:, ts * VW:(ts + 1) * VW]
                            .rearrange("p (h d) -> p h d", h=HPC)[:, :, 0:DK],
                        ps_V[tsl % 2][:, :].rearrange("p (h d) -> p h d", h=HPC),
                        AF.Copy, scale=r_cols[:, ts:ts + 1]),
                        waits=[tmv, t_rr, t_vms], inc="sa")

                shmask = list(range(16, 32)) + list(range(16))
                for pair in range(2):
                    for which, base in ((0, Qp[pair]), (1, Kp[pair])):
                        wdep = qk_evac[(g, pair * 2 + which)]
                        t1_ = em("vector", lambda v, base=base, g=g: v.stream_shuffle(
                            ropet[0][:, :], base[:, g * 512:(g + 1) * 512], shmask),
                            waits=[wdep, tcS], inc="sv")
                        t2_ = em("vector", lambda v, sl=sl: v.tensor_tensor(
                            ropet[0][:, :], ropet[0][:, :], ropeSb[sl][:, :], ALU.mult),
                            waits=[t1_], inc="sv")
                        t3_ = em("vector", lambda v, base=base, g=g, sl=sl: v.tensor_tensor(
                            ropet[1][:, :], base[:, g * 512:(g + 1) * 512],
                            ropeCb[sl][:, :], ALU.mult),
                            waits=[t2_], inc="sv")
                        t4_ = em("vector", lambda v, base=base, g=g: v.tensor_tensor(
                            base[:, g * 512:(g + 1) * 512], ropet[0][:, :],
                            ropet[1][:, :], ALU.add),
                            waits=[t3_], inc="sv")
                        rope_done.append(t4_)

                for m in range(4):
                    pair, isK = m // 2, (m % 2 == 1)
                    srcT = (Kp if isK else Qp)[pair]
                    tsq = em("vector", lambda v, srcT=srcT, g=g: v.tensor_tensor(
                        qsq[:, :], srcT[:, g * 512:(g + 1) * 512],
                        srcT[:, g * 512:(g + 1) * 512], ALU.mult),
                        waits=[rope_done[4 * g + m], bnd_mm_prev], inc="sv")
                    bnd_mm = em("tensor", lambda t: t.matmul(
                        ps_b[0:1, :], onescol[:, 0:1], qsq[:, :], start=True, stop=True),
                        waits=[tsq, bnd_red_last, t_onesc], inc="st")
                    bnd_mm_prev = bnd_mm
                    bnd_red_last = em("vector", lambda v, g=g, m=m: v.tensor_reduce(
                        bcol[0:1, 4 * g + m:4 * g + m + 1], ps_b[0:1, :], AXX, ALU.max),
                        waits=[bnd_mm], inc="sv")
        qkv_sv = ("sv", P.cnt["sv"])
        qkv_sa = ("sa", P.cnt["sa"])
        qkv_st = ("st", P.cnt["st"])

        with contextlib.ExitStack() as esP:
            ps_bc = esP.enter_context(nc.psum_tensor("ps_bc", [128, 1], F32))
            tq = em("vector", lambda v: v.tensor_reduce(
                bsc[0:1, 0:1], bcol[0:1, 0:4 * NG:2], AXX, ALU.max),
                waits=[bnd_red_last, qkv_sv], inc="sv")
            tk = em("vector", lambda v: v.tensor_reduce(
                bcol[0:1, 0:1], bcol[0:1, 1:4 * NG:2], AXX, ALU.max),
                waits=[tq], inc="sv")
            tm = em("vector", lambda v: v.tensor_mul(bsc[0:1, 0:1], bsc[0:1, 0:1],
                                                     bcol[0:1, 0:1]), waits=[tk], inc="sv")
            tl = em("scalar", lambda s: s.activation(bsc[0:1, 0:1], bsc[0:1, 0:1], AF.Ln),
                    waits=[tm], inc="sa")
            te = em("scalar", lambda s: s.activation(bsc[0:1, 0:1], bsc[0:1, 0:1],
                                                     AF.Exp, scale=0.5),
                    waits=[tl], inc="sa")
            tcst = em("vector", lambda v: v.tensor_copy(bscb[0:1, 0:1], bsc[0:1, 0:1]),
                      waits=[te], inc="sv")
            tbc = em("tensor", lambda t: t.matmul(ps_bc[:, 0:1], onesb[0:1, 0:128],
                                                  bscb[0:1, 0:1], start=True, stop=True),
                     waits=[tcst], inc="st")
            bound_sv = em("vector", lambda v: v.tensor_scalar(
                bneg[:, 0:1], ps_bc[:, 0:1], -0.125, None, ALU.mult),
                waits=[tbc], inc="sv")

        with contextlib.ExitStack() as esP:
            ps_S = [esP.enter_context(nc.psum_tensor(f"ps_S{i}", [128, 1024], F32))
                    for i in range(3)]
            ps_O = [esP.enter_context(nc.psum_tensor(f"ps_O{h}", [65, 512], F32))
                    for h in range(2)]

            i_glob = 0
            seq = 0
            score_tok, exp_tok, mask_tok, pv_tok = {}, {}, {}, {}
            evac_toks, store_toks = [], []
            cc_toks = []
            for pair in range(2):
                for qt in range(NQT):
                    nkt = 4 * (qt + 1)
                    base_i = i_glob

                    def emit_sc(j):
                        kt = j
                        i = base_i + j
                        w = [bound_sv]
                        if i == 0:
                            w += [rope_done[-1], qkv_sv, qkv_sa]
                        if i - 3 >= 0:
                            w.append(exp_tok[i - 3])
                        def fsc(t, pair=pair, qt=qt, kt=kt, i=i):
                            ii = None
                            for h in range(2):
                                ii = t.matmul(
                                    ps_S[i % 3][:, h * 512:(h + 1) * 512],
                                    Kp[pair][h * 64:(h + 1) * 64,
                                             kt * 128:(kt + 1) * 128],
                                    Qp[pair][h * 64:(h + 1) * 64,
                                             qt * 512:(qt + 1) * 512],
                                    start=True, stop=True)
                            return ii
                        score_tok[i] = em("tensor", fsc, waits=w, inc="st")
                        w = [score_tok[i]]
                        if i - 3 >= 0:
                            w.append(pv_tok[i - 3])
                        exp_tok[i] = em("scalar", lambda s, i=i: s.activation(
                            Pt[i % 3][:, :], ps_S[i % 3][:, :], AF.Exp,
                            bias=bneg[:, 0:1], scale=0.125),
                            waits=w, inc="sa")
                        if kt >= 4 * qt:
                            mi = kt - 4 * qt
                            def fmk(v, i=i, mi=mi):
                                v.tensor_tensor(Pt[i % 3][:, 0:512], Pt[i % 3][:, 0:512],
                                                mask_sb[:, mi * 512:(mi + 1) * 512],
                                                ALU.mult)
                                return v.tensor_tensor(
                                    Pt[i % 3][:, 512:1024], Pt[i % 3][:, 512:1024],
                                    mask_sb[:, mi * 512:(mi + 1) * 512], ALU.mult)
                            mask_tok[i] = em("vector", fmk,
                                             waits=[exp_tok[i], t_msk], inc="sv")

                    def emit_pv(j):
                        kt = j
                        i = base_i + j
                        w = [mask_tok[i] if i in mask_tok else exp_tok[i]]
                        if j == 0:
                            w.append(evac_toks[-1] if evac_toks else qkv_sa)
                        def fpv(t, pair=pair, kt=kt, i=i, first=(j == 0),
                                last=(j == nkt - 1)):
                            ii = None
                            for h in range(2):
                                hl = 2 * pair + h
                                ii = t.matmul(
                                    ps_O[h][0:65, :],
                                    Vtm[:, kt * VW + hl * (DK + 1):
                                        kt * VW + hl * (DK + 1) + 65],
                                    Pt[i % 3][:, h * 512:(h + 1) * 512],
                                    start=first, stop=last)
                            return ii
                        pv_tok[i] = em("tensor", fpv, waits=w, inc="st")

                    for j in range(nkt):
                        emit_sc(j)
                        if j >= 2:
                            emit_pv(j - 2)
                    emit_pv(nkt - 2)
                    emit_pv(nkt - 1)
                    i_glob = base_i + nkt

                    for h in range(2):
                        hl = 2 * pair + h
                        w = [pv_tok[i_glob - 1]]
                        if seq >= 1:
                            w.append(store_toks[2 * (seq - 1) + h])
                        tev = em("vector", lambda v, h=h: v.tensor_copy(
                            ostg[h][0:65, :], ps_O[h][0:65, :]), waits=w, inc="sv")
                        evac_toks.append(tev)
                        store_toks.append(dmas(
                            lambda s, h=h, hl=hl, qt=qt: s.dma_start(
                                out=xch_in.ap()[hl * 65:(hl + 1) * 65,
                                                qt * 512:(qt + 1) * 512],
                                in_=ostg[h][0:65, :]),
                            f"dso{h}", waits=[tev]))
                    seq += 1
                cc_toks.append(em(
                    "gpsimd",
                    lambda g, pair=pair: g.collective_compute(
                        "AllGather", ALU.bypass,
                        replica_groups=[list(range(8))],
                        ins=[xch_in.ap()[pair * 130:(pair + 1) * 130, :]],
                        outs=[xch_out[pair].ap()]),
                    waits=[("dso0", P.cnt["dso0"]), ("dso1", P.cnt["dso1"])],
                    inc="scc"))

        for s in ("st", "sv", "sa", "dg", "ds"):
            AE[s] = P.cnt[s]

    with contextlib.ExitStack() as esC:
        sbc = lambda n, sh, dt: esC.enter_context(nc.sbuf_tensor(n, list(sh), dt))
        z_sb = sbc("z_sb", [128, NOT * D], F32)
        xres_b = [sbc(f"xres{i}", [128, D], F32) for i in range(2)]
        yln = sbc("yln", [128, D], F32)
        st12c = sbc("st12c", [128, 6 * NST], F32)
        mvc = sbc("mvc", [128, 2], F32)
        nmc = sbc("nmc", [128, 1], F32)
        rsc = sbc("rsc", [128, 1], F32)
        zbf = sbc("zbf", [128, D], BF16)
        negmu4 = sbc("negmu4", [128, NOT], F32)
        lnv4 = sbc("lnv4", [128, NOT], F32)
        r4 = sbc("r4", [128, NOT], F32)
        rowtr2 = sbc("rowtr2", [128, 128], BF16)
        nm4row = [sbc(f"nm4row{i}", [1, 512], BF16) for i in range(2)]
        ro_sb = sbc("ro_sb", [E, 512], F32)
        gtm = sbc("gtm", [128, 4 * E], F32)
        gt4 = sbc("gt4", [128, 20], F32)
        gmask = sbc("gmask", [128, 4 * E], BF16)
        wtm = sbc("wtm", [128, 4 * E], F32)
        wfm = sbc("wfm", [E, 512], BF16)
        wr_sb = sbc("wr_sb", [LR, 512], BF16)
        hs_sb = sbc("hs_sb", [LR, 512], BF16)
        ffn1 = sbc("ffn1", [128, D], F32)
        outb = [sbc("outb0", [128, D], F32)]
        outb.append(outb[0])
        d1 = sbc("d1", [128, 1], F32)
        rk_sb = sbc("rk_sb", [1, 1], U32)

        AEw = [("st", AE["st"]), ("sv", AE["sv"]), ("sa", AE["sa"])]
        zfm = sbc("zfm", [128, DCH * OWN], BF16)

        with contextlib.ExitStack() as esC1:
            sbc1 = lambda n, sh, dt: esC1.enter_context(nc.sbuf_tensor(n, list(sh), dt))
            Och = sbc1("Och", [128, FCH * OWN], BF16)
            sums_sb = [sbc1("sums_sbX", [8, OWN], BF16)]
            sums_sb.append(sums_sb[0])
            recf = sbc1("recf", [8, 256], F32)
            esP = esC1.enter_context(contextlib.ExitStack())
            ps_y = [esP.enter_context(nc.psum_tensor(f"ps_y{i}", [128, D], F32))
                    for i in range(2)]
            ps_rc = [esP.enter_context(nc.psum_tensor(f"ps_rc{i}", [128, 512], F32))
                     for i in range(2)]

            roff = nc.alloc_register(mybir.EngineType.SP, "roff")
            rtmp = nc.alloc_register(mybir.EngineType.SP, "rtmp")
            t_rk = dmas(lambda s: s.dma_start(out=rk_sb[:, :], in_=rkoff.ap()), "dsv0")

            def mk_recv(p):
                def f_recv(s):
                    if p == 0:
                        s.reg_load(roff, rk_sb[0:1, 0:1])
                    for k in range(p, FCH, 2):
                        rowbase = 2 * (DK + 1) * (k // 2)
                        s.reg_mov(rtmp, rowbase * S)
                        s.reg_add(rtmp, rtmp, roff)
                        s.dma_start(
                            out=Och[:, k * OWN:(k + 1) * OWN],
                            in_=bass.AP(xch_out[p], rtmp,
                                        [[(DK + 1) * S, 2], [S, DK], [1, OWN]])
                        ).then_inc(SEMS[f"dsv{p}"], 16)
                    s.reg_mov(rtmp, DK * S)
                    s.reg_add(rtmp, rtmp, roff)
                    return s.dma_start(
                        out=sums_sb[p][:, :],
                        in_=bass.AP(xch_out[p], rtmp,
                                    [[2 * (DK + 1) * S, 4], [(DK + 1) * S, 2], [1, OWN]]))
                return f_recv
            t_recv = [None, None]
            t_rcb = [None, None]
            div_last = None
            div_tts = {}
            di = 0
            for p in range(2):
                P.cnt[f"dsv{p}"] += 16 * (FCH // 2)
                wrecv = [("scc", p + 1), t_rk]
                if p == 1:
                    wrecv.append(("st", P.cnt["st"]))
                t_recv[p] = em("sync", mk_recv(p), waits=wrecv, inc=f"dsv{p}", incn=16)
                for _c in range(OWN // 256):
                    a_ = em("vector", lambda v, p=p, _c=_c: v.tensor_copy(
                        recf[:, :], sums_sb[p][:, _c * 256:(_c + 1) * 256]),
                        waits=[t_recv[p]] + AEw, inc="sv")
                    b_ = em("vector", lambda v: v.reciprocal(recf[:, :], recf[:, :]),
                            waits=[a_], inc="sv")
                    t_rcb[p] = em("vector", lambda v, p=p, _c=_c: v.tensor_copy(
                        sums_sb[p][:, _c * 256:(_c + 1) * 256], recf[:, :]),
                        waits=[b_], inc="sv")
                for k in range(p, FCH, 2):
                    for nt in range(OWN // 512):
                        w = [t_rcb[p], ("sa", AE["sa"]), L["bsel"], div_tts.get(di - 2)]
                        tmm = em("tensor", lambda t, p=p, k=k, nt=nt, di=di: t.matmul(
                            ps_rc[di % 2][:, :], bsel_sb[0:8, k * 128:(k + 1) * 128],
                            sums_sb[p][:, nt * 512:(nt + 1) * 512],
                            start=True, stop=True),
                            waits=w, inc="st")
                        div_last = em("vector", lambda v, k=k, nt=nt, di=di: v.tensor_tensor(
                            Och[:, k * OWN + nt * 512:k * OWN + (nt + 1) * 512],
                            Och[:, k * OWN + nt * 512:k * OWN + (nt + 1) * 512],
                            ps_rc[di % 2][:, :], ALU.mult),
                            waits=[tmm], inc="sv")
                        div_tts[di] = div_last
                        di += 1

            z_toks = []
            prev_ln = None
            prev_ln2 = [None, None]
            prev_ta3 = None
            xr_war = [None, None]
            for ti in range(NOT):
                sl = ti % 2
                t_xr = dmas(lambda s, ti=ti, sl=sl: s.dma_start(
                    out=xres_b[sl][:, :], in_=x_res.ap()[ti * 128:(ti + 1) * 128, :]),
                    f"dsr{sl}", waits=[xr_war[sl]])
                w = [div_last, L["woT"], prev_ln2[ti % 2]]
                def fwo(t, ti=ti):
                    ii = None
                    for hh in range(NDH):
                        w0 = hh * 512
                        wd = min(512, D - w0)
                        for k in range(FCH):
                            ii = t.matmul(ps_y[ti % 2][:, w0:w0 + wd],
                                          Och[:, k * OWN + ti * 128:k * OWN + (ti + 1) * 128],
                                          woT_sb[:, k * D + w0:k * D + w0 + wd],
                                          start=(k == 0), stop=(k == FCH - 1))
                    return ii
                tmy = em("tensor", fwo, waits=w, inc="st")
                def fst3(v, ti=ti):
                    ii = None
                    for j in range(NST):
                        c0 = j * 512
                        cw = min(512, D - c0)
                        ii = v.bn_stats(st12c[:, 6 * j:6 * j + 6],
                                        ps_y[ti % 2][:, c0:c0 + cw])
                    return ii
                ts1 = em("vector", fst3, waits=[tmy], inc="sv")
                ts2 = em("vector", lambda v: v.bn_aggr(mvc[:, :], st12c[:, 0:6 * NST]),
                         waits=[ts1, prev_ta3], inc="sv")
                ts3 = em("vector", lambda v: v.tensor_scalar(
                    nmc[:, 0:1], mvc[:, 0:1], -1.0, None, ALU.mult), waits=[ts2], inc="sv")
                ta1 = em("scalar", lambda s: s.activation(
                    rsc[:, 0:1], mvc[:, 1:2], AF.Ln, bias=epst[:, 0:1]),
                    waits=[ts2, prev_ln], inc="sa")
                ta2 = em("scalar", lambda s: s.activation(
                    rsc[:, 0:1], rsc[:, 0:1], AF.Exp, scale=-0.5), waits=[ta1], inc="sa")
                ts4 = em("vector", lambda v, ti=ti: v.tensor_scalar(
                    yln[:, :], ps_y[ti % 2][:, :], nmc[:, 0:1], rsc[:, 0:1],
                    ALU.add, ALU.mult),
                    waits=[ts3, ta2], inc="sv")
                prev_ln = ts4
                prev_ln2[ti % 2] = ts4
                prev_ta3 = ta2
                tz = em("vector", lambda v, ti=ti, sl=sl: v.tensor_tensor(
                    z_sb[:, ti * D:(ti + 1) * D], yln[:, :], xres_b[sl][:, :], ALU.add),
                    waits=[ts4, t_xr], inc="sv")
                xr_war[sl] = tz
                z_toks.append(tz)

        with contextlib.ExitStack() as esP:
            ps_zt = [esP.enter_context(nc.psum_tensor(f"ps_zt{i}", [128, 128], BF16))
                     for i in range(2)]
            ps_tr2 = esP.enter_context(nc.psum_tensor("ps_tr2", [128, 128], F32))
            ln4_nm, ln4_r = [], []
            prev_tr = None
            ztev = {}
            nzt = 0
            for ti in range(NOT):
                tc = em("vector", lambda v, ti=ti: v.tensor_copy(
                    zbf[:, :], z_sb[:, ti * D:(ti + 1) * D]),
                    waits=[z_toks[ti], prev_tr], inc="sv")
                for k in range(DCH):
                    tt = em("tensor", lambda t, k=k, nzt=nzt: t.transpose(
                        ps_zt[nzt % 2][:, :], zbf[:, k * 128:(k + 1) * 128], idenb[:, :]),
                        waits=[tc, L["idb"], ztev.get(nzt - 2)], inc="st")
                    prev_tr = em("vector", lambda v, k=k, ti=ti, nzt=nzt: v.tensor_copy(
                        zfm[:, k * OWN + ti * 128:k * OWN + (ti + 1) * 128],
                        ps_zt[nzt % 2][:, :]),
                        waits=[tt], inc="sv")
                    ztev[nzt] = prev_tr
                    nzt += 1
                def fs4(v, ti=ti):
                    ii = None
                    for j in range(NST):
                        c0 = j * 512
                        cw = min(512, D - c0)
                        ii = v.bn_stats(st12c[:, 6 * j:6 * j + 6],
                                        z_sb[:, ti * D + c0:ti * D + c0 + cw])
                    return ii
                t41 = em("vector", fs4, waits=[z_toks[ti]], inc="sv")
                t42 = em("vector", lambda v: v.bn_aggr(mvc[:, :], st12c[:, 0:6 * NST]),
                         waits=[t41, prev_ta3, ln4_r[-1] if ln4_r else None], inc="sv")
                t43 = em("vector", lambda v, ti=ti: v.tensor_scalar(
                    negmu4[:, ti:ti + 1], mvc[:, 0:1], -1.0, None, ALU.mult),
                    waits=[t42], inc="sv")
                t44 = em("scalar", lambda s, ti=ti: s.activation(
                    lnv4[:, ti:ti + 1], mvc[:, 1:2], AF.Ln, bias=epst[:, 0:1]),
                    waits=[t42], inc="sa")
                t45 = em("scalar", lambda s, ti=ti: s.activation(
                    r4[:, ti:ti + 1], lnv4[:, ti:ti + 1], AF.Exp, scale=-0.5),
                    waits=[t44], inc="sa")
                ln4_nm.append(t43)
                ln4_r.append(t45)
            ztr_sv = ("sv", P.cnt["sv"])

            assert NOT <= 128
            tz1 = em("tensor", lambda t: t.transpose(
                ps_tr2[0:NOT, 0:128], negmu4[:, 0:NOT], idenf[:, :]),
                waits=ln4_nm + [prev_tr], inc="st")
            cz1 = em("vector", lambda v: v.tensor_copy(rowtr2[0:NOT, 0:128],
                                                       ps_tr2[0:NOT, 0:128]),
                     waits=[tz1], inc="sv")
            sz1 = dmas(lambda s: s.dma_start(
                out=zstats_d.ap()[0:1, :].rearrange("o (a b) -> (o a) b", b=128),
                in_=rowtr2[0:NOT, 0:128]), "dst", waits=[cz1])
            c_mid_st = ("st", P.cnt["st"])

        with contextlib.ExitStack() as esP:
            ps_ffn = esP.enter_context(nc.psum_tensor("ps_ffn", [128, D], F32))
            ps_lr = [esP.enter_context(nc.psum_tensor(f"ps_lr{i}", [LR, 512], F32))
                     for i in range(2)]
            ps_ro = esP.enter_context(nc.psum_tensor("ps_ro", [E, 512], F32))
            ps_g = esP.enter_context(nc.psum_tensor("ps_g", [128, 4 * E], F32))
            ps_wt = esP.enter_context(nc.psum_tensor("ps_wt", [E, 512], F32))
            ps_wb = esP.enter_context(nc.psum_tensor("ps_wb", [LR, 512], F32))

            prev_rows = None
            prev_hs = None
            prev_hs2 = [None, None]
            prev_ffn = None
            out_final = None
            prev_ua2 = [None]
            prev_u4 = [None]
            for g4 in range(cfg.NFG):
                sl = g4 % 2
                l1 = dmag(lambda g, g4=g4, sl=sl: g.dma_start(
                    out=nm4row[sl][:, :],
                    in_=zstats_d.ap()[0:1, g4 * 512:(g4 + 1) * 512]),
                    f"dgz{sl}", waits=[sz1, prev_rows])

                w = [ztr_sv, l1, L["lrAT"], e_w1lr, c_mid_st, prev_hs2[g4 % 2]]
                def flr(t, g4=g4, sl=sl):
                    for k in range(DCH):
                        t.matmul(ps_lr[g4 % 2][0:LR, :], lrAT_sb[:, k * LR:(k + 1) * LR],
                                 zfm[:, k * OWN + g4 * 512:k * OWN + (g4 + 1) * 512],
                                 start=(k == 0), stop=False)
                    return t.matmul(ps_lr[g4 % 2][0:LR, :], w1lr[0:1, :],
                                    nm4row[sl][0:1, :], start=False, stop=True)
                t_lr = em("tensor", flr, waits=w, inc="st")

                wro = [L["routerWT"], e_w1ro]
                def fro(t, g4=g4, sl=sl):
                    for k in range(DCH):
                        t.matmul(ps_ro[0:E, :], routerWT_sb[:, k * E:(k + 1) * E],
                                 zfm[:, k * OWN + g4 * 512:k * OWN + (g4 + 1) * 512],
                                 start=(k == 0), stop=False)
                    return t.matmul(ps_ro[0:E, :], w1ro[0:1, :], nm4row[sl][0:1, :],
                                    start=False, stop=True)
                t_ro = em("tensor", fro, waits=wro, inc="st")
                t_roe = em("vector", lambda v: v.tensor_copy(ro_sb[0:E, :], ps_ro[0:E, :]),
                           waits=[t_ro], inc="sv")

                tr_toks = []
                for j in range(4):
                    tt = em("tensor", lambda t, j=j: t.transpose(
                        ps_g[:, j * E:(j + 1) * E], ro_sb[0:E, j * 128:(j + 1) * 128],
                        idenf[0:E, 0:E]),
                        waits=[t_roe], inc="st")
                    tr_toks.append(tt)
                tg0 = em("vector", lambda v: v.tensor_copy(gtm[:, :], ps_g[:, :]),
                         waits=tr_toks, inc="sv")
                gk = tg0
                for j in range(4):
                    ti4 = 4 * g4 + j
                    gk = em("vector", lambda v, j=j, ti4=ti4: v.tensor_scalar(
                        gtm[:, j * E:(j + 1) * E], gtm[:, j * E:(j + 1) * E],
                        r4[:, ti4:ti4 + 1], None, ALU.mult),
                        waits=[gk, ln4_r[ti4]], inc="sv")
                gk = em("vector", lambda v: v.tensor_add(gtm[:, :], gtm[:, :], rb6_sb[:, :]),
                        waits=[gk, e_rb6], inc="sv")
                G3 = gtm[:, :].rearrange("p (j e) -> p j e", j=4)
                M3 = gmask[:, :].rearrange("p (j e) -> p j e", j=4)
                W3 = wtm[:, :].rearrange("p (j e) -> p j e", j=4)
                bc = lambda ap: ap.unsqueeze(2).broadcast_to([128, 4, E])
                gk = em("vector", lambda v: v.tensor_reduce(
                    gt4[:, 0:4], G3, AXX, ALU.max, negate=True), waits=[gk], inc="sv")
                gk = em("vector", lambda v: v.tensor_tensor(
                    G3, G3, bc(gt4[:, 0:4]), ALU.add), waits=[gk], inc="sv")
                gexp = em("scalar", lambda s: s.activation(gtm[:, :], gtm[:, :], AF.Exp),
                          waits=[gk], inc="sa")
                gk = em("vector", lambda v: v.tensor_reduce(
                    gt4[:, 4:8], G3, AXX, ALU.add), waits=[gexp], inc="sv")
                gk = em("vector", lambda v: v.reciprocal(gt4[:, 4:8], gt4[:, 4:8]),
                        waits=[gk], inc="sv")
                gk = em("vector", lambda v: v.tensor_tensor(
                    G3, G3, bc(gt4[:, 4:8]), ALU.mult), waits=[gk], inc="sv")
                gk = em("vector", lambda v: v.tensor_reduce(
                    gt4[:, 0:4], G3, AXX, ALU.max), waits=[gk], inc="sv")
                gk = em("vector", lambda v: v.tensor_tensor(
                    M3, G3, bc(gt4[:, 0:4]), ALU.is_ge), waits=[gk], inc="sv")
                gk = em("vector", lambda v: v.scalar_tensor_tensor(
                    W3, M3, -1e30, G3, ALU.mult, ALU.add), waits=[gk], inc="sv")
                gk = em("vector", lambda v: v.tensor_reduce(
                    gt4[:, 8:12], W3, AXX, ALU.max), waits=[gk], inc="sv")
                gk = em("vector", lambda v: v.tensor_reduce(
                    gt4[:, 12:16], M3, AXX, ALU.add), waits=[gk], inc="sv")
                gk = em("vector", lambda v: v.tensor_scalar(
                    gt4[:, 12:16], gt4[:, 12:16], -1.0, 1.0, ALU.add, ALU.min),
                    waits=[gk], inc="sv")
                gk = em("vector", lambda v: v.tensor_sub(
                    gt4[:, 16:20], gt4[:, 0:4], gt4[:, 8:12]), waits=[gk], inc="sv")
                gk = em("vector", lambda v: v.tensor_mul(
                    gt4[:, 16:20], gt4[:, 16:20], gt4[:, 12:16]), waits=[gk], inc="sv")
                gk = em("vector", lambda v: v.tensor_add(
                    gt4[:, 8:12], gt4[:, 8:12], gt4[:, 16:20]), waits=[gk], inc="sv")
                gk = em("vector", lambda v: v.tensor_tensor(
                    M3, G3, bc(gt4[:, 8:12]), ALU.is_ge), waits=[gk], inc="sv")
                gk = em("vector", lambda v: v.tensor_tensor(
                    W3, G3, M3, ALU.mult), waits=[gk], inc="sv")
                gk = em("vector", lambda v: v.tensor_reduce(
                    gt4[:, 4:8], W3, AXX, ALU.add), waits=[gk], inc="sv")
                gk = em("vector", lambda v: v.tensor_scalar(
                    gt4[:, 4:8], gt4[:, 4:8], 1e-6, None, ALU.add), waits=[gk], inc="sv")
                gk = em("vector", lambda v: v.reciprocal(gt4[:, 4:8], gt4[:, 4:8]),
                        waits=[gk], inc="sv")
                gk = em("vector", lambda v: v.tensor_tensor(
                    W3, W3, bc(gt4[:, 4:8]), ALU.mult), waits=[gk], inc="sv")
                wtr = []
                for j in range(4):
                    tt = em("tensor", lambda t, j=j: t.transpose(
                        ps_wt[0:E, j * 128:(j + 1) * 128], wtm[:, j * E:(j + 1) * E],
                        idenf[:, :]),
                        waits=[gk], inc="st")
                    wtr.append(tt)
                twf = em("vector", lambda v: v.tensor_copy(wfm[0:E, :], ps_wt[0:E, :]),
                         waits=wtr, inc="sv")
                def fwb(t):
                    t.matmul(ps_wb[0:LR, :], g6_sb[0:E, :], wfm[0:E, :],
                             start=True, stop=False)
                    return t.matmul(ps_wb[0:LR, :], shsel_sb[0:1, :], onesb[0:1, :],
                                    start=False, stop=True)
                t_wb = em("tensor", fwb, waits=[twf, L["g6"], L["sh"], t_ones], inc="st")
                t_wre = em("vector", lambda v: v.tensor_copy(wr_sb[0:LR, :], ps_wb[0:LR, :]),
                           waits=[t_wb], inc="sv")
                t_hs = em("vector", lambda v, g4=g4: v.tensor_tensor(
                    hs_sb[0:LR, :], ps_lr[g4 % 2][0:LR, :], wr_sb[0:LR, :], ALU.mult),
                    waits=[t_wre, t_lr], inc="sv")
                prev_hs = t_hs
                prev_hs2[g4 % 2] = t_hs

                for tsl in range(4):
                    ti = 4 * g4 + tsl
                    w = [t_hs, L["baseWT"], L["Bstack"], e_w1b[-1], l1]
                    if prev_ffn is not None:
                        w.append(prev_ffn)
                    def fbase(t, g4=g4, tsl=tsl, sl=sl):
                        ii = None
                        for hh in range(NDH):
                            w0 = hh * 512
                            wd = min(512, D - w0)
                            for k in range(DCH):
                                t.matmul(ps_ffn[:, w0:w0 + wd],
                                         zfm[:, k * OWN + g4 * 512 + tsl * 128:
                                             k * OWN + g4 * 512 + (tsl + 1) * 128],
                                         baseWT_sb[:, k * D + w0:k * D + w0 + wd],
                                         start=(k == 0), stop=False)
                            t.matmul(ps_ffn[:, w0:w0 + wd],
                                     nm4row[sl][0:1, tsl * 128:(tsl + 1) * 128],
                                     w1base[0:1, w0:w0 + wd], start=False, stop=False)
                            ii = t.matmul(ps_ffn[:, w0:w0 + wd],
                                          hs_sb[0:LR, tsl * 128:(tsl + 1) * 128],
                                          BstackT_sb[0:LR, w0:w0 + wd],
                                          start=False, stop=True)
                        return ii
                    t_f = em("tensor", fbase, waits=w, inc="st")
                    tf1 = em("vector", lambda v, ti=ti: v.scalar_tensor_tensor(
                        ffn1[:, :], ps_ffn[:, :], r4[:, ti:ti + 1], bbc_sb[:, :],
                        ALU.mult, ALU.add),
                        waits=[t_f, e_bbc, ln4_r[ti]], inc="sv")
                    prev_ffn = tf1
                    def fs5(v):
                        ii = None
                        for j in range(NST):
                            c0 = j * 512
                            cw = min(512, D - c0)
                            ii = v.bn_stats(st12c[:, 6 * j:6 * j + 6], ffn1[:, c0:c0 + cw])
                        return ii
                    u1 = em("vector", fs5, waits=[tf1], inc="sv")
                    u2 = em("vector", lambda v: v.bn_aggr(mvc[:, :], st12c[:, 0:6 * NST]),
                            waits=[u1, prev_ua2[0]], inc="sv")
                    u3 = em("vector", lambda v: v.tensor_scalar(
                        nmc[:, 0:1], mvc[:, 0:1], -1.0, None, ALU.mult),
                        waits=[u2], inc="sv")
                    ua1 = em("scalar", lambda s: s.activation(
                        rsc[:, 0:1], mvc[:, 1:2], AF.Ln, bias=epst[:, 0:1]),
                        waits=[u2, prev_u4[0]], inc="sa")
                    ua2 = em("scalar", lambda s: s.activation(
                        rsc[:, 0:1], rsc[:, 0:1], AF.Exp, scale=-0.5),
                        waits=[ua1], inc="sa")
                    u4 = em("vector", lambda v: v.tensor_scalar(
                        yln[:, :], ffn1[:, :], nmc[:, 0:1], rsc[:, 0:1],
                        ALU.add, ALU.mult),
                        waits=[u3, ua2], inc="sv")
                    prev_ua2[0] = ua2
                    prev_u4[0] = u4
                    u5 = em("vector", lambda v, ti=ti, tsl=tsl: v.tensor_tensor(
                        outb[tsl % 2][:, :], yln[:, :], z_sb[:, ti * D:(ti + 1) * D],
                        ALU.add),
                        waits=[u4, ("dou0", P.cnt["dou0"])],
                        inc="sv")
                    out_final = dmas(lambda s, ti=ti, tsl=tsl: s.dma_start(
                        out=out_t.ap()[ti * 128:(ti + 1) * 128, :],
                        in_=outb[tsl % 2][:, :]), "dou0", waits=[u5])
                prev_rows = prev_ffn

            wt("sync", [(s, P.cnt[s]) for s in list(P.cnt.keys())
                        if s.startswith("d") or s == "scc"])

    sems = {name: es.enter_context(nc.semaphore(name)) for name in P.cnt}
    SEMS.update(sems)
    nc._keepalive_es = es
    return nc, sems, P


def emit(nc, sems, P):
    with nc.Block() as blk:
        def mk(name):
            def run(eng):
                seen = {}
                for ws, fn, tok, incn in P.ops[name]:
                    for s, v in ws:
                        if v > seen.get(s, -1):
                            eng.wait_ge(sems[s], v)
                            seen[s] = v
                    if fn is None:
                        continue
                    inst = fn(eng)
                    if tok is not None:
                        inst.then_inc(sems[tok[0]], incn)
            return run
        blk.tensor(mk("tensor"))
        blk.vector(mk("vector"))
        blk.scalar(mk("scalar"))
        blk.gpsimd(mk("gpsimd"))
        blk.sync(mk("sync"))
    return nc



def host_consts(cfg: Cfg):
    S, DK = cfg.S, cfg.DK
    rope_dim = DK // 2
    inv = 1.0 / ROPE_BASE ** (np.arange(0, rope_dim, 2, dtype=np.float32) / rope_dim)
    t = np.arange(S, dtype=np.float32)
    fr = np.outer(t, inv)
    emb = np.concatenate([fr, fr], axis=1)
    cos = np.cos(emb).astype(np.float32)
    sin = np.sin(emb).astype(np.float32)
    C = np.ones((128, S), np.float32)
    Sg = np.zeros((128, S), np.float32)
    for blk in (0, 64):
        C[blk:blk + 32] = cos.T
        Sg[blk:blk + 16] = -sin.T[:16]
        Sg[blk + 16:blk + 32] = sin.T[16:32]
    masks = np.zeros((4 * 128, 512), np.float32)
    p = np.arange(128)[:, None]
    c = np.arange(512)[None, :]
    for i in range(4):
        masks[i * 128:(i + 1) * 128] = (c >= 128 * i + p)
    bsel = np.zeros((16, 8 * 128), np.float32)
    for k in range(8):
        for a in range(2):
            row = 2 * (k // 2) + a
            bsel[row, k * 128 + a * 64:k * 128 + (a + 1) * 64] = 1.0
    g6 = np.zeros((cfg.E, cfg.LR), np.float32)
    for e in range(cfg.E):
        g6[e, cfg.R + cfg.R * e:cfg.R + cfg.R * (e + 1)] = 1.0
    sh = np.zeros((1, cfg.LR), np.float32)
    sh[0, :cfg.R] = 1.0
    return {
        "ropeC": C.astype(BF), "ropeS": Sg.astype(BF),
        "masks": masks.astype(BF),
        "iden_bf": np.eye(128, dtype=np.float32).astype(BF),
        "iden_f": np.eye(128, dtype=np.float32),
        "bsel": bsel.astype(BF), "g6sel": g6.astype(BF), "shsel": sh.astype(BF),
    }


def core_inputs(cfg: Cfg, A, c, consts):
    b, hg = c // 4, c % 4
    x = np.asarray(A["L0_attn_x"][b])
    Wqkv = np.asarray(A["L0_attn_W_qkv"])
    hs = [4 * hg + i for i in range(cfg.HPC)]
    qp = []
    for pr in range(2):
        h0, h1 = hs[2 * pr], hs[2 * pr + 1]
        qp.append(np.concatenate([Wqkv[0, h0].T, Wqkv[0, h1].T], axis=1))
        qp.append(np.concatenate([Wqkv[1, h0].T, Wqkv[1, h1].T], axis=1))
    wqkT = np.concatenate([qp[0], qp[1], qp[2], qp[3]], axis=1)
    wvT = np.concatenate([Wqkv[2, h].T for h in hs], axis=1)
    d = {
        "x_fm": np.ascontiguousarray(x.T),
        "x_tm": np.ascontiguousarray(x),
        "x_res": np.ascontiguousarray(x[hg * cfg.OWN:(hg + 1) * cfg.OWN]),
        "rkoff": np.array([[b * 4 * 2 * (cfg.DK + 1) * cfg.S + hg * cfg.OWN]],
                          np.uint32),
        "wqkT": np.ascontiguousarray(wqkT),
        "wvT": np.ascontiguousarray(wvT),
        "woT": np.ascontiguousarray(np.asarray(A["L0_attn_W_o"]).T),
        "baseWT": np.ascontiguousarray(np.asarray(A["L0_ffn_base_W"]).T),
        "baseb": np.asarray(A["L0_ffn_base_b"]).reshape(1, -1),
        "lrAT": np.ascontiguousarray(np.concatenate(
            [np.asarray(A["L0_ffn_shared_A"]).T] +
            [np.asarray(A["L0_ffn_routed_A"][e]).T for e in range(cfg.E)], axis=1)),
        "routerWT": np.ascontiguousarray(np.asarray(A["L0_ffn_router_W"]).T),
        "routerb": np.asarray(A["L0_ffn_router_b"]).reshape(1, -1),
        "BstackT": np.ascontiguousarray(np.concatenate(
            [np.asarray(A["L0_ffn_shared_B"]).T] +
            [np.asarray(A["L0_ffn_routed_B"][e]).T for e in range(cfg.E)], axis=0)),
    }
    d.update(consts)
    return d


_BUILD_CACHE = {}


def kernel(**inputs):
    cfg = CFG
    key = cfg.S
    if key not in _BUILD_CACHE:
        nc, sems, P = build(cfg)
        emit(nc, sems, P)
        _BUILD_CACHE[key] = nc
    nc = _BUILD_CACHE[key]
    consts = host_consts(cfg)
    in_maps = [core_inputs(cfg, inputs, c, consts) for c in range(8)]
    res = run_bass_kernel_spmd(nc, in_maps, core_ids=list(range(8)))
    out = np.empty((2, cfg.S, cfg.D), np.float32)
    for c in range(8):
        b, hg = c // 4, c % 4
        out[b, hg * cfg.OWN:(hg + 1) * cfg.OWN] = res.results[c]["out"]
    return out


# version 12
# speedup vs baseline: 1.0774x; 1.0108x over previous
import math
from dataclasses import dataclass

import numpy as np
import ml_dtypes

import concourse.bass as bass
import concourse.mybir as mybir
from concourse.bass_utils import run_bass_kernel_spmd

F32 = mybir.dt.float32
BF16 = mybir.dt.bfloat16
U32 = mybir.dt.uint32
AF = mybir.ActivationFunctionType
ALU = mybir.AluOpType
AXX = mybir.AxisListType.X
BF = ml_dtypes.bfloat16

ROPE_BASE = 6000.0
EPS = 1e-5


@dataclass
class Cfg:
    S: int = 8192
    D: int = 1024
    F: int = 1024
    NH: int = 16
    HPC: int = 4
    DK: int = 64
    E: int = 6
    R: int = 16

    @property
    def DCH(self): return self.D // 128
    @property
    def FCH(self): return self.F // 128
    @property
    def NG(self): return self.S // 512
    @property
    def NQT(self): return self.S // 512
    @property
    def NKT(self): return self.S // 128
    @property
    def OWN(self): return self.S // 4
    @property
    def NOT(self): return self.OWN // 128
    @property
    def NFG(self): return self.OWN // 512
    @property
    def NST(self): return -(-self.D // 512)
    @property
    def LR(self): return self.R * (1 + self.E)
    @property
    def VW(self): return self.HPC * (self.DK + 1)
    @property
    def NTS(self): return self.S // 128


CFG = Cfg()
ENGS = ("tensor", "vector", "scalar", "gpsimd", "sync")
SELF_SEM = {"vector": "sv", "scalar": "sa"}


class Plan:
    def __init__(self):
        import collections
        self.ops = {e: [] for e in ENGS}
        self.cnt = collections.defaultdict(int)

    def em(self, eng, fn, waits=(), inc=None, incn=1):
        ws = [w for w in waits if w is not None]
        if eng in SELF_SEM:
            ws.append((SELF_SEM[eng], self.cnt[SELF_SEM[eng]]))
        tok = None
        if inc is not None:
            self.cnt[inc] += incn
            tok = (inc, self.cnt[inc])
        self.ops[eng].append((ws, fn, tok, incn))
        return tok

    def wait(self, eng, waits):
        self.ops[eng].append(([w for w in waits if w is not None], None, None, 0))


def build(cfg: Cfg):
    SEMS = {}
    nc = bass.Bass("TRN2", target_bir_lowering=False)
    S, D, Fd = cfg.S, cfg.D, cfg.F
    DCH, FCH, NST = cfg.DCH, cfg.FCH, cfg.NST
    NG, NQT, NKT, OWN, NOT = cfg.NG, cfg.NQT, cfg.NKT, cfg.OWN, cfg.NOT
    HPC, DK, LR, E, R, VW, NTS = cfg.HPC, cfg.DK, cfg.LR, cfg.E, cfg.R, cfg.VW, cfg.NTS
    NDH = -(-D // 512)

    din = lambda n, sh, dt=F32: nc.dram_tensor(n, list(sh), dt, kind="ExternalInput")
    x_fm = din("x_fm", [D, S]);           x_tm = din("x_tm", [S, D])
    x_res = din("x_res", [OWN, D]);       rkoff = din("rkoff", [1, 1], U32)
    wqkT = din("wqkT", [D, 512]);         wvT = din("wvT", [D, HPC * DK])
    woT = din("woT", [Fd, D]);            baseWT = din("baseWT", [D, D])
    baseb = din("baseb", [1, D]);         lrAT = din("lrAT", [D, LR])
    routerWT = din("routerWT", [D, E]);   routerb = din("routerb", [1, E])
    BstackT = din("BstackT", [LR, D])
    ropeC = din("ropeC", [128, S], BF16); ropeS = din("ropeS", [128, S], BF16)
    masksI = din("masks", [512, 512], BF16)
    iden_bf = din("iden_bf", [128, 128], BF16)
    iden_f = din("iden_f", [128, 128], F32)
    bselI = din("bsel", [16, 8 * 128], BF16)
    g6I = din("g6sel", [E, LR], BF16);    shI = din("shsel", [1, LR], BF16)
    out_t = nc.dram_tensor("out", [OWN, D], F32, kind="ExternalOutput")

    stats_d = nc.dram_tensor("stats_d", [2, S], F32)
    zstats_d = nc.dram_tensor("zstats_d", [1, OWN], BF16)
    xch_in = nc.dram_tensor("xch_in", [HPC * (DK + 1), S], BF16)
    xch_out = [nc.dram_tensor(f"xch_out{p}", [4 * 2 * (DK + 1), S], BF16)
               for p in range(2)]

    P = Plan()
    em, wt = P.em, P.wait

    import contextlib
    es = contextlib.ExitStack()
    sb = lambda n, sh, dt: es.enter_context(nc.sbuf_tensor(n, list(sh), dt))

    onesb = sb("onesb", [1, 512], BF16)
    onescol = sb("onescol", [128, 1], BF16)
    epst = sb("epst", [128, 1], F32)
    idenb = sb("idenb", [128, 128], BF16)
    idenf = sb("idenf", [128, 128], F32)
    bneg = sb("bneg", [128, 1], F32)
    r_cols = sb("r_cols", [128, NTS], F32)
    woT_sb = sb("woT_sb", [128, FCH * D], BF16)
    baseWT_sb = sb("baseWT_sb", [128, DCH * D], BF16)
    lrAT_sb = sb("lrAT_sb", [128, DCH * LR], BF16)
    routerWT_sb = sb("routerWT_sb", [128, DCH * E], BF16)
    BstackT_sb = sb("BstackT_sb", [LR, D], BF16)
    bsel_sb = sb("bsel_sb", [16, 8 * 128], BF16)
    g6_sb = sb("g6_sb", [E, LR], BF16)
    shsel_sb = sb("shsel_sb", [1, LR], BF16)
    rb6_sb = sb("rb6_sb", [128, 4 * E], F32)
    bbc_sb = sb("bbc_sb", [128, D], BF16)
    w1base = sb("w1base", [1, D], BF16)
    w1lr = sb("w1lr", [1, LR], BF16)
    w1ro = sb("w1ro", [1, E], BF16)
    bscb = sb("bscb", [1, 1], BF16)

    dmag = lambda fn, ch, waits=(): em("gpsimd", fn, waits=waits, inc=ch, incn=16)
    dmas = lambda fn, ch, waits=(): em("sync", fn, waits=waits, inc=ch, incn=16)

    def chunked_load(dst, src_t, nchunk, width, ch, q="g"):
        f = lambda g: g.dma_start(
            out=dst[:, :].rearrange("p (c n) -> p c n", c=nchunk),
            in_=src_t.ap().rearrange("(c p) n -> p c n", p=128))
        return (dmag if q == "g" else dmas)(f, ch)

    L = {}
    L["woT"] = chunked_load(woT_sb, woT, FCH, D, "dgw")
    L["baseWT"] = chunked_load(baseWT_sb, baseWT, DCH, D, "dgw")
    L["lrAT"] = chunked_load(lrAT_sb, lrAT, DCH, LR, "dgw")
    L["routerWT"] = chunked_load(routerWT_sb, routerWT, DCH, E, "dgw")
    L["Bstack"] = dmag(lambda g: g.dma_start(out=BstackT_sb[:, :], in_=BstackT.ap()), "dgw")
    L["bbrow"] = dmag(lambda g: g.dma_start(out=bbrow[:, :], in_=baseb.ap()), "dgw")
    L["rbrow"] = dmag(lambda g: g.dma_start(out=rbrow[:, :], in_=routerb.ap()), "dgw")
    L["bsel"] = dmas(lambda s: s.dma_start(out=bsel_sb[:, :], in_=bselI.ap()), "dsw")
    L["g6"] = dmas(lambda s: s.dma_start(out=g6_sb[:, :], in_=g6I.ap()), "dsw")
    L["sh"] = dmas(lambda s: s.dma_start(out=shsel_sb[:, :], in_=shI.ap()), "dsw")
    L["idb"] = dmas(lambda s: s.dma_start(out=idenb[:, :], in_=iden_bf.ap()), "dsw")
    L["idf"] = dmas(lambda s: s.dma_start(out=idenf[:, :], in_=iden_f.ap()), "dsw")

    t_ones = em("vector", lambda v: v.memset(onesb[:, :], 1.0), inc="sv")
    t_onesc = em("vector", lambda v: v.memset(onescol[:, :], 1.0), inc="sv")
    t_eps = em("vector", lambda v: v.memset(epst[:, :], EPS), inc="sv")

    AE = {}

    with contextlib.ExitStack() as esA:
        sba = lambda n, sh, dt: esA.enter_context(nc.sbuf_tensor(n, list(sh), dt))
        rbrow = sba("rbrow", [1, E], BF16)
        bbrow = sba("bbrow", [1, D], BF16)
        Qp = [sba(f"Qp{p}", [128, S], BF16) for p in range(2)]
        Kp = [sba(f"Kp{p}", [128, S], BF16) for p in range(2)]
        Vtm = sba("Vtm", [128, NTS * VW], BF16)
        wqk_sb = sba("wqk_sb", [128, DCH * 512], BF16)
        wv_sb = sba("wv_sb", [128, DCH * HPC * DK], BF16)
        xbuf = [sba(f"xbuf{i}", [128, DCH * 512], BF16) for i in range(2)]
        ropeCb = [sba(f"ropeCb{i}", [128, 512], BF16) for i in range(2)]
        ropeSb = [sba(f"ropeSb{i}", [128, 512], BF16) for i in range(2)]
        nmrow = [sba(f"nmrow{i}", [1, 512], BF16) for i in range(2)]
        rrow = [sba(f"rrow{i}", [1, 512], BF16) for i in range(2)]
        rb_sb = sba("rb_sb", [128, 512], F32)
        ropet = [sba(f"ropet{i}", [128, 512], BF16) for i in range(2)]
        w1qk = sba("w1qk", [1, 512], BF16)
        w1v = sba("w1v", [1, HPC * DK], BF16)
        Pt = [sba(f"Pt{i}", [128, 1024], BF16) for i in range(3)]
        ostg = [sba(f"ostg{i}", [65, 512], BF16) for i in range(2)]
        mask_sb = sba("mask_sb", [128, 4 * 512], BF16)
        qsq = sba("qsq", [128, 512], BF16)
        bcol = sba("bcol", [1, 4 * NG], F32)
        bsc = sba("bsc", [1, 1], F32)

        t_msk = dmas(lambda s: s.dma_start(
            out=mask_sb[:, :].rearrange("p (i n) -> p i n", i=4),
            in_=masksI.ap().rearrange("(i p) n -> p i n", p=128)), "dsw")
        t_wqk = chunked_load(wqk_sb, wqkT, DCH, 512, "dgw")
        t_wv = chunked_load(wv_sb, wvT, DCH, HPC * DK, "dgw")
        _gw = ("dgw", P.cnt["dgw"]); _sw = ("dsw", P.cnt["dsw"])
        for _k in list(L):
            L[_k] = _gw if L[_k][0] == "dgw" else _sw
        t_msk, t_wqk, t_wv = _sw, _gw, _gw

        with contextlib.ExitStack() as esP:
            ps_w1 = esP.enter_context(nc.psum_tensor("ps_w1", [1, 512], F32))
            ps_bb = esP.enter_context(nc.psum_tensor("ps_bb", [128, 512], F32))

            _w1prev = [None]
            def w1row(dst, wsb, width, stride, ldtok, w0=0):
                def fn(t):
                    ii = None
                    for k in range(DCH):
                        ii = t.matmul(ps_w1[0:1, 0:width], onescol[:, 0:1],
                                      wsb[:, k * stride + w0:k * stride + w0 + width],
                                      start=(k == 0), stop=(k == DCH - 1))
                    return ii
                tok = em("tensor", fn, waits=[ldtok, t_onesc, _w1prev[0]], inc="st")
                ev = em("vector", lambda v: v.tensor_copy(dst, ps_w1[0:1, 0:width]),
                        waits=[tok], inc="sv")
                _w1prev[0] = ev
                return ev

            e_w1qk = w1row(w1qk[0:1, :], wqk_sb, 512, 512, t_wqk)
            e_w1v = w1row(w1v[0:1, :], wv_sb, HPC * DK, HPC * DK, t_wv)
            e_w1b = []
            for h in range(NDH):
                w0 = h * 512
                wd = min(512, D - w0)
                e_w1b.append(w1row(w1base[0:1, w0:w0 + wd], baseWT_sb, wd, D,
                                   L["baseWT"], w0=w0))
            e_w1lr = w1row(w1lr[0:1, :], lrAT_sb, LR, LR, L["lrAT"])
            e_w1ro = w1row(w1ro[0:1, :], routerWT_sb, E, E, L["routerWT"])

            t1 = em("tensor", lambda t: t.matmul(ps_bb[:, 0:E], onesb[0:1, 0:128],
                                                 rbrow[0:1, :], start=True, stop=True),
                    waits=[L["rbrow"], t_ones, e_w1ro], inc="st")
            e_rb6 = em("vector", lambda v: v.tensor_copy(
                rb6_sb[:, :].rearrange("p (j e) -> p j e", j=4),
                ps_bb[:, 0:E].unsqueeze(1).broadcast_to([128, 4, E])),
                waits=[t1], inc="sv")
            e_bbc = None
            for h in range(NDH):
                w0 = h * 512
                wd = min(512, D - w0)
                t2 = em("tensor", lambda t, w0=w0, wd=wd: t.matmul(
                    ps_bb[:, 0:wd], onesb[0:1, 0:128], bbrow[0:1, w0:w0 + wd],
                    start=True, stop=True),
                    waits=[L["bbrow"], t_ones, e_rb6 if h == 0 else e_bbc], inc="st")
                e_bbc = em("vector", lambda v, w0=w0, wd=wd: v.tensor_copy(
                    bbc_sb[:, w0:w0 + wd], ps_bb[:, 0:wd]), waits=[t2], inc="sv")

        with contextlib.ExitStack() as esS, contextlib.ExitStack() as esP:
            sbs = lambda n, sh, dt: esS.enter_context(nc.sbuf_tensor(n, list(sh), dt))
            xtbuf = [sbs(f"xtbuf{i}", [128, D], F32) for i in range(2)]
            stat12 = sbs("stat12", [128, 6 * NST], F32)
            mvall = sbs("mvall", [128, 2 * NTS], F32)
            negmu_c = sbs("negmu_c", [128, NTS], F32)
            lnv_c = sbs("lnv_c", [128, NTS], F32)
            rowtr = sbs("rowtr", [128, 128], F32)
            ps_tr = esP.enter_context(nc.psum_tensor("ps_tr", [128, 128], F32))

            prev_aggr = [None, None]
            for ts in range(NTS):
                sl = ts % 2
                ld = dmas(lambda s, ts=ts, sl=sl: s.dma_start(
                    out=xtbuf[sl][:, :], in_=x_tm.ap()[ts * 128:(ts + 1) * 128, :]),
                    f"dsx{sl}", waits=[prev_aggr[sl]])
                def fstat(v, sl=sl):
                    ii = None
                    for j in range(NST):
                        c0 = j * 512
                        cw = min(512, D - c0)
                        ii = v.bn_stats(stat12[:, 6 * j:6 * j + 6], xtbuf[sl][:, c0:c0 + cw])
                    return ii
                tstat = em("vector", fstat, waits=[ld], inc="sv")
                taggr = em("vector", lambda v, ts=ts: v.bn_aggr(
                    mvall[:, 2 * ts:2 * ts + 2], stat12[:, 0:6 * NST]),
                    waits=[tstat], inc="sv")
                prev_aggr[sl] = taggr
            allstats = prev_aggr[(NTS - 1) % 2]
            t_nm = em("vector", lambda v: v.tensor_scalar(
                negmu_c[:, :], mvall[:, 0:2 * NTS:2], -1.0, None, ALU.mult),
                waits=[allstats], inc="sv")
            t_lv = em("scalar", lambda s: s.activation(
                lnv_c[:, :], mvall[:, 1:2 * NTS:2], AF.Ln, bias=epst[:, 0:1]),
                waits=[allstats, t_eps], inc="sa")
            t_rr = em("scalar", lambda s: s.activation(
                r_cols[:, 0:NTS], lnv_c[:, :], AF.Exp, scale=-0.5),
                waits=[t_lv], inc="sa")
            assert NTS <= 128
            tk1 = em("tensor", lambda t: t.transpose(ps_tr[0:NTS, 0:128], negmu_c[:, 0:NTS],
                                                     idenf[:, :]),
                     waits=[t_nm, L["idf"]], inc="st")
            ck1 = em("vector", lambda v: v.tensor_copy(rowtr[0:NTS, 0:128],
                                                       ps_tr[0:NTS, 0:128]),
                     waits=[tk1], inc="sv")
            st1 = dmas(lambda s: s.dma_start(
                out=stats_d.ap()[0:1, :].rearrange("o (a b) -> (o a) b", b=128),
                in_=rowtr[0:NTS, 0:128]), "dst", waits=[ck1])
            tk2 = em("tensor", lambda t: t.transpose(ps_tr[0:NTS, 0:128], r_cols[:, 0:NTS],
                                                     idenf[:, :]),
                     waits=[t_rr, ck1], inc="st")
            ck2 = em("vector", lambda v: v.tensor_copy(rowtr[0:NTS, 0:128],
                                                       ps_tr[0:NTS, 0:128]),
                     waits=[tk2, st1], inc="sv")
            st2 = dmas(lambda s: s.dma_start(
                out=stats_d.ap()[1:2, :].rearrange("o (a b) -> (o a) b", b=128),
                in_=rowtr[0:NTS, 0:128]), "dst", waits=[ck2])

        rope_done = []
        with contextlib.ExitStack() as esP:
            ps_qkv = [esP.enter_context(nc.psum_tensor(f"ps_qkv{m}", [128, 512], F32))
                      for m in range(4)]
            ps_V = [esP.enter_context(nc.psum_tensor(f"ps_V{i}", [128, HPC * DK], F32))
                    for i in range(2)]
            ps_rb = esP.enter_context(nc.psum_tensor("ps_rb", [128, 512], F32))
            ps_b = esP.enter_context(nc.psum_tensor("ps_b", [1, 512], F32))

            t_vms = em("vector", lambda v: v.memset(Vtm[:, :], 1.0), inc="sv")
            qk_evac, v_evac = {}, {}
            rb_prev = None
            bnd_mm_prev = None
            bnd_red_last = None
            for g in range(NG):
                sl = g % 2
                wx = []
                if g >= 2:
                    wx = [qk_evac[(g - 2, 3)], v_evac[4 * (g - 2) + 3],
                          rope_done[4 * (g - 1) - 1]]
                xl = dmag(lambda gg, g=g, sl=sl: gg.dma_start(
                    out=xbuf[sl][:, :].rearrange("p (c n) -> p c n", c=DCH),
                    in_=x_fm.ap().rearrange("(c p) s -> p c s", p=128)
                        [:, :, g * 512:(g + 1) * 512]), f"dgx{sl}", waits=wx)
                rl1 = dmag(lambda gg, g=g, sl=sl: gg.dma_start(
                    out=nmrow[sl][:, :], in_=stats_d.ap()[0:1, g * 512:(g + 1) * 512]),
                    f"dgr{sl}", waits=wx + [st1])
                rl2 = dmag(lambda gg, g=g, sl=sl: gg.dma_start(
                    out=rrow[sl][:, :], in_=stats_d.ap()[1:2, g * 512:(g + 1) * 512]),
                    f"dgs{sl}", waits=[st2, rb_prev])
                wct = [rope_done[4 * (g - 1) - 1]] if g >= 2 else []
                ct = dmas(lambda s, g=g, sl=sl: s.dma_start(
                    out=ropeCb[sl][:, :], in_=ropeC.ap()[:, g * 512:(g + 1) * 512]),
                    f"dtc{sl}", waits=wct)
                st_ = dmas(lambda s, g=g, sl=sl: s.dma_start(
                    out=ropeSb[sl][:, :], in_=ropeS.ap()[:, g * 512:(g + 1) * 512]),
                    f"dts{sl}", waits=wct)

                trb = em("tensor", lambda t, sl=sl: t.matmul(
                    ps_rb[:, :], onesb[0:1, 0:128], rrow[sl][0:1, :],
                    start=True, stop=True),
                    waits=[rl2, t_ones, rb_prev], inc="st")
                tre = em("vector", lambda v: v.tensor_copy(rb_sb[:, :], ps_rb[:, :]),
                         waits=[trb], inc="sv")
                rb_prev = tre
                tcC = em("vector", lambda v, sl=sl: v.tensor_tensor(
                    ropeCb[sl][:, :], ropeCb[sl][:, :], rb_sb[:, :], ALU.mult),
                    waits=[ct, tre], inc="sv")
                tcS = em("vector", lambda v, sl=sl: v.tensor_tensor(
                    ropeSb[sl][:, :], ropeSb[sl][:, :], rb_sb[:, :], ALU.mult),
                    waits=[st_, tcC], inc="sv")

                for m in range(4):
                    wqkv = [xl, rl1, t_wqk, e_w1qk]
                    if g >= 1:
                        wqkv.append(qk_evac[(g - 1, m)])
                    def fqk(t, m=m, sl=sl):
                        for k in range(DCH):
                            t.matmul(ps_qkv[m][:, :],
                                     wqk_sb[:, k * 512 + m * 128:k * 512 + (m + 1) * 128],
                                     xbuf[sl][:, k * 512:(k + 1) * 512],
                                     start=(k == 0), stop=False)
                        return t.matmul(ps_qkv[m][:, :], w1qk[0:1, m * 128:(m + 1) * 128],
                                        nmrow[sl][0:1, :], start=False, stop=True)
                    tmm = em("tensor", fqk, waits=wqkv, inc="st")
                    pair, isK = m // 2, (m % 2 == 1)
                    dst = (Kp if isK else Qp)[pair]
                    qk_evac[(g, m)] = em("scalar", lambda s, dst=dst, m=m, g=g: s.activation(
                        dst[:, g * 512:(g + 1) * 512], ps_qkv[m][:, :], AF.Copy),
                        waits=[tmm], inc="sa")

                for tsl in range(4):
                    ts = 4 * g + tsl
                    wv_ = [xl, rl1, t_wv, e_w1v, t_vms]
                    if ts >= 2:
                        wv_.append(v_evac[ts - 2])
                    def fv(t, tsl=tsl, sl=sl):
                        for k in range(DCH):
                            t.matmul(ps_V[tsl % 2][:, :],
                                     xbuf[sl][:, k * 512 + tsl * 128:k * 512 + (tsl + 1) * 128],
                                     wv_sb[:, k * HPC * DK:(k + 1) * HPC * DK],
                                     start=(k == 0), stop=False)
                        return t.matmul(ps_V[tsl % 2][:, :],
                                        nmrow[sl][0:1, tsl * 128:(tsl + 1) * 128],
                                        w1v[0:1, :], start=False, stop=True)
                    tmv = em("tensor", fv, waits=wv_, inc="st")
                    v_evac[ts] = em("scalar", lambda s, ts=ts, tsl=tsl: s.activation(
                        Vtm[:, ts * VW:(ts + 1) * VW]
                            .rearrange("p (h d) -> p h d", h=HPC)[:, :, 0:DK],
                        ps_V[tsl % 2][:, :].rearrange("p (h d) -> p h d", h=HPC),
                        AF.Copy, scale=r_cols[:, ts:ts + 1]),
                        waits=[tmv, t_rr, t_vms], inc="sa")

                shmask = list(range(16, 32)) + list(range(16))
                for pair in range(2):
                    for which, base in ((0, Qp[pair]), (1, Kp[pair])):
                        wdep = qk_evac[(g, pair * 2 + which)]
                        t1_ = em("vector", lambda v, base=base, g=g: v.stream_shuffle(
                            ropet[0][:, :], base[:, g * 512:(g + 1) * 512], shmask),
                            waits=[wdep, tcS], inc="sv")
                        t2_ = em("vector", lambda v, sl=sl: v.tensor_tensor(
                            ropet[0][:, :], ropet[0][:, :], ropeSb[sl][:, :], ALU.mult),
                            waits=[t1_], inc="sv")
                        t3_ = em("vector", lambda v, base=base, g=g, sl=sl: v.tensor_tensor(
                            ropet[1][:, :], base[:, g * 512:(g + 1) * 512],
                            ropeCb[sl][:, :], ALU.mult),
                            waits=[t2_], inc="sv")
                        t4_ = em("vector", lambda v, base=base, g=g: v.tensor_tensor(
                            base[:, g * 512:(g + 1) * 512], ropet[0][:, :],
                            ropet[1][:, :], ALU.add),
                            waits=[t3_], inc="sv")
                        rope_done.append(t4_)

                for m in range(4):
                    pair, isK = m // 2, (m % 2 == 1)
                    srcT = (Kp if isK else Qp)[pair]
                    tsq = em("vector", lambda v, srcT=srcT, g=g: v.tensor_tensor(
                        qsq[:, :], srcT[:, g * 512:(g + 1) * 512],
                        srcT[:, g * 512:(g + 1) * 512], ALU.mult),
                        waits=[rope_done[4 * g + m], bnd_mm_prev], inc="sv")
                    bnd_mm = em("tensor", lambda t: t.matmul(
                        ps_b[0:1, :], onescol[:, 0:1], qsq[:, :], start=True, stop=True),
                        waits=[tsq, bnd_red_last, t_onesc], inc="st")
                    bnd_mm_prev = bnd_mm
                    bnd_red_last = em("vector", lambda v, g=g, m=m: v.tensor_reduce(
                        bcol[0:1, 4 * g + m:4 * g + m + 1], ps_b[0:1, :], AXX, ALU.max),
                        waits=[bnd_mm], inc="sv")
        qkv_sv = ("sv", P.cnt["sv"])
        qkv_sa = ("sa", P.cnt["sa"])
        qkv_st = ("st", P.cnt["st"])

        with contextlib.ExitStack() as esP:
            ps_bc = esP.enter_context(nc.psum_tensor("ps_bc", [128, 1], F32))
            tq = em("vector", lambda v: v.tensor_reduce(
                bsc[0:1, 0:1], bcol[0:1, 0:4 * NG:2], AXX, ALU.max),
                waits=[bnd_red_last, qkv_sv], inc="sv")
            tk = em("vector", lambda v: v.tensor_reduce(
                bcol[0:1, 0:1], bcol[0:1, 1:4 * NG:2], AXX, ALU.max),
                waits=[tq], inc="sv")
            tm = em("vector", lambda v: v.tensor_mul(bsc[0:1, 0:1], bsc[0:1, 0:1],
                                                     bcol[0:1, 0:1]), waits=[tk], inc="sv")
            tl = em("scalar", lambda s: s.activation(bsc[0:1, 0:1], bsc[0:1, 0:1], AF.Ln),
                    waits=[tm], inc="sa")
            te = em("scalar", lambda s: s.activation(bsc[0:1, 0:1], bsc[0:1, 0:1],
                                                     AF.Exp, scale=0.5),
                    waits=[tl], inc="sa")
            tcst = em("vector", lambda v: v.tensor_copy(bscb[0:1, 0:1], bsc[0:1, 0:1]),
                      waits=[te], inc="sv")
            tbc = em("tensor", lambda t: t.matmul(ps_bc[:, 0:1], onesb[0:1, 0:128],
                                                  bscb[0:1, 0:1], start=True, stop=True),
                     waits=[tcst], inc="st")
            bound_sv = em("vector", lambda v: v.tensor_scalar(
                bneg[:, 0:1], ps_bc[:, 0:1], -0.125, None, ALU.mult),
                waits=[tbc], inc="sv")

        with contextlib.ExitStack() as esP:
            ps_S = [esP.enter_context(nc.psum_tensor(f"ps_S{i}", [128, 1024], F32))
                    for i in range(3)]
            ps_O = [esP.enter_context(nc.psum_tensor(f"ps_O{h}", [65, 512], F32))
                    for h in range(2)]

            i_glob = 0
            seq = 0
            score_tok, exp_tok, mask_tok, pv_tok = {}, {}, {}, {}
            evac_toks, store_toks = [], []
            cc_toks = []
            for pair in range(2):
                for qt in range(NQT):
                    nkt = 4 * (qt + 1)
                    base_i = i_glob

                    def emit_sc(j):
                        kt = j
                        i = base_i + j
                        w = [bound_sv]
                        if i == 0:
                            w += [rope_done[-1], qkv_sv, qkv_sa]
                        if i - 3 >= 0:
                            w.append(exp_tok[i - 3])
                        def fsc(t, pair=pair, qt=qt, kt=kt, i=i):
                            ii = None
                            for h in range(2):
                                ii = t.matmul(
                                    ps_S[i % 3][:, h * 512:(h + 1) * 512],
                                    Kp[pair][h * 64:(h + 1) * 64,
                                             kt * 128:(kt + 1) * 128],
                                    Qp[pair][h * 64:(h + 1) * 64,
                                             qt * 512:(qt + 1) * 512],
                                    start=True, stop=True)
                            return ii
                        score_tok[i] = em("tensor", fsc, waits=w, inc="st")
                        w = [score_tok[i]]
                        if i - 3 >= 0:
                            w.append(pv_tok[i - 3])
                        exp_tok[i] = em("scalar", lambda s, i=i: s.activation(
                            Pt[i % 3][:, :], ps_S[i % 3][:, :], AF.Exp,
                            bias=bneg[:, 0:1], scale=0.125),
                            waits=w, inc="sa")
                        if kt >= 4 * qt:
                            mi = kt - 4 * qt
                            def fmk(v, i=i, mi=mi):
                                v.tensor_tensor(Pt[i % 3][:, 0:512], Pt[i % 3][:, 0:512],
                                                mask_sb[:, mi * 512:(mi + 1) * 512],
                                                ALU.mult)
                                return v.tensor_tensor(
                                    Pt[i % 3][:, 512:1024], Pt[i % 3][:, 512:1024],
                                    mask_sb[:, mi * 512:(mi + 1) * 512], ALU.mult)
                            mask_tok[i] = em("vector", fmk,
                                             waits=[exp_tok[i], t_msk], inc="sv")

                    def emit_pv(j):
                        kt = j
                        i = base_i + j
                        w = [mask_tok[i] if i in mask_tok else exp_tok[i]]
                        if j == 0:
                            w.append(evac_toks[-1] if evac_toks else qkv_sa)
                        def fpv(t, pair=pair, kt=kt, i=i, first=(j == 0),
                                last=(j == nkt - 1)):
                            ii = None
                            for h in range(2):
                                hl = 2 * pair + h
                                ii = t.matmul(
                                    ps_O[h][0:65, :],
                                    Vtm[:, kt * VW + hl * (DK + 1):
                                        kt * VW + hl * (DK + 1) + 65],
                                    Pt[i % 3][:, h * 512:(h + 1) * 512],
                                    start=first, stop=last)
                            return ii
                        pv_tok[i] = em("tensor", fpv, waits=w, inc="st")

                    for j in range(nkt):
                        emit_sc(j)
                        if j >= 2:
                            emit_pv(j - 2)
                    emit_pv(nkt - 2)
                    emit_pv(nkt - 1)
                    i_glob = base_i + nkt

                    for h in range(2):
                        hl = 2 * pair + h
                        w = [pv_tok[i_glob - 1]]
                        if seq >= 1:
                            w.append(store_toks[2 * (seq - 1) + h])
                        tev = em("vector", lambda v, h=h: v.tensor_copy(
                            ostg[h][0:65, :], ps_O[h][0:65, :]), waits=w, inc="sv")
                        evac_toks.append(tev)
                        store_toks.append(dmas(
                            lambda s, h=h, hl=hl, qt=qt: s.dma_start(
                                out=xch_in.ap()[hl * 65:(hl + 1) * 65,
                                                qt * 512:(qt + 1) * 512],
                                in_=ostg[h][0:65, :]),
                            f"dso{h}", waits=[tev]))
                    seq += 1
                cc_toks.append(em(
                    "gpsimd",
                    lambda g, pair=pair: g.collective_compute(
                        "AllGather", ALU.bypass,
                        replica_groups=[[0, 1, 2, 3], [4, 5, 6, 7]],
                        ins=[xch_in.ap()[pair * 130:(pair + 1) * 130, :]],
                        outs=[xch_out[pair].ap()]),
                    waits=[("dso0", P.cnt["dso0"]), ("dso1", P.cnt["dso1"])],
                    inc="scc"))

        for s in ("st", "sv", "sa", "dg", "ds"):
            AE[s] = P.cnt[s]

    with contextlib.ExitStack() as esC:
        sbc = lambda n, sh, dt: esC.enter_context(nc.sbuf_tensor(n, list(sh), dt))
        z_sb = sbc("z_sb", [128, NOT * D], F32)
        xres_b = [sbc(f"xres{i}", [128, D], F32) for i in range(2)]
        yln = sbc("yln", [128, D], F32)
        st12c = sbc("st12c", [128, 6 * NST], F32)
        mvc = sbc("mvc", [128, 2], F32)
        nmc = sbc("nmc", [128, 1], F32)
        rsc = sbc("rsc", [128, 1], F32)
        zbf = sbc("zbf", [128, D], BF16)
        negmu4 = sbc("negmu4", [128, NOT], F32)
        lnv4 = sbc("lnv4", [128, NOT], F32)
        r4 = sbc("r4", [128, NOT], F32)
        rowtr2 = sbc("rowtr2", [128, 128], BF16)
        nm4row = [sbc(f"nm4row{i}", [1, 512], BF16) for i in range(2)]
        ro_sb = sbc("ro_sb", [E, 512], F32)
        gtm = sbc("gtm", [128, 4 * E], F32)
        gt4 = sbc("gt4", [128, 20], F32)
        gmask = sbc("gmask", [128, 4 * E], BF16)
        wtm = sbc("wtm", [128, 4 * E], F32)
        wfm = sbc("wfm", [E, 512], BF16)
        wr_sb = sbc("wr_sb", [LR, 512], BF16)
        hs_sb = sbc("hs_sb", [LR, 512], BF16)
        ffn1 = sbc("ffn1", [128, D], F32)
        outb = [sbc("outb0", [128, D], F32)]
        outb.append(outb[0])
        d1 = sbc("d1", [128, 1], F32)
        rk_sb = sbc("rk_sb", [1, 1], U32)

        AEw = [("st", AE["st"]), ("sv", AE["sv"]), ("sa", AE["sa"])]
        zfm = sbc("zfm", [128, DCH * OWN], BF16)

        with contextlib.ExitStack() as esC1:
            sbc1 = lambda n, sh, dt: esC1.enter_context(nc.sbuf_tensor(n, list(sh), dt))
            Och = sbc1("Och", [128, FCH * OWN], BF16)
            sums_sb = [sbc1("sums_sbX", [8, OWN], BF16)]
            sums_sb.append(sums_sb[0])
            recf = sbc1("recf", [8, 256], F32)
            esP = esC1.enter_context(contextlib.ExitStack())
            ps_y = [esP.enter_context(nc.psum_tensor(f"ps_y{i}", [128, D], F32))
                    for i in range(2)]
            ps_rc = [esP.enter_context(nc.psum_tensor(f"ps_rc{i}", [128, 512], F32))
                     for i in range(2)]

            roff = nc.alloc_register(mybir.EngineType.SP, "roff")
            rtmp = nc.alloc_register(mybir.EngineType.SP, "rtmp")
            t_rk = dmas(lambda s: s.dma_start(out=rk_sb[:, :], in_=rkoff.ap()), "dsv0")

            def mk_recv(p):
                def f_recv(s):
                    if p == 0:
                        s.reg_load(roff, rk_sb[0:1, 0:1])
                    for k in range(p, FCH, 2):
                        rowbase = 2 * (DK + 1) * (k // 2)
                        s.reg_mov(rtmp, rowbase * S)
                        s.reg_add(rtmp, rtmp, roff)
                        s.dma_start(
                            out=Och[:, k * OWN:(k + 1) * OWN],
                            in_=bass.AP(xch_out[p], rtmp,
                                        [[(DK + 1) * S, 2], [S, DK], [1, OWN]])
                        ).then_inc(SEMS[f"dsv{p}"], 16)
                    s.reg_mov(rtmp, DK * S)
                    s.reg_add(rtmp, rtmp, roff)
                    return s.dma_start(
                        out=sums_sb[p][:, :],
                        in_=bass.AP(xch_out[p], rtmp,
                                    [[2 * (DK + 1) * S, 4], [(DK + 1) * S, 2], [1, OWN]]))
                return f_recv
            t_recv = [None, None]
            t_rcb = [None, None]
            div_last = None
            div_tts = {}
            di = 0
            for p in range(2):
                P.cnt[f"dsv{p}"] += 16 * (FCH // 2)
                wrecv = [("scc", p + 1), t_rk]
                if p == 1:
                    wrecv.append(("st", P.cnt["st"]))
                t_recv[p] = em("sync", mk_recv(p), waits=wrecv, inc=f"dsv{p}", incn=16)
                for _c in range(OWN // 256):
                    a_ = em("vector", lambda v, p=p, _c=_c: v.tensor_copy(
                        recf[:, :], sums_sb[p][:, _c * 256:(_c + 1) * 256]),
                        waits=[t_recv[p]] + AEw, inc="sv")
                    b_ = em("vector", lambda v: v.reciprocal(recf[:, :], recf[:, :]),
                            waits=[a_], inc="sv")
                    t_rcb[p] = em("vector", lambda v, p=p, _c=_c: v.tensor_copy(
                        sums_sb[p][:, _c * 256:(_c + 1) * 256], recf[:, :]),
                        waits=[b_], inc="sv")
                for k in range(p, FCH, 2):
                    for nt in range(OWN // 512):
                        w = [t_rcb[p], ("sa", AE["sa"]), L["bsel"], div_tts.get(di - 2)]
                        tmm = em("tensor", lambda t, p=p, k=k, nt=nt, di=di: t.matmul(
                            ps_rc[di % 2][:, :], bsel_sb[0:8, k * 128:(k + 1) * 128],
                            sums_sb[p][:, nt * 512:(nt + 1) * 512],
                            start=True, stop=True),
                            waits=w, inc="st")
                        div_last = em("vector", lambda v, k=k, nt=nt, di=di: v.tensor_tensor(
                            Och[:, k * OWN + nt * 512:k * OWN + (nt + 1) * 512],
                            Och[:, k * OWN + nt * 512:k * OWN + (nt + 1) * 512],
                            ps_rc[di % 2][:, :], ALU.mult),
                            waits=[tmm], inc="sv")
                        div_tts[di] = div_last
                        di += 1

            z_toks = []
            prev_ln = None
            prev_ln2 = [None, None]
            prev_ta3 = None
            xr_war = [None, None]
            for ti in range(NOT):
                sl = ti % 2
                t_xr = dmas(lambda s, ti=ti, sl=sl: s.dma_start(
                    out=xres_b[sl][:, :], in_=x_res.ap()[ti * 128:(ti + 1) * 128, :]),
                    f"dsr{sl}", waits=[xr_war[sl]])
                w = [div_last, L["woT"], prev_ln2[ti % 2]]
                def fwo(t, ti=ti):
                    ii = None
                    for hh in range(NDH):
                        w0 = hh * 512
                        wd = min(512, D - w0)
                        for k in range(FCH):
                            ii = t.matmul(ps_y[ti % 2][:, w0:w0 + wd],
                                          Och[:, k * OWN + ti * 128:k * OWN + (ti + 1) * 128],
                                          woT_sb[:, k * D + w0:k * D + w0 + wd],
                                          start=(k == 0), stop=(k == FCH - 1))
                    return ii
                tmy = em("tensor", fwo, waits=w, inc="st")
                def fst3(v, ti=ti):
                    ii = None
                    for j in range(NST):
                        c0 = j * 512
                        cw = min(512, D - c0)
                        ii = v.bn_stats(st12c[:, 6 * j:6 * j + 6],
                                        ps_y[ti % 2][:, c0:c0 + cw])
                    return ii
                ts1 = em("vector", fst3, waits=[tmy], inc="sv")
                ts2 = em("vector", lambda v: v.bn_aggr(mvc[:, :], st12c[:, 0:6 * NST]),
                         waits=[ts1, prev_ta3], inc="sv")
                ts3 = em("vector", lambda v: v.tensor_scalar(
                    nmc[:, 0:1], mvc[:, 0:1], -1.0, None, ALU.mult), waits=[ts2], inc="sv")
                ta1 = em("scalar", lambda s: s.activation(
                    rsc[:, 0:1], mvc[:, 1:2], AF.Ln, bias=epst[:, 0:1]),
                    waits=[ts2, prev_ln], inc="sa")
                ta2 = em("scalar", lambda s: s.activation(
                    rsc[:, 0:1], rsc[:, 0:1], AF.Exp, scale=-0.5), waits=[ta1], inc="sa")
                ts4 = em("vector", lambda v, ti=ti: v.tensor_scalar(
                    yln[:, :], ps_y[ti % 2][:, :], nmc[:, 0:1], rsc[:, 0:1],
                    ALU.add, ALU.mult),
                    waits=[ts3, ta2], inc="sv")
                prev_ln = ts4
                prev_ln2[ti % 2] = ts4
                prev_ta3 = ta2
                tz = em("vector", lambda v, ti=ti, sl=sl: v.tensor_tensor(
                    z_sb[:, ti * D:(ti + 1) * D], yln[:, :], xres_b[sl][:, :], ALU.add),
                    waits=[ts4, t_xr], inc="sv")
                xr_war[sl] = tz
                z_toks.append(tz)

        with contextlib.ExitStack() as esP:
            ps_zt = [esP.enter_context(nc.psum_tensor(f"ps_zt{i}", [128, 128], BF16))
                     for i in range(2)]
            ps_tr2 = esP.enter_context(nc.psum_tensor("ps_tr2", [128, 128], F32))
            ln4_nm, ln4_r = [], []
            prev_tr = None
            ztev = {}
            nzt = 0
            for ti in range(NOT):
                tc = em("vector", lambda v, ti=ti: v.tensor_copy(
                    zbf[:, :], z_sb[:, ti * D:(ti + 1) * D]),
                    waits=[z_toks[ti], prev_tr], inc="sv")
                for k in range(DCH):
                    tt = em("tensor", lambda t, k=k, nzt=nzt: t.transpose(
                        ps_zt[nzt % 2][:, :], zbf[:, k * 128:(k + 1) * 128], idenb[:, :]),
                        waits=[tc, L["idb"], ztev.get(nzt - 2)], inc="st")
                    prev_tr = em("vector", lambda v, k=k, ti=ti, nzt=nzt: v.tensor_copy(
                        zfm[:, k * OWN + ti * 128:k * OWN + (ti + 1) * 128],
                        ps_zt[nzt % 2][:, :]),
                        waits=[tt], inc="sv")
                    ztev[nzt] = prev_tr
                    nzt += 1
                def fs4(v, ti=ti):
                    ii = None
                    for j in range(NST):
                        c0 = j * 512
                        cw = min(512, D - c0)
                        ii = v.bn_stats(st12c[:, 6 * j:6 * j + 6],
                                        z_sb[:, ti * D + c0:ti * D + c0 + cw])
                    return ii
                t41 = em("vector", fs4, waits=[z_toks[ti]], inc="sv")
                t42 = em("vector", lambda v: v.bn_aggr(mvc[:, :], st12c[:, 0:6 * NST]),
                         waits=[t41, prev_ta3, ln4_r[-1] if ln4_r else None], inc="sv")
                t43 = em("vector", lambda v, ti=ti: v.tensor_scalar(
                    negmu4[:, ti:ti + 1], mvc[:, 0:1], -1.0, None, ALU.mult),
                    waits=[t42], inc="sv")
                t44 = em("scalar", lambda s, ti=ti: s.activation(
                    lnv4[:, ti:ti + 1], mvc[:, 1:2], AF.Ln, bias=epst[:, 0:1]),
                    waits=[t42], inc="sa")
                t45 = em("scalar", lambda s, ti=ti: s.activation(
                    r4[:, ti:ti + 1], lnv4[:, ti:ti + 1], AF.Exp, scale=-0.5),
                    waits=[t44], inc="sa")
                ln4_nm.append(t43)
                ln4_r.append(t45)
            ztr_sv = ("sv", P.cnt["sv"])

            assert NOT <= 128
            tz1 = em("tensor", lambda t: t.transpose(
                ps_tr2[0:NOT, 0:128], negmu4[:, 0:NOT], idenf[:, :]),
                waits=ln4_nm + [prev_tr], inc="st")
            cz1 = em("vector", lambda v: v.tensor_copy(rowtr2[0:NOT, 0:128],
                                                       ps_tr2[0:NOT, 0:128]),
                     waits=[tz1], inc="sv")
            sz1 = dmas(lambda s: s.dma_start(
                out=zstats_d.ap()[0:1, :].rearrange("o (a b) -> (o a) b", b=128),
                in_=rowtr2[0:NOT, 0:128]), "dst", waits=[cz1])
            c_mid_st = ("st", P.cnt["st"])

        with contextlib.ExitStack() as esP:
            ps_ffn = esP.enter_context(nc.psum_tensor("ps_ffn", [128, D], F32))
            ps_lr = [esP.enter_context(nc.psum_tensor(f"ps_lr{i}", [LR, 512], F32))
                     for i in range(2)]
            ps_ro = esP.enter_context(nc.psum_tensor("ps_ro", [E, 512], F32))
            ps_g = esP.enter_context(nc.psum_tensor("ps_g", [128, 4 * E], F32))
            ps_wt = esP.enter_context(nc.psum_tensor("ps_wt", [E, 512], F32))
            ps_wb = esP.enter_context(nc.psum_tensor("ps_wb", [LR, 512], F32))

            prev_rows = None
            prev_hs = None
            prev_hs2 = [None, None]
            prev_ffn = None
            out_final = None
            prev_ua2 = [None]
            prev_u4 = [None]
            for g4 in range(cfg.NFG):
                sl = g4 % 2
                l1 = dmag(lambda g, g4=g4, sl=sl: g.dma_start(
                    out=nm4row[sl][:, :],
                    in_=zstats_d.ap()[0:1, g4 * 512:(g4 + 1) * 512]),
                    f"dgz{sl}", waits=[sz1, prev_rows])

                w = [ztr_sv, l1, L["lrAT"], e_w1lr, c_mid_st, prev_hs2[g4 % 2]]
                def flr(t, g4=g4, sl=sl):
                    for k in range(DCH):
                        t.matmul(ps_lr[g4 % 2][0:LR, :], lrAT_sb[:, k * LR:(k + 1) * LR],
                                 zfm[:, k * OWN + g4 * 512:k * OWN + (g4 + 1) * 512],
                                 start=(k == 0), stop=False)
                    return t.matmul(ps_lr[g4 % 2][0:LR, :], w1lr[0:1, :],
                                    nm4row[sl][0:1, :], start=False, stop=True)
                t_lr = em("tensor", flr, waits=w, inc="st")

                wro = [L["routerWT"], e_w1ro]
                def fro(t, g4=g4, sl=sl):
                    for k in range(DCH):
                        t.matmul(ps_ro[0:E, :], routerWT_sb[:, k * E:(k + 1) * E],
                                 zfm[:, k * OWN + g4 * 512:k * OWN + (g4 + 1) * 512],
                                 start=(k == 0), stop=False)
                    return t.matmul(ps_ro[0:E, :], w1ro[0:1, :], nm4row[sl][0:1, :],
                                    start=False, stop=True)
                t_ro = em("tensor", fro, waits=wro, inc="st")
                t_roe = em("vector", lambda v: v.tensor_copy(ro_sb[0:E, :], ps_ro[0:E, :]),
                           waits=[t_ro], inc="sv")

                tr_toks = []
                for j in range(4):
                    tt = em("tensor", lambda t, j=j: t.transpose(
                        ps_g[:, j * E:(j + 1) * E], ro_sb[0:E, j * 128:(j + 1) * 128],
                        idenf[0:E, 0:E]),
                        waits=[t_roe], inc="st")
                    tr_toks.append(tt)
                tg0 = em("vector", lambda v: v.tensor_copy(gtm[:, :], ps_g[:, :]),
                         waits=tr_toks, inc="sv")
                gk = tg0
                for j in range(4):
                    ti4 = 4 * g4 + j
                    gk = em("vector", lambda v, j=j, ti4=ti4: v.tensor_scalar(
                        gtm[:, j * E:(j + 1) * E], gtm[:, j * E:(j + 1) * E],
                        r4[:, ti4:ti4 + 1], None, ALU.mult),
                        waits=[gk, ln4_r[ti4]], inc="sv")
                gk = em("vector", lambda v: v.tensor_add(gtm[:, :], gtm[:, :], rb6_sb[:, :]),
                        waits=[gk, e_rb6], inc="sv")
                G3 = gtm[:, :].rearrange("p (j e) -> p j e", j=4)
                M3 = gmask[:, :].rearrange("p (j e) -> p j e", j=4)
                W3 = wtm[:, :].rearrange("p (j e) -> p j e", j=4)
                bc = lambda ap: ap.unsqueeze(2).broadcast_to([128, 4, E])
                gk = em("vector", lambda v: v.tensor_reduce(
                    gt4[:, 0:4], G3, AXX, ALU.max, negate=True), waits=[gk], inc="sv")
                gk = em("vector", lambda v: v.tensor_tensor(
                    G3, G3, bc(gt4[:, 0:4]), ALU.add), waits=[gk], inc="sv")
                gexp = em("scalar", lambda s: s.activation(gtm[:, :], gtm[:, :], AF.Exp),
                          waits=[gk], inc="sa")
                gk = em("vector", lambda v: v.tensor_reduce(
                    gt4[:, 4:8], G3, AXX, ALU.add), waits=[gexp], inc="sv")
                gk = em("vector", lambda v: v.reciprocal(gt4[:, 4:8], gt4[:, 4:8]),
                        waits=[gk], inc="sv")
                gk = em("vector", lambda v: v.tensor_tensor(
                    G3, G3, bc(gt4[:, 4:8]), ALU.mult), waits=[gk], inc="sv")
                gk = em("vector", lambda v: v.tensor_reduce(
                    gt4[:, 0:4], G3, AXX, ALU.max), waits=[gk], inc="sv")
                gk = em("vector", lambda v: v.tensor_tensor(
                    M3, G3, bc(gt4[:, 0:4]), ALU.is_ge), waits=[gk], inc="sv")
                gk = em("vector", lambda v: v.scalar_tensor_tensor(
                    W3, M3, -1e30, G3, ALU.mult, ALU.add), waits=[gk], inc="sv")
                gk = em("vector", lambda v: v.tensor_reduce(
                    gt4[:, 8:12], W3, AXX, ALU.max), waits=[gk], inc="sv")
                gk = em("vector", lambda v: v.tensor_reduce(
                    gt4[:, 12:16], M3, AXX, ALU.add), waits=[gk], inc="sv")
                gk = em("vector", lambda v: v.tensor_scalar(
                    gt4[:, 12:16], gt4[:, 12:16], -1.0, 1.0, ALU.add, ALU.min),
                    waits=[gk], inc="sv")
                gk = em("vector", lambda v: v.tensor_sub(
                    gt4[:, 16:20], gt4[:, 0:4], gt4[:, 8:12]), waits=[gk], inc="sv")
                gk = em("vector", lambda v: v.tensor_mul(
                    gt4[:, 16:20], gt4[:, 16:20], gt4[:, 12:16]), waits=[gk], inc="sv")
                gk = em("vector", lambda v: v.tensor_add(
                    gt4[:, 8:12], gt4[:, 8:12], gt4[:, 16:20]), waits=[gk], inc="sv")
                gk = em("vector", lambda v: v.tensor_tensor(
                    M3, G3, bc(gt4[:, 8:12]), ALU.is_ge), waits=[gk], inc="sv")
                gk = em("vector", lambda v: v.tensor_tensor(
                    W3, G3, M3, ALU.mult), waits=[gk], inc="sv")
                gk = em("vector", lambda v: v.tensor_reduce(
                    gt4[:, 4:8], W3, AXX, ALU.add), waits=[gk], inc="sv")
                gk = em("vector", lambda v: v.tensor_scalar(
                    gt4[:, 4:8], gt4[:, 4:8], 1e-6, None, ALU.add), waits=[gk], inc="sv")
                gk = em("vector", lambda v: v.reciprocal(gt4[:, 4:8], gt4[:, 4:8]),
                        waits=[gk], inc="sv")
                gk = em("vector", lambda v: v.tensor_tensor(
                    W3, W3, bc(gt4[:, 4:8]), ALU.mult), waits=[gk], inc="sv")
                wtr = []
                for j in range(4):
                    tt = em("tensor", lambda t, j=j: t.transpose(
                        ps_wt[0:E, j * 128:(j + 1) * 128], wtm[:, j * E:(j + 1) * E],
                        idenf[:, :]),
                        waits=[gk], inc="st")
                    wtr.append(tt)
                twf = em("vector", lambda v: v.tensor_copy(wfm[0:E, :], ps_wt[0:E, :]),
                         waits=wtr, inc="sv")
                def fwb(t):
                    t.matmul(ps_wb[0:LR, :], g6_sb[0:E, :], wfm[0:E, :],
                             start=True, stop=False)
                    return t.matmul(ps_wb[0:LR, :], shsel_sb[0:1, :], onesb[0:1, :],
                                    start=False, stop=True)
                t_wb = em("tensor", fwb, waits=[twf, L["g6"], L["sh"], t_ones], inc="st")
                t_wre = em("vector", lambda v: v.tensor_copy(wr_sb[0:LR, :], ps_wb[0:LR, :]),
                           waits=[t_wb], inc="sv")
                t_hs = em("vector", lambda v, g4=g4: v.tensor_tensor(
                    hs_sb[0:LR, :], ps_lr[g4 % 2][0:LR, :], wr_sb[0:LR, :], ALU.mult),
                    waits=[t_wre, t_lr], inc="sv")
                prev_hs = t_hs
                prev_hs2[g4 % 2] = t_hs

                for tsl in range(4):
                    ti = 4 * g4 + tsl
                    w = [t_hs, L["baseWT"], L["Bstack"], e_w1b[-1], l1]
                    if prev_ffn is not None:
                        w.append(prev_ffn)
                    def fbase(t, g4=g4, tsl=tsl, sl=sl):
                        ii = None
                        for hh in range(NDH):
                            w0 = hh * 512
                            wd = min(512, D - w0)
                            for k in range(DCH):
                                t.matmul(ps_ffn[:, w0:w0 + wd],
                                         zfm[:, k * OWN + g4 * 512 + tsl * 128:
                                             k * OWN + g4 * 512 + (tsl + 1) * 128],
                                         baseWT_sb[:, k * D + w0:k * D + w0 + wd],
                                         start=(k == 0), stop=False)
                            t.matmul(ps_ffn[:, w0:w0 + wd],
                                     nm4row[sl][0:1, tsl * 128:(tsl + 1) * 128],
                                     w1base[0:1, w0:w0 + wd], start=False, stop=False)
                            ii = t.matmul(ps_ffn[:, w0:w0 + wd],
                                          hs_sb[0:LR, tsl * 128:(tsl + 1) * 128],
                                          BstackT_sb[0:LR, w0:w0 + wd],
                                          start=False, stop=True)
                        return ii
                    t_f = em("tensor", fbase, waits=w, inc="st")
                    tf1 = em("vector", lambda v, ti=ti: v.scalar_tensor_tensor(
                        ffn1[:, :], ps_ffn[:, :], r4[:, ti:ti + 1], bbc_sb[:, :],
                        ALU.mult, ALU.add),
                        waits=[t_f, e_bbc, ln4_r[ti]], inc="sv")
                    prev_ffn = tf1
                    def fs5(v):
                        ii = None
                        for j in range(NST):
                            c0 = j * 512
                            cw = min(512, D - c0)
                            ii = v.bn_stats(st12c[:, 6 * j:6 * j + 6], ffn1[:, c0:c0 + cw])
                        return ii
                    u1 = em("vector", fs5, waits=[tf1], inc="sv")
                    u2 = em("vector", lambda v: v.bn_aggr(mvc[:, :], st12c[:, 0:6 * NST]),
                            waits=[u1, prev_ua2[0]], inc="sv")
                    u3 = em("vector", lambda v: v.tensor_scalar(
                        nmc[:, 0:1], mvc[:, 0:1], -1.0, None, ALU.mult),
                        waits=[u2], inc="sv")
                    ua1 = em("scalar", lambda s: s.activation(
                        rsc[:, 0:1], mvc[:, 1:2], AF.Ln, bias=epst[:, 0:1]),
                        waits=[u2, prev_u4[0]], inc="sa")
                    ua2 = em("scalar", lambda s: s.activation(
                        rsc[:, 0:1], rsc[:, 0:1], AF.Exp, scale=-0.5),
                        waits=[ua1], inc="sa")
                    u4 = em("vector", lambda v: v.tensor_scalar(
                        yln[:, :], ffn1[:, :], nmc[:, 0:1], rsc[:, 0:1],
                        ALU.add, ALU.mult),
                        waits=[u3, ua2], inc="sv")
                    prev_ua2[0] = ua2
                    prev_u4[0] = u4
                    u5 = em("vector", lambda v, ti=ti, tsl=tsl: v.tensor_tensor(
                        outb[tsl % 2][:, :], yln[:, :], z_sb[:, ti * D:(ti + 1) * D],
                        ALU.add),
                        waits=[u4, ("dou0", P.cnt["dou0"])],
                        inc="sv")
                    out_final = dmas(lambda s, ti=ti, tsl=tsl: s.dma_start(
                        out=out_t.ap()[ti * 128:(ti + 1) * 128, :],
                        in_=outb[tsl % 2][:, :]), "dou0", waits=[u5])
                prev_rows = prev_ffn

            wt("sync", [(s, P.cnt[s]) for s in list(P.cnt.keys())
                        if s.startswith("d") or s == "scc"])

    sems = {name: es.enter_context(nc.semaphore(name)) for name in P.cnt}
    SEMS.update(sems)
    nc._keepalive_es = es
    return nc, sems, P


def emit(nc, sems, P):
    with nc.Block() as blk:
        def mk(name):
            def run(eng):
                seen = {}
                for ws, fn, tok, incn in P.ops[name]:
                    for s, v in ws:
                        if v > seen.get(s, -1):
                            eng.wait_ge(sems[s], v)
                            seen[s] = v
                    if fn is None:
                        continue
                    inst = fn(eng)
                    if tok is not None:
                        inst.then_inc(sems[tok[0]], incn)
            return run
        blk.tensor(mk("tensor"))
        blk.vector(mk("vector"))
        blk.scalar(mk("scalar"))
        blk.gpsimd(mk("gpsimd"))
        blk.sync(mk("sync"))
    return nc



def host_consts(cfg: Cfg):
    S, DK = cfg.S, cfg.DK
    rope_dim = DK // 2
    inv = 1.0 / ROPE_BASE ** (np.arange(0, rope_dim, 2, dtype=np.float32) / rope_dim)
    t = np.arange(S, dtype=np.float32)
    fr = np.outer(t, inv)
    emb = np.concatenate([fr, fr], axis=1)
    cos = np.cos(emb).astype(np.float32)
    sin = np.sin(emb).astype(np.float32)
    C = np.ones((128, S), np.float32)
    Sg = np.zeros((128, S), np.float32)
    for blk in (0, 64):
        C[blk:blk + 32] = cos.T
        Sg[blk:blk + 16] = -sin.T[:16]
        Sg[blk + 16:blk + 32] = sin.T[16:32]
    masks = np.zeros((4 * 128, 512), np.float32)
    p = np.arange(128)[:, None]
    c = np.arange(512)[None, :]
    for i in range(4):
        masks[i * 128:(i + 1) * 128] = (c >= 128 * i + p)
    bsel = np.zeros((16, 8 * 128), np.float32)
    for k in range(8):
        for a in range(2):
            row = 2 * (k // 2) + a
            bsel[row, k * 128 + a * 64:k * 128 + (a + 1) * 64] = 1.0
    g6 = np.zeros((cfg.E, cfg.LR), np.float32)
    for e in range(cfg.E):
        g6[e, cfg.R + cfg.R * e:cfg.R + cfg.R * (e + 1)] = 1.0
    sh = np.zeros((1, cfg.LR), np.float32)
    sh[0, :cfg.R] = 1.0
    return {
        "ropeC": C.astype(BF), "ropeS": Sg.astype(BF),
        "masks": masks.astype(BF),
        "iden_bf": np.eye(128, dtype=np.float32).astype(BF),
        "iden_f": np.eye(128, dtype=np.float32),
        "bsel": bsel.astype(BF), "g6sel": g6.astype(BF), "shsel": sh.astype(BF),
    }


def core_inputs(cfg: Cfg, A, c, consts):
    b, hg = c // 4, c % 4
    x = np.asarray(A["L0_attn_x"][b])
    Wqkv = np.asarray(A["L0_attn_W_qkv"])
    hs = [4 * hg + i for i in range(cfg.HPC)]
    qp = []
    for pr in range(2):
        h0, h1 = hs[2 * pr], hs[2 * pr + 1]
        qp.append(np.concatenate([Wqkv[0, h0].T, Wqkv[0, h1].T], axis=1))
        qp.append(np.concatenate([Wqkv[1, h0].T, Wqkv[1, h1].T], axis=1))
    wqkT = np.concatenate([qp[0], qp[1], qp[2], qp[3]], axis=1)
    wvT = np.concatenate([Wqkv[2, h].T for h in hs], axis=1)
    d = {
        "x_fm": np.ascontiguousarray(x.T),
        "x_tm": np.ascontiguousarray(x),
        "x_res": np.ascontiguousarray(x[hg * cfg.OWN:(hg + 1) * cfg.OWN]),
        "rkoff": np.array([[hg * cfg.OWN]], np.uint32),
        "wqkT": np.ascontiguousarray(wqkT),
        "wvT": np.ascontiguousarray(wvT),
        "woT": np.ascontiguousarray(np.asarray(A["L0_attn_W_o"]).T),
        "baseWT": np.ascontiguousarray(np.asarray(A["L0_ffn_base_W"]).T),
        "baseb": np.asarray(A["L0_ffn_base_b"]).reshape(1, -1),
        "lrAT": np.ascontiguousarray(np.concatenate(
            [np.asarray(A["L0_ffn_shared_A"]).T] +
            [np.asarray(A["L0_ffn_routed_A"][e]).T for e in range(cfg.E)], axis=1)),
        "routerWT": np.ascontiguousarray(np.asarray(A["L0_ffn_router_W"]).T),
        "routerb": np.asarray(A["L0_ffn_router_b"]).reshape(1, -1),
        "BstackT": np.ascontiguousarray(np.concatenate(
            [np.asarray(A["L0_ffn_shared_B"]).T] +
            [np.asarray(A["L0_ffn_routed_B"][e]).T for e in range(cfg.E)], axis=0)),
    }
    d.update(consts)
    return d


_BUILD_CACHE = {}


def kernel(**inputs):
    cfg = CFG
    key = cfg.S
    if key not in _BUILD_CACHE:
        nc, sems, P = build(cfg)
        emit(nc, sems, P)
        _BUILD_CACHE[key] = nc
    nc = _BUILD_CACHE[key]
    consts = host_consts(cfg)
    in_maps = [core_inputs(cfg, inputs, c, consts) for c in range(8)]
    res = run_bass_kernel_spmd(nc, in_maps, core_ids=list(range(8)))
    out = np.empty((2, cfg.S, cfg.D), np.float32)
    for c in range(8):
        b, hg = c // 4, c % 4
        out[b, hg * cfg.OWN:(hg + 1) * cfg.OWN] = res.results[c]["out"]
    return out
